# version 04
# baseline (speedup 1.0000x reference)
import math
from contextlib import ExitStack

import numpy as np
import ml_dtypes

import concourse.bass as bass
import concourse.mybir as mybir
from concourse.bass_utils import run_bass_kernel_spmd

F32 = mybir.dt.float32
BF16 = mybir.dt.bfloat16
AF = mybir.ActivationFunctionType
ALU = mybir.AluOpType
AX = mybir.AxisListType.X

D = 4096
SBT = 4096
TOK = 1024
NDM = 32
NH = 8
NSW = 4
NTC = 16
TCW = 256
NG = 8
NG2 = 16
CEXP = 10.0
SCL = 1.0 / math.sqrt(128.0)
WCH = [64, 128, 256, 512, 512, 512, 512, 512]
MASKNEG = -1.0e9


def _nkt(g):
    return 4 * g + 4


BLOCKS_PER_HEAD = sum(_nkt(g) for g in range(NG))
BLK_BEFORE_G = [sum(_nkt(gg) for gg in range(g)) for g in range(NG)]


def _blk_idx(h, g, kt):
    return h * BLOCKS_PER_HEAD + BLK_BEFORE_G[g] + kt


def _qk_before_g(h, g):
    return h * (BLOCKS_PER_HEAD + NG) + BLK_BEFORE_G[g] + g


def _tab_meta():
    idx = {}
    for hl in range(NH):
        w = WCH[hl]
        for g in range(NG):
            for kt in range(_nkt(g)):
                for qc in range(512 // w):
                    q_hi = 512 * g + qc * w + w - 1
                    if q_hi < 128 * kt:
                        continue
                    key = (hl, 128 * kt - q_hi)
                    if key not in idx:
                        idx[key] = len(idx)
    return idx


TAB = _tab_meta()
NV = len(TAB)


def _exp_pieces(hl, g, kt):
    w = WCH[hl]
    j = kt - 4 * g
    diag = j >= 0
    pieces = []
    for qc in range(512 // w):
        ql, qh = qc * w, qc * w + w
        q_hi_g = 512 * g + qh - 1
        if q_hi_g < 128 * kt:
            continue
        col = TAB[(hl, 128 * kt - q_hi_g)]
        if diag:
            w_lo, w_hi = 128 * j, 128 * j + 128
            a, b = max(ql, w_lo), min(qh, w_hi)
            if a < b:
                pieces.append((a, b, "sm", col))
            if qh > w_hi:
                pieces.append((max(ql, w_hi), qh, "ps", col))
        else:
            pieces.append((ql, qh, "ps", col))
    return pieces


SQ128 = math.sqrt(128.0)


def _slopes_f():
    m = np.arange(1, 33, dtype=np.float32) * 0.25
    return (1.0 / np.power(np.float32(2.0), m)).astype(np.float32)


def _live(hl, g, kt):
    sl = float(_slopes_f()[4 * hl + 3])
    w = WCH[hl]
    bias_max = sl * (128 * kt + 127 - (512 * g + w - 1))
    return kt >= 4 * g or bias_max > -80.0


LIVE = {(hl, g): [kt for kt in range(_nkt(g)) if _live(hl, g, kt)]
        for hl in range(NH) for g in range(NG)}
HB2D = 3

EXPC = {}
SMC = {}
QKC = {}
MSKC = {}
AV_AFTER_HEAD = [0] * (NH + 1)
_exp = _sm = _msk = 0
_qk = 0
for _h in range(NH):
    for _g in range(NG):
        QKC[(_h, _g)] = _qk
        lv = LIVE[(_h, _g)]
        for _kt in lv:
            _exp += 1
            EXPC[(_h, _g, _kt)] = _exp
            if _h < HB2D:
                _sm += 1
                SMC[(_h, _g, _kt)] = _sm
            elif _kt >= 4 * _g:
                _msk += 1
                MSKC[(_h, _g, _kt)] = _msk
        _qk += len(lv) + 1
    AV_AFTER_HEAD[_h + 1] = 2 * _exp
T_EXP_ALL = _exp

FIS = [(0, 0), (1, 0), (0, 1), (1, 1)]
YIS = [(oc2, tt2, l) for oc2 in range(2) for tt2 in range(2) for l in range(4)]


def _cm_dm(g2, k):
    return 48 * g2 + 8 * (k + 1)


def _ca_F(g2, fo):
    return 20 * g2 + (1, 2, 11, 12)[fo]


def _ca_y(g2, yi):
    return 20 * g2 + (3 + yi if yi < 8 else 5 + yi)


def _cd_base(g2):
    return 1 + 40 * g2


def _cd_bias(g2, fo):
    return _cd_base(g2) + 10 * fo + 1


def _cd_xc(g2, fo):
    return _cd_base(g2) + 10 * fo + 2


def _cd_tt(g2, fo, l):
    return _cd_base(g2) + 10 * fo + 3 + 2 * l


def _cd_add(g2, fo, l):
    return _cd_base(g2) + 10 * fo + 4 + 2 * l


def _cd_final(g2, fo):
    return _cd_add(g2, fo, 3)


CTI = {}
for _h in range(HB2D):
    for _g in range(NG):
        for _kt in LIVE[(_h, _g)]:
            _d = _kt - 4 * _g
            if (_h, _d) not in CTI:
                CTI[(_h, _d)] = len(CTI)
NCT = len(CTI)

LDC = {"mtri": 16, "btab": 32, "ident": 48, "bqkvqk": 64, "vbias": 80,
       "b2d": 96, "mtri4": 112, "ctab": 128, "bias_bc": 144, "boutbc": 176}


def _v1cm_base(g2):
    return 52 * g2


def _v1cd_base(g2):
    return 1 + 41 * g2


def _v1cd_hevac(g2):
    return _v1cd_base(g2) + 1


def _v1cd_biasadd(g2, fi):
    return _v1cd_base(g2) + 1 + 10 * fi + 1


def _v1cd_xcadd(g2, fi):
    return _v1cd_base(g2) + 1 + 10 * fi + 2


def _v1cd_stt(g2, fi, l):
    return _v1cd_base(g2) + 1 + 10 * fi + 3 + 2 * l


def _v1cd_add(g2, fi, l):
    return _v1cd_base(g2) + 1 + 10 * fi + 4 + 2 * l


def _v1cd_final(g2, fi):
    return _v1cd_add(g2, fi, 3)


def _p3_ring_guard_xn(u):
    g2, k = divmod(u, 8)
    if k < 4:
        return _v1cm_base(g2) + k + 1
    return _v1cm_base(g2) + 4 + 8 * (k - 4) + 8


def _p3_ring_guard_ao(u):
    g2, k = divmod(u, 4)
    return _v1cm_base(g2) + 4 + 8 * k + 8



def build_nc():
    nc = bass.Bass("TRN2", target_bir_lowering=False)

    ap = lambda t: t.ap()
    x_full = ap(nc.dram_tensor("x_full", [SBT, D], F32, kind="ExternalInput"))
    x_cols = ap(nc.dram_tensor("x_cols", [SBT, TOK], F32, kind="ExternalInput"))
    wqkvT = ap(nc.dram_tensor("wqkvT", [D, 3072], F32, kind="ExternalInput"))
    bqkv_qk = ap(nc.dram_tensor("bqkv_qk", [128, 16], F32, kind="ExternalInput"))
    bqkv_vb = ap(nc.dram_tensor("bqkv_vb", [128, 1024], BF16, kind="ExternalInput"))
    woutT = ap(nc.dram_tensor("woutT", [D, 1024], F32, kind="ExternalInput"))
    w0T = ap(nc.dram_tensor("w0T", [D, 1024], F32, kind="ExternalInput"))
    b0bc = ap(nc.dram_tensor("b0bc", [128, 1024], F32, kind="ExternalInput"))
    boutbc = ap(nc.dram_tensor("boutbc", [128, 1024], F32, kind="ExternalInput"))
    aeff = ap(nc.dram_tensor("aeff", [D, 128], F32, kind="ExternalInput"))
    beff = ap(nc.dram_tensor("beff", [4, 32, 1024], F32, kind="ExternalInput"))
    masks = ap(nc.dram_tensor("masks", [4, SBT, 1024], BF16, kind="ExternalInput"))
    btab = ap(nc.dram_tensor("btab", [128, NV], F32, kind="ExternalInput"))
    mtri = ap(nc.dram_tensor("mtri", [128, 128], F32, kind="ExternalInput"))
    b2d = ap(nc.dram_tensor("b2d", [128, HB2D * 512], F32, kind="ExternalInput"))
    mtri4 = ap(nc.dram_tensor("mtri4", [128, 2048], F32, kind="ExternalInput"))
    ctab = ap(nc.dram_tensor("ctab", [128, NCT], F32, kind="ExternalInput"))
    ident = ap(nc.dram_tensor("ident", [128, 128], BF16, kind="ExternalInput"))
    out_sh = ap(nc.dram_tensor("out_shard", [SBT, TOK], F32, kind="ExternalOutput"))

    xn_sp = ap(nc.dram_tensor("xn_sp", [D, SBT], BF16, kind="Internal"))
    cc2_in = nc.dram_tensor("cc2_in", [TOK, SBT], BF16, kind="Internal")
    cc2_out = nc.dram_tensor("cc2_out", [D, SBT], BF16, kind="Internal")
    qk_sp = ap(nc.dram_tensor("qk_sp", [2048, SBT], BF16, kind="Internal"))
    v_sp = ap(nc.dram_tensor("v_sp", [SBT, 1024], BF16, kind="Internal"))
    masks_r = masks.rearrange("l s c -> (l s) c")
    beff_r = beff.rearrange("l r c -> (l r) c")

    RG = [[0, 1, 2, 3], [4, 5, 6, 7]]
    SP_ALL = 16 * 2 * NSW * NTC

    es = ExitStack()
    sem = {}
    for name in ["ld_x0", "ld_x1", "ld_c", "ld_w0", "ld_w1", "ld_xn0", "ld_xn1",
                 "ld_kv0", "ld_kv1", "ld_qt0", "ld_qt1", "ld_cw",
                 "ld_ao0", "ld_ao1", "ld_ao2", "ld_xnm0", "ld_xnm1", "ld_xnm2",
                 "ld_mk0", "ld_mk1", "ld_mk2", "ld_mk3",
                 "ld_mk4", "ld_mk5", "ld_mk6", "ld_mk7",
                 "st_out0", "st_out1", "st_out2", "st_out3", "ld_xc", "ld_hl",
                 "s_lnA", "s_lnV", "s_tp", "s_ev", "st_xn",
                 "p_mm", "e_qk", "e_v", "st_sp",
                 "t_qk", "t_exp", "t_msk", "t_av", "t_dve", "t_sm", "t_ss", "st_cc2", "cc2s", "c_act", "c_hev", "c_fo",
                 "c_mm", "c_dve", "st_out"]:
        sem[name] = es.enter_context(nc.semaphore(name))

    mtri_sb = es.enter_context(nc.sbuf_tensor("mtri_sb", [128, 128], F32))
    btab_sb = es.enter_context(nc.sbuf_tensor("btab_sb", [128, NV], F32))
    ident_sb = es.enter_context(nc.sbuf_tensor("ident_sb", [128, 128], BF16))
    bqkvqk_sb = es.enter_context(nc.sbuf_tensor("bqkvqk_sb", [128, 16], F32))

    with es, nc.Block() as block:
        with ExitStack() as p01:
            xa = p01.enter_context(nc.sbuf_tensor("xa", [128, 2 * D], F32))
            xsq = p01.enter_context(nc.sbuf_tensor("xsq", [128, 2048], F32))
            xnbf = p01.enter_context(nc.sbuf_tensor("xnbf", [128, 3 * 2048], BF16))
            xnT = p01.enter_context(nc.sbuf_tensor("xnT", [128, NDM * 256], BF16))
            lnv_ = p01.enter_context(nc.sbuf_tensor("lnv", [128, 12], F32))
            pt0 = p01.enter_context(nc.psum_tensor("pt0", [128, 512], BF16))
            pt1 = p01.enter_context(nc.psum_tensor("pt1", [128, 512], BF16))
            ptb = [pt0, pt1]
            wsw = p01.enter_context(nc.sbuf_tensor("wsw", [128, 2 * NDM * 768], BF16))
            xnsb = p01.enter_context(nc.sbuf_tensor("xnsb", [128, 2 * NDM * TCW], BF16))
            vbias = p01.enter_context(nc.sbuf_tensor("vbias", [128, 1024], BF16))
            qkev = p01.enter_context(nc.sbuf_tensor("qkev", [128, 4 * TCW], BF16))
            vev = p01.enter_context(nc.sbuf_tensor("vev", [128, 2 * TCW], BF16))
            qkp = [p01.enter_context(nc.psum_tensor(f"qkp{i}", [128, TCW], F32)) for i in range(2)]
            vp = [p01.enter_context(nc.psum_tensor(f"vp{i}", [128, TCW], F32)) for i in range(2)]
            COL = {k: i for i, k in enumerate(
                ["rs", "rssa", "rssb", "rss", "mean", "ms", "var", "lnvv", "rstd", "nb", "eps"])}

            def lv(k):
                return lnv_[:, COL[k]:COL[k] + 1]

            def wb_xn(eng, j):
                eng.dma_start(
                    out=xn_sp[:, 256 * j:256 * j + 256].rearrange("(dm p) c -> p dm c", p=128),
                    in_=xnT.ap().rearrange("p (dm c) -> p dm c", c=256)).then_inc(sem["st_xn"], 16)

            @block.sync
            def _(s):
                for j in range(16):
                    for t2 in range(2):
                        i = 2 * j + t2
                        if i >= 2:
                            s.wait_ge(sem["s_lnA"], 6 * i - 6)
                        s.dma_start(out=xa[:, (i % 2) * D:(i % 2) * D + D],
                                    in_=x_full[128 * i:128 * i + 128, :]).then_inc(sem[f"ld_x{i % 2}"], 16)
                    if j >= 1:
                        s.wait_ge(sem["s_ev"], 16 * j)
                        wb_xn(s, j - 1)
                s.wait_ge(sem["s_ev"], 16 * 16)
                wb_xn(s, 15)
                s.dma_start(out=vbias[:, :], in_=bqkv_vb).then_inc(sem["ld_c"], 16)
                s.wait_ge(sem["st_xn"], 16 * 16)
                for gi in range(NSW * NTC):
                    sw, tc = divmod(gi, NTC)
                    if gi >= 2:
                        s.wait_ge(sem["p_mm"], 6 * (gi - 1))
                    for q4 in range(4):
                        src = xn_sp[1024 * q4:1024 * q4 + 1024, 256 * tc:256 * tc + 256] \
                            .rearrange("(dd p) c -> p dd c", p=128)
                        dst = xnsb.ap().rearrange("p (dm c) -> p dm c", c=TCW)[
                            :, (gi % 2) * NDM + 8 * q4:(gi % 2) * NDM + 8 * q4 + 8, :]
                        s.dma_start(out=dst, in_=src).then_inc(sem[f"ld_xn{gi % 2}"], 16)

            @block.scalar
            def _(sc):
                sc.dma_start(out=mtri_sb[:, :], in_=mtri).then_inc(sem["ld_c"], 16)
                sc.dma_start(out=btab_sb[:, :], in_=btab).then_inc(sem["ld_c"], 16)
                sc.dma_start(out=ident_sb[:, :], in_=ident).then_inc(sem["ld_c"], 16)
                sc.dma_start(out=bqkvqk_sb[:, :], in_=bqkv_qk).then_inc(sem["ld_c"], 16)
                for i in range(32):
                    xi = xa[:, (i % 2) * D:(i % 2) * D + D]
                    sa = ((2 * i) % 3) * 2048
                    sb = ((2 * i + 1) % 3) * 2048
                    sc.wait_ge(sem[f"ld_x{i % 2}"], 16 * (i // 2 + 1))
                    if i >= 1:
                        sc.wait_ge(sem["s_lnV"], 8 * i - 5)
                    sc.activation(xsq[:, :], xi[:, 0:2048], AF.Square).then_inc(sem["s_lnA"], 1)
                    sc.wait_ge(sem["s_lnV"], 8 * i + 2)
                    sc.activation(xsq[:, :], xi[:, 2048:4096], AF.Square).then_inc(sem["s_lnA"], 1)
                    sc.wait_ge(sem["s_lnV"], 8 * i + 7)
                    sc.activation(lv("lnvv"), lv("var"), AF.Ln, bias=lv("eps")).then_inc(sem["s_lnA"], 1)
                    sc.wait_ge(sem["s_lnA"], 6 * i + 3)
                    sc.activation(lv("rstd"), lv("lnvv"), AF.Exp, scale=-0.5).then_inc(sem["s_lnA"], 1)
                    sc.wait_ge(sem["s_lnV"], 8 * i + 8)
                    if i >= 2:
                        sc.wait_ge(sem["s_tp"], 8 * i - 8)
                    sc.activation(xnbf[:, sa:sa + 2048], xi[:, 0:2048], AF.Identity,
                                  bias=lv("nb"), scale=lv("rstd")).then_inc(sem["s_lnA"], 1)
                    if i >= 1:
                        sc.wait_ge(sem["s_tp"], 8 * i - 4)
                    sc.activation(xnbf[:, sb:sb + 2048], xi[:, 2048:4096], AF.Identity,
                                  bias=lv("nb"), scale=lv("rstd")).then_inc(sem["s_lnA"], 1)

            @block.vector
            def _(v):
                def evac_tile(v, i):
                    if i >= 2 and i % 2 == 0:
                        v.wait_ge(sem["st_xn"], 16 * (i // 2))
                    for b in range(8):
                        gi = 8 * i + b
                        v.wait_ge(sem["s_tp"], gi + 1)
                        dst = xnT.ap().rearrange("p (dm c) -> p dm c", c=256)[
                            :, 4 * b:4 * b + 4, 128 * (i % 2):128 * (i % 2) + 128]
                        src = ptb[gi % 2].ap().rearrange("p (j c) -> p j c", c=128)
                        v.tensor_copy(dst, src).then_inc(sem["s_ev"], 1)

                v.memset(lv("eps"), 1e-5)
                for i in range(32):
                    xi = xa[:, (i % 2) * D:(i % 2) * D + D]
                    v.wait_ge(sem[f"ld_x{i % 2}"], 16 * (i // 2 + 1))
                    v.reduce_sum(lv("rs"), xi, axis=AX).then_inc(sem["s_lnV"], 1)
                    v.wait_ge(sem["s_lnA"], 6 * i + 1)
                    v.reduce_sum(lv("rssa"), xsq[:, :], axis=AX).then_inc(sem["s_lnV"], 1)
                    v.wait_ge(sem["s_lnA"], 6 * i + 2)
                    v.reduce_sum(lv("rssb"), xsq[:, :], axis=AX).then_inc(sem["s_lnV"], 1)
                    v.wait_ge(sem["s_lnV"], 8 * i + 3)
                    v.tensor_add(lv("rss"), lv("rssa"), lv("rssb")).then_inc(sem["s_lnV"], 1)
                    v.wait_ge(sem["s_lnV"], 8 * i + 1)
                    v.tensor_scalar_mul(lv("mean"), lv("rs"), 1.0 / D).then_inc(sem["s_lnV"], 1)
                    v.wait_ge(sem["s_lnV"], 8 * i + 5)
                    v.tensor_mul(lv("ms"), lv("mean"), lv("mean")).then_inc(sem["s_lnV"], 1)
                    v.wait_ge(sem["s_lnV"], 8 * i + 6)
                    if i >= 1:
                        v.wait_ge(sem["s_lnA"], 6 * i - 3)
                    v.scalar_tensor_tensor(lv("var"), lv("rss"), 1.0 / D, lv("ms"),
                                           ALU.mult, ALU.subtract).then_inc(sem["s_lnV"], 1)
                    v.wait_ge(sem["s_lnA"], 6 * i + 4)
                    v.scalar_tensor_tensor(lv("nb"), lv("mean"), -1.0, lv("rstd"),
                                           ALU.mult, ALU.mult).then_inc(sem["s_lnV"], 1)
                    if i >= 1:
                        evac_tile(v, i - 1)
                evac_tile(v, 31)

            @block.tensor
            def _(t):
                t.wait_ge(sem["ld_c"], 16 * 4)
                for i in range(32):
                    sa = ((2 * i) % 3) * 2048
                    sb = ((2 * i + 1) % 3) * 2048
                    for b in range(8):
                        gi = 8 * i + b
                        t.wait_ge(sem["s_lnA"], 6 * i + (5 if b < 4 else 6))
                        if gi >= 2:
                            t.wait_ge(sem["s_ev"], gi - 1)
                        last = None
                        for j in range(4):
                            dmc = 4 * b + j
                            off = (sa + 128 * dmc) if b < 4 else (sb + 128 * (dmc - 16))
                            last = t.transpose(
                                ptb[gi % 2][:, 128 * j:128 * j + 128],
                                xnbf[:, off:off + 128],
                                ident_sb[:, :])
                        last.then_inc(sem["s_tp"], 1)


            @block.gpsimd
            def _(g):
                def load_sweep(s):
                    if s >= 2:
                        g.wait_ge(sem["p_mm"], 6 * NTC * (s - 1))
                    for q4 in range(4):
                        src = wqkvT[128 * 8 * q4:128 * 8 * (q4 + 1), 768 * s:768 * s + 768] \
                            .rearrange("(dd p) c -> p dd c", p=128)
                        dst = wsw.ap().rearrange("p (dm c) -> p dm c", c=768)[
                            :, (s % 2) * NDM + 8 * q4:(s % 2) * NDM + 8 * q4 + 8, :]
                        g.dma_start(out=dst, in_=src).then_inc(sem[f"ld_w{s % 2}"], 16)

                load_sweep(0)
                load_sweep(1)
                for gi in range(NSW * NTC):
                    if gi % NTC == 0 and 2 <= gi // NTC + 1 < NSW:
                        load_sweep(gi // NTC + 1)
                    s, tc = divmod(gi, NTC)
                    g.wait_ge(sem["e_qk"], 4 * (gi + 1))
                    dstq = qk_sp[512 * s:512 * s + 512, TCW * tc:TCW * tc + TCW] \
                        .rearrange("(oc p) c -> p oc c", p=128)
                    g.dma_start(out=dstq, in_=qkev.ap().rearrange("p (oc c) -> p oc c", c=TCW)) \
                        .then_inc(sem["st_sp"], 16)
                    g.wait_ge(sem["e_v"], 2 * (gi + 1))
                    dstv = v_sp[TCW * tc:TCW * tc + TCW, 256 * s:256 * s + 256] \
                        .rearrange("(tt p) c -> p tt c", p=128)
                    g.dma_start(out=dstv, in_=vev.ap().rearrange("p (tt c) -> p tt c", c=TCW)) \
                        .then_inc(sem["st_sp"], 16)

            @block.tensor
            def _(t):
                qk_guard = [0, 0]
                v_guard = [0, 0]
                for gi in range(NSW * NTC):
                    s, tc = divmod(gi, NTC)
                    if tc == 0:
                        t.wait_ge(sem[f"ld_w{s % 2}"], 16 * 4 * (s // 2 + 1))
                    t.wait_ge(sem[f"ld_xn{gi % 2}"], 16 * 4 * (gi // 2 + 1))
                    wof = (s % 2) * NDM * 768
                    xof = (gi % 2) * NDM * TCW
                    for oc in range(4):
                        if qk_guard[oc % 2]:
                            t.wait_ge(sem["e_qk"], qk_guard[oc % 2])
                        last = None
                        for dm in range(NDM):
                            last = t.matmul(qkp[oc % 2][:, :],
                                            wsw[:, wof + 768 * dm + 128 * oc:wof + 768 * dm + 128 * oc + 128],
                                            xnsb[:, xof + TCW * dm:xof + TCW * dm + TCW],
                                            start=(dm == 0), stop=(dm == NDM - 1))
                        last.then_inc(sem["p_mm"], 1)
                        qk_guard[oc % 2] = 4 * gi + oc + 1
                    for tt in range(2):
                        if v_guard[tt % 2]:
                            t.wait_ge(sem["e_v"], v_guard[tt % 2])
                        last = None
                        for dm in range(NDM):
                            last = t.matmul(vp[tt % 2][:, :],
                                            xnsb[:, xof + TCW * dm + 128 * tt:xof + TCW * dm + 128 * tt + 128],
                                            wsw[:, wof + 768 * dm + 512:wof + 768 * dm + 768],
                                            start=(dm == 0), stop=(dm == NDM - 1))
                        last.then_inc(sem["p_mm"], 1)
                        v_guard[tt % 2] = 2 * gi + tt + 1

            @block.scalar
            def _(sc):
                sc.wait_ge(sem["ld_c"], 16 * 4)
                for gi in range(NSW * NTC):
                    s, tc = divmod(gi, NTC)
                    for oc in range(4):
                        sc.wait_ge(sem["p_mm"], 6 * gi + oc + 1)
                        if gi >= 1:
                            sc.wait_ge(sem["st_sp"], 16 * 2 * gi)
                        sc.activation(qkev[:, TCW * oc:TCW * oc + TCW], qkp[oc % 2][:, :],
                                      AF.Identity, bias=bqkvqk_sb[:, 4 * s + oc:4 * s + oc + 1]) \
                            .then_inc(sem["e_qk"], 1)

            @block.vector
            def _(v):
                v.wait_ge(sem["ld_c"], 16 * 5)
                for gi in range(NSW * NTC):
                    s, tc = divmod(gi, NTC)
                    for tt in range(2):
                        v.wait_ge(sem["p_mm"], 6 * gi + 4 + tt + 1)
                        if gi >= 1:
                            v.wait_ge(sem["st_sp"], 16 * 2 * gi)
                        v.tensor_add(vev[:, TCW * tt:TCW * tt + TCW], vp[tt % 2][:, :],
                                     vbias[:, 256 * s:256 * s + 256]).then_inc(sem["e_v"], 1)

        with ExitStack() as p2:
            ktsb = p2.enter_context(nc.sbuf_tensor("ktsb", [128, 2 * SBT], BF16))
            vsb = p2.enter_context(nc.sbuf_tensor("vsb", [128, 2 * SBT], BF16))
            qtsb = p2.enter_context(nc.sbuf_tensor("qtsb", [128, 2 * 512], BF16))
            etc_l = [p2.enter_context(nc.sbuf_tensor(f"etc{j}", [128, 512], BF16)) for j in range(2)]
            etd_l = [p2.enter_context(nc.sbuf_tensor(f"etd{j}", [128, 512], BF16)) for j in range(4)]
            etn_l = [p2.enter_context(nc.sbuf_tensor(f"etn{j}", [128, 512], BF16)) for j in range(2)]
            smb = [p2.enter_context(nc.sbuf_tensor(f"sm{j}", [128, 128], F32)) for j in range(2)]
            smn = [p2.enter_context(nc.sbuf_tensor(f"smn{j}", [128, 512], F32)) for j in range(2)]
            b2d_sb = p2.enter_context(nc.sbuf_tensor("b2d_sb", [128, HB2D * 512], F32))
            ctab_sb = p2.enter_context(nc.sbuf_tensor("ctab_sb", [128, NCT], F32))
            mtri4_sb = p2.enter_context(nc.sbuf_tensor("mtri4_sb", [128, 2048], F32))
            ones_c = p2.enter_context(nc.sbuf_tensor("ones_c", [128, 1], BF16))
            ones_r = p2.enter_context(nc.sbuf_tensor("ones_r", [1, 128], BF16))
            rinv_f = p2.enter_context(nc.sbuf_tensor("rinv_f", [1, 512], F32))
            rinv_b = p2.enter_context(nc.sbuf_tensor("rinv_b", [1, 512], BF16))
            rbsb = p2.enter_context(nc.sbuf_tensor("rbsb", [128, 512], F32))
            osb = p2.enter_context(nc.sbuf_tensor("osb", [128, 2 * 512], BF16))
            stp_ = [p2.enter_context(nc.psum_tensor(f"stp{i}", [128, 512], F32)) for i in range(2)]
            otp = p2.enter_context(nc.psum_tensor("otp", [128, 512], F32))
            lp = p2.enter_context(nc.psum_tensor("lp", [1, 512], F32))
            rbp = p2.enter_context(nc.psum_tensor("rbp", [128, 512], F32))

            def etbuf(h, g, kt):
                j = kt - 4 * g
                if h < HB2D:
                    return etn_l[kt % 2]
                return etd_l[j] if j >= 0 else etc_l[kt % 2]

            @block.sync
            def _(s_):
                s_.wait_ge(sem["st_sp"], SP_ALL)
                s_.dma_start(out=b2d_sb[:, :], in_=b2d).then_inc(sem["ld_c"], 16)
                s_.dma_start(out=mtri4_sb[:, :], in_=mtri4).then_inc(sem["ld_c"], 16)
                s_.dma_start(out=ctab_sb[:, :], in_=ctab).then_inc(sem["ld_c"], 16)
                for h in range(NH):
                    if h >= 2:
                        s_.wait_ge(sem["t_av"], AV_AFTER_HEAD[h - 1])
                    s_.dma_start(out=ktsb[:, (h % 2) * SBT:(h % 2) * SBT + SBT],
                                 in_=qk_sp[256 * h + 128:256 * h + 256, :]).then_inc(sem[f"ld_kv{h % 2}"], 16)
                    for q4 in range(4):
                        src = v_sp[1024 * q4:1024 * (q4 + 1), 128 * h:128 * h + 128] \
                            .rearrange("(kt p) c -> p kt c", p=128)
                        dst = vsb.ap().rearrange("p (kt c) -> p kt c", c=128)[
                            :, (h % 2) * NDM + 8 * q4:(h % 2) * NDM + 8 * q4 + 8, :]
                        s_.dma_start(out=dst, in_=src).then_inc(sem[f"ld_kv{h % 2}"], 16)
                    for g in range(NG):
                        hg = h * NG + g
                        if hg >= 2:
                            h2, g2_ = divmod(hg - 2, NG)
                            s_.wait_ge(sem["t_qk"], QKC[(h2, g2_)] + len(LIVE[(h2, g2_)]))
                        s_.dma_start(out=qtsb[:, (g % 2) * 512:(g % 2) * 512 + 512],
                                     in_=qk_sp[256 * h:256 * h + 128, 512 * g:512 * g + 512]) \
                            .then_inc(sem[f"ld_qt{g % 2}"], 16)

            @block.tensor
            def _(t):
                t.wait_ge(sem["st_sp"], SP_ALL)
                t.wait_ge(sem["t_dve"], 1)
                bank_guard = [None, None]
                for h in range(NH):
                    t.wait_ge(sem[f"ld_kv{h % 2}"], 16 * 5 * (h // 2 + 1))
                    kof = (h % 2) * SBT
                    vof = (h % 2) * SBT
                    for g in range(NG):
                        hg = h * NG + g
                        lv = LIVE[(h, g)]
                        qb = QKC[(h, g)]
                        t.wait_ge(sem[f"ld_qt{g % 2}"], 16 * (h * 4 + g // 2 + 1))
                        qt_ap = qtsb[:, (g % 2) * 512:(g % 2) * 512 + 512]

                        def st_mm(i, h=h, g=g, qt_ap=qt_ap, qb=qb, lv=lv):
                            kt = lv[i]
                            if bank_guard[kt % 2] is not None:
                                sname, v = bank_guard[kt % 2]
                                t.wait_ge(sem[sname], v)
                            t.matmul(stp_[kt % 2][:, :],
                                     ktsb[:, kof + 128 * kt:kof + 128 * kt + 128],
                                     qt_ap, start=True, stop=True).then_inc(sem["t_qk"], 1)
                            if h < HB2D:
                                bank_guard[kt % 2] = ("t_sm", SMC[(h, g, kt)])
                            else:
                                bank_guard[kt % 2] = ("t_exp", EXPC[(h, g, kt)])

                        st_mm(0)
                        if len(lv) > 1:
                            st_mm(1)
                        for i, kt in enumerate(lv):
                            t.wait_ge(sem["t_exp"], EXPC[(h, g, kt)])
                            if i == 0 and hg > 0:
                                t.wait_ge(sem["t_dve"], 4 * hg + 1)
                            eb = etbuf(h, g, kt)
                            t.matmul(otp[:, :], vsb[:, vof + 128 * kt:vof + 128 * kt + 128],
                                     eb[:, :], start=(i == 0), stop=(i == len(lv) - 1),
                                     skip_group_check=True).then_inc(sem["t_av"], 1)
                            t.matmul(lp[:, :], ones_c[:, :], eb[:, :],
                                     start=(i == 0), stop=(i == len(lv) - 1),
                                     skip_group_check=True).then_inc(sem["t_av"], 1)
                            if i + 2 < len(lv):
                                st_mm(i + 2)
                        t.wait_ge(sem["t_dve"], 4 * hg + 3)
                        t.matmul(rbp[:, :], ones_r[:, :], rinv_b[:, :],
                                 start=True, stop=True).then_inc(sem["t_qk"], 1)

            @block.vector
            def _(v):
                v.wait_ge(sem["st_sp"], SP_ALL)
                v.wait_ge(sem["ld_c"], 16 * 8)
                v.memset(ones_c[:, :], 1.0)
                v.memset(ones_r[:, :], 1.0)
                v.memset(etd_l[1][:, 0:128], 0.0)
                v.memset(etd_l[2][:, 0:256], 0.0)
                v.memset(etd_l[3][:, 0:384], 0.0).then_inc(sem["t_dve"], 1)
                dve_cum = 1
                ss_cum = 0
                sm_guard = [0, 0]
                smn_guard = [0, 0]
                for h in range(NH):
                    for g in range(NG):
                        hg = h * NG + g
                        lv = LIVE[(h, g)]
                        qb = QKC[(h, g)]
                        if h < HB2D:
                            for i, kt in enumerate(lv):
                                j = kt - 4 * g
                                v.wait_ge(sem["t_qk"], qb + i + 1)
                                if smn_guard[kt % 2]:
                                    v.wait_ge(sem["t_exp"], smn_guard[kt % 2])
                                ci = CTI[(h, kt - 4 * g)]
                                last = v.scalar_tensor_tensor(
                                    smn[kt % 2][:, :], stp_[kt % 2][:, :],
                                    ctab_sb[:, ci:ci + 1],
                                    b2d_sb[:, 512 * h:512 * h + 512], ALU.add, ALU.add)
                                if j >= 0:
                                    last.then_inc(sem["t_ss"], 1)
                                    ss_cum += 1
                                    v.wait_ge(sem["t_ss"], ss_cum)
                                    last = v.tensor_add(
                                        smn[kt % 2][:, 0:128 * (j + 1)],
                                        smn[kt % 2][:, 0:128 * (j + 1)],
                                        mtri4_sb[:, 512 * j:512 * j + 128 * (j + 1)])
                                last.then_inc(sem["t_sm"], 1)
                                smn_guard[kt % 2] = EXPC[(h, g, kt)]
                        else:
                            for kt in range(4 * g, 4 * g + 4):
                                j = kt - 4 * g
                                i = lv.index(kt)
                                v.wait_ge(sem["t_qk"], qb + i + 1)
                                if sm_guard[j % 2]:
                                    v.wait_ge(sem["t_exp"], sm_guard[j % 2])
                                v.tensor_add(smb[j % 2][:, :], stp_[kt % 2][:, 128 * j:128 * j + 128],
                                             mtri_sb[:, :]).then_inc(sem["t_msk"], 1)
                                sm_guard[j % 2] = EXPC[(h, g, kt)]
                        v.wait_ge(sem["t_av"], 2 * EXPC[(h, g, lv[-1])])
                        v.reciprocal(rinv_f[:, :], lp[:, :]).then_inc(sem["t_dve"], 1)
                        dve_cum += 1
                        v.wait_ge(sem["t_dve"], dve_cum)
                        v.tensor_copy(rinv_b[:, :], rinv_f[:, :]).then_inc(sem["t_dve"], 1)
                        dve_cum += 1
                        v.wait_ge(sem["t_qk"], qb + len(lv) + 1)
                        v.tensor_copy(rbsb[:, :], rbp[:, :]).then_inc(sem["t_dve"], 1)
                        dve_cum += 1
                        v.wait_ge(sem["t_dve"], dve_cum)
                        if hg >= 2:
                            v.wait_ge(sem["st_cc2"], 16 * hg)
                        v.tensor_mul(osb[:, (hg % 2) * 512:(hg % 2) * 512 + 512], otp[:, :],
                                     rbsb[:, :]).then_inc(sem["t_dve"], 1)
                        dve_cum += 1

            @block.scalar
            def _(sc):
                sc.wait_ge(sem["ld_c"], 16 * 4)
                sc.wait_ge(sem["t_dve"], 1)
                etc_guard = [0, 0]
                etd_guard = [0, 0, 0, 0]
                etn_guard = [0, 0]
                for h in range(NH):
                    for g in range(NG):
                        hg = h * NG + g
                        lv = LIVE[(h, g)]
                        qb = QKC[(h, g)]
                        for i, kt in enumerate(lv):
                            j = kt - 4 * g
                            if h < HB2D:
                                sc.wait_ge(sem["t_sm"], SMC[(h, g, kt)])
                                if etn_guard[kt % 2]:
                                    sc.wait_ge(sem["t_av"], etn_guard[kt % 2])
                                dest = etn_l[kt % 2]
                                etn_guard[kt % 2] = 2 * EXPC[(h, g, kt)]
                                sc.activation(dest[:, :], smn[kt % 2][:, :], AF.Exp,
                                              scale=SCL).then_inc(sem["t_exp"], 1)
                                continue
                            sc.wait_ge(sem["t_qk"], qb + i + 1)
                            if j >= 0:
                                sc.wait_ge(sem["t_msk"], MSKC[(h, g, kt)])
                                if etd_guard[j]:
                                    sc.wait_ge(sem["t_av"], etd_guard[j])
                                dest = etd_l[j]
                                etd_guard[j] = 2 * EXPC[(h, g, kt)]
                            else:
                                if etc_guard[kt % 2]:
                                    sc.wait_ge(sem["t_av"], etc_guard[kt % 2])
                                dest = etc_l[kt % 2]
                                etc_guard[kt % 2] = 2 * EXPC[(h, g, kt)]
                            last = None
                            for (a, b, srck, col) in _exp_pieces(h, g, kt):
                                if srck == "sm":
                                    srcp = smb[j % 2][:, a - 128 * j:b - 128 * j]
                                else:
                                    srcp = stp_[kt % 2][:, a:b]
                                last = sc.activation(dest[:, a:b], srcp, AF.Exp,
                                                     bias=btab_sb[:, col:col + 1], scale=SCL)
                            last.then_inc(sem["t_exp"], 1)

            @block.gpsimd
            def _(g_):
                for h in range(NH):
                    for g in range(NG):
                        hg = h * NG + g
                        g_.wait_ge(sem["t_dve"], 1 + 4 * (hg + 1))
                        g_.dma_start(out=cc2_in.ap()[128 * h:128 * h + 128, 512 * g:512 * g + 512],
                                     in_=osb[:, (hg % 2) * 512:(hg % 2) * 512 + 512]) \
                            .then_inc(sem["st_cc2"], 16)
                g_.wait_ge(sem["st_cc2"], 16 * NH * NG)
                g_.collective_compute("AllGather", ALU.bypass, replica_groups=RG,
                                      ins=[cc2_in.ap()], outs=[cc2_out.ap()]).then_inc(sem["cc2s"])

        with ExitStack() as p3:
            wout_sb = p3.enter_context(nc.sbuf_tensor("wout_sb", [128, NDM * 1024], BF16))
            w0_sb = p3.enter_context(nc.sbuf_tensor("w0_sb", [128, NDM * 1024], BF16))
            aeff_sb = p3.enter_context(nc.sbuf_tensor("aeff_sb", [128, NDM * 128], BF16))
            beff_sb = p3.enter_context(nc.sbuf_tensor("beff_sb", [32, 4 * 1024], BF16))
            bias_bc = p3.enter_context(nc.sbuf_tensor("bias_bc", [128, 1024], F32))
            aorng = p3.enter_context(nc.sbuf_tensor("aorng", [128, 3 * 8 * TCW], BF16))
            xnrng = p3.enter_context(nc.sbuf_tensor("xnrng", [128, 3 * 8 * TCW], BF16))
            mkrng = p3.enter_context(nc.sbuf_tensor("mkrng", [128, 4 * 1024], BF16))
            xcs = p3.enter_context(nc.sbuf_tensor("xcs", [128, 2048], F32))
            hsb = p3.enter_context(nc.sbuf_tensor("hsb", [128, TCW], BF16))
            hlsb = p3.enter_context(nc.sbuf_tensor("hlsb", [32, 1024], BF16))
            ysb = p3.enter_context(nc.sbuf_tensor("ysb", [128, 4 * 512], BF16))
            fsb = [p3.enter_context(nc.sbuf_tensor(f"fsb{i}", [128, 512], F32)) for i in range(2)]
            fout = [p3.enter_context(nc.sbuf_tensor(f"fout{i}", [128, 512], F32)) for i in range(2)]
            tmp_sb = p3.enter_context(nc.sbuf_tensor("tmp_sb", [128, 512], F32))
            hp = p3.enter_context(nc.psum_tensor("hp", [128, TCW], F32))
            Pp = [[p3.enter_context(nc.psum_tensor(f"Pp{t_}{o_}", [128, 512], F32))
                   for o_ in range(2)] for t_ in range(2)]
            yp = [p3.enter_context(nc.psum_tensor(f"yp{i}", [128, 512], F32)) for i in range(2)]

            @block.gpsimd
            def _(g_):
                g_.wait_ge(sem["t_av"], 2 * T_EXP_ALL)
                g_.wait_ge(sem["t_dve"], 1 + 4 * NH * NG)
                for (srcT, dstT, cw) in ((woutT, wout_sb, 1024), (w0T, w0_sb, 1024),
                                         (aeff, aeff_sb, 128)):
                    for q4 in range(4):
                        srcq = srcT[128 * 8 * q4:128 * 8 * (q4 + 1), :] \
                            .rearrange("(dd p) c -> p dd c", p=128)
                        dst = dstT.ap().rearrange("p (dm c) -> p dm c", c=cw)[
                            :, 8 * q4:8 * q4 + 8, :]
                        g_.dma_start(out=dst, in_=srcq).then_inc(sem["ld_cw"], 16)
                for l in range(4):
                    g_.dma_start(out=beff_sb[:, 1024 * l:1024 * l + 1024],
                                 in_=beff_r[32 * l:32 * l + 32, :]).then_inc(sem["ld_cw"], 16)
                mk_guard = [0, 0, 0, 0]
                for g2 in range(NG2):
                    g_.wait_ge(sem["c_hev"], g2 + 1)
                    if g2 >= 1:
                        g_.wait_ge(sem["c_mm"], 48 * g2)
                    for l in range(4):
                        g_.dma_start(out=hlsb[:, TCW * l:TCW * l + TCW],
                                     in_=hsb[32 * l:32 * l + 32, :]).then_inc(sem["ld_hl"], 16)
                    for oc2 in range(2):
                        for l in range(4):
                            if mk_guard[l]:
                                g_.wait_ge(sem["c_dve"], mk_guard[l])
                            srcm = masks_r[SBT * l + TCW * g2:SBT * l + TCW * g2 + TCW,
                                           512 * oc2:512 * oc2 + 512] \
                                .rearrange("(tt p) c -> p tt c", p=128)
                            dst = mkrng.ap()[:, 1024 * l:1024 * l + 1024] \
                                .rearrange("p (tt c) -> p tt c", c=512)
                            g_.dma_start(out=dst, in_=srcm).then_inc(sem[f"ld_mk{l}"], 16)
                            mk_guard[l] = _cd_tt(g2, 2 * oc2 + 1, l)

            @block.sync
            def _(s_):
                s_.wait_ge(sem["cc2s"], 1)
                s_.dma_start(out=bias_bc[:, :], in_=b0bc).then_inc(sem["ld_c"], 16)
                s_.dma_start(out=fsb[0][:, :], in_=boutbc[:, 0:512]).then_inc(sem["ld_c"], 16)
                s_.dma_start(out=fsb[1][:, :], in_=boutbc[:, 512:1024]).then_inc(sem["ld_c"], 16)
                for g2 in range(NG2):
                    if g2 >= 1:
                        for fo, (tt2, oc2) in enumerate(FIS):
                            s_.wait_ge(sem["c_fo"], 4 * (g2 - 1) + fo + 1)
                            s_.dma_start(
                                out=out_sh[TCW * (g2 - 1) + 128 * tt2:TCW * (g2 - 1) + 128 * tt2 + 128,
                                           512 * oc2:512 * oc2 + 512],
                                in_=fout[fo % 2][:, :]).then_inc(sem[f"st_out{fo % 2}"], 16)
                    for k in range(4):
                        u = g2 * 4 + k
                        if u >= 3:
                            s_.wait_ge(sem["c_mm"], _cm_dm(*divmod(u - 3, 4)))
                        srcx = xn_sp[1024 * k:1024 * k + 1024,
                                     TCW * g2:TCW * g2 + TCW] \
                            .rearrange("(dd p) c -> p dd c", p=128)
                        dst = xnrng.ap().rearrange("p (dm c) -> p dm c", c=TCW)[
                            :, (u % 3) * 8:(u % 3) * 8 + 8, :]
                        s_.dma_start(out=dst, in_=srcx).then_inc(sem[f"ld_xnm{u % 3}"], 16)
                    if g2 >= 1:
                        s_.wait_ge(sem["c_dve"], _cd_xc(g2 - 1, 3))
                    srcx = x_cols[TCW * g2:TCW * g2 + TCW, :].rearrange("(tt p) c -> p tt c", p=128)
                    s_.dma_start(out=xcs.ap().rearrange("p (tt c) -> p tt c", c=1024),
                                 in_=srcx).then_inc(sem["ld_xc"], 16)
                g2 = NG2 - 1
                for fo, (tt2, oc2) in enumerate(FIS):
                    s_.wait_ge(sem["c_fo"], 4 * g2 + fo + 1)
                    s_.dma_start(out=out_sh[TCW * g2 + 128 * tt2:TCW * g2 + 128 * tt2 + 128,
                                            512 * oc2:512 * oc2 + 512],
                                 in_=fout[fo % 2][:, :]).then_inc(sem[f"st_out{fo % 2}"], 16)
                s_.wait_ge(sem["st_out0"], 16 * 2 * NG2)
                s_.wait_ge(sem["st_out1"], 16 * 2 * NG2)

            @block.scalar
            def _(sc):
                sc.wait_ge(sem["cc2s"], 1)
                fs_guard = [0, 0]
                fout_guard = [0, 0]
                for g2 in range(NG2):
                    for k in range(4):
                        u = g2 * 4 + k
                        if u >= 3:
                            sc.wait_ge(sem["c_mm"], _cm_dm(*divmod(u - 3, 4)))
                        srca = cc2_out.ap()[128 * 8 * k:128 * 8 * (k + 1), TCW * g2:TCW * g2 + TCW] \
                            .rearrange("(dd p) c -> p dd c", p=128)
                        dst = aorng.ap().rearrange("p (dm c) -> p dm c", c=TCW)[
                            :, (u % 3) * 8:(u % 3) * 8 + 8, :]
                        sc.dma_start(out=dst, in_=srca).then_inc(sem[f"ld_ao{u % 3}"], 16)
                    sc.wait_ge(sem["c_mm"], 48 * g2 + 32)
                    if g2 >= 1:
                        sc.wait_ge(sem["ld_hl"], 16 * 4 * g2)
                    sc.copy(hsb[:, :], hp[:, :]).then_inc(sem["c_hev"], 1)

                    def fcopy(fo, g2=g2):
                        tt2, oc2 = FIS[fo]
                        if g2 == 0 and fo < 2:
                            sc.wait_ge(sem["c_dve"], 1)
                        if fs_guard[fo % 2]:
                            sc.wait_ge(sem[f"st_out{fo % 2}"], fs_guard[fo % 2])
                        sc.copy(fsb[fo % 2][:, :], Pp[tt2][oc2][:, :]).then_inc(sem["c_act"], 1)
                        fs_guard[fo % 2] = 16 * (2 * g2 + fo // 2 + 1)

                    def yevac(yi, g2=g2):
                        oc2, tt2, l = YIS[yi]
                        sc.wait_ge(sem["c_mm"], 48 * g2 + 32 + yi + 1)
                        if g2 > 0 or yi >= 4:
                            gy, yy = (g2, yi - 4) if yi >= 4 else (g2 - 1, 12 + yi)
                            sc.wait_ge(sem["c_dve"], _cd_tt(gy, 2 * YIS[yy][0] + YIS[yy][1], YIS[yy][2]))
                        sc.activation(ysb[:, 512 * (yi % 4):512 * (yi % 4) + 512], yp[yi % 2][:, :],
                                      AF.Identity, scale=8.0).then_inc(sem["c_act"], 1)

                    def foutcopy(fo, g2=g2):
                        sc.wait_ge(sem["c_dve"], _cd_final(g2, fo))
                        if fout_guard[fo % 2]:
                            sc.wait_ge(sem[f"st_out{fo % 2}"], fout_guard[fo % 2])
                        sc.copy(fout[fo % 2][:, :], fsb[fo % 2][:, :]).then_inc(sem["c_fo"], 1)
                        fout_guard[fo % 2] = 16 * (2 * g2 + fo // 2 + 1)

                    fcopy(0)
                    fcopy(1)
                    for yi in range(8):
                        yevac(yi)
                    foutcopy(0)
                    foutcopy(1)
                    fcopy(2)
                    fcopy(3)
                    for yi in range(8, 16):
                        yevac(yi)
                    foutcopy(2)
                    foutcopy(3)

            @block.tensor
            def _(t):
                t.wait_ge(sem["cc2s"], 1)
                t.wait_ge(sem["ld_cw"], 16 * 16)
                for g2 in range(NG2):
                    for k in range(4):
                        u = g2 * 4 + k
                        t.wait_ge(sem[f"ld_xnm{u % 3}"], 16 * (u // 3 + 1))
                        t.wait_ge(sem[f"ld_ao{u % 3}"], 16 * (u // 3 + 1))
                        for dd in range(8):
                            dm = 8 * k + dd
                            if dm == 0 and g2 >= 1:
                                t.wait_ge(sem["c_act"], _ca_F(g2 - 1, 3))
                                t.wait_ge(sem["c_hev"], g2)
                            xof = (u % 3) * 8 * TCW + TCW * dd
                            aof = (u % 3) * 8 * TCW + TCW * dd
                            for tt2 in range(2):
                                for oc2 in range(2):
                                    t.matmul(Pp[tt2][oc2][:, :],
                                             aorng[:, aof + 128 * tt2:aof + 128 * tt2 + 128],
                                             wout_sb[:, 1024 * dm + 512 * oc2:1024 * dm + 512 * oc2 + 512],
                                             start=(dm == 0), stop=False,
                                             skip_group_check=True)
                                    t.matmul(Pp[tt2][oc2][:, :],
                                             xnrng[:, xof + 128 * tt2:xof + 128 * tt2 + 128],
                                             w0_sb[:, 1024 * dm + 512 * oc2:1024 * dm + 512 * oc2 + 512],
                                             start=False, stop=(dm == NDM - 1),
                                             skip_group_check=True)
                            t.matmul(hp[:, :], aeff_sb[:, 128 * dm:128 * dm + 128],
                                     xnrng[:, xof:xof + TCW],
                                     start=(dm == 0), stop=(dm == NDM - 1),
                                     skip_group_check=True).then_inc(sem["c_mm"], 1)
                    t.wait_ge(sem["ld_hl"], 16 * 4 * (g2 + 1))
                    for yi, (oc2, tt2, l) in enumerate(YIS):
                        if yi >= 2:
                            t.wait_ge(sem["c_act"], _ca_y(g2, yi - 2))
                        elif g2 >= 1:
                            t.wait_ge(sem["c_act"], _ca_y(g2 - 1, 14 + yi))
                        t.matmul(yp[yi % 2][:, :],
                                 hlsb[:, TCW * l + 128 * tt2:TCW * l + 128 * tt2 + 128],
                                 beff_sb[:, 1024 * l + 512 * oc2:1024 * l + 512 * oc2 + 512],
                                 start=True, stop=True).then_inc(sem["c_mm"], 1)

            @block.vector
            def _(v):
                v.wait_ge(sem["ld_c"], 16 * 11)
                v.tensor_add(bias_bc[:, 0:512], bias_bc[:, 0:512], fsb[0][:, :])
                v.tensor_add(bias_bc[:, 512:1024], bias_bc[:, 512:1024],
                             fsb[1][:, :]).then_inc(sem["c_dve"], 1)
                cd = 1
                for g2 in range(NG2):
                    for fo, (tt2, oc2) in enumerate(FIS):
                        v.wait_ge(sem["c_act"], _ca_F(g2, fo))
                        v.wait_ge(sem["c_dve"], cd)
                        v.tensor_add(fsb[fo % 2][:, :], fsb[fo % 2][:, :],
                                     bias_bc[:, 512 * oc2:512 * oc2 + 512]).then_inc(sem["c_dve"], 1)
                        cd += 1
                        v.wait_ge(sem["ld_xc"], 16 * (g2 + 1))
                        v.wait_ge(sem["c_dve"], cd)
                        v.tensor_add(fsb[fo % 2][:, :], fsb[fo % 2][:, :],
                                     xcs[:, 1024 * tt2 + 512 * oc2:1024 * tt2 + 512 * oc2 + 512]) \
                            .then_inc(sem["c_dve"], 1)
                        cd += 1
                        for l in range(4):
                            yi = 8 * oc2 + 4 * tt2 + l
                            v.wait_ge(sem["c_act"], _ca_y(g2, yi))
                            v.wait_ge(sem[f"ld_mk{l}"], 16 * (2 * g2 + oc2 + 1))
                            v.wait_ge(sem["c_dve"], cd)
                            v.tensor_mul(tmp_sb[:, :], ysb[:, 512 * (yi % 4):512 * (yi % 4) + 512],
                                         mkrng[:, 1024 * l + 512 * tt2:1024 * l + 512 * tt2 + 512]) \
                                .then_inc(sem["c_dve"], 1)
                            cd += 1
                            v.wait_ge(sem["c_dve"], cd)
                            v.tensor_add(fsb[fo % 2][:, :], fsb[fo % 2][:, :], tmp_sb[:, :]) \
                                .then_inc(sem["c_dve"], 1)
                            cd += 1
    return nc



def _mask_u():
    def threefry_pair(k0, k1, x0, x1):
        ks = [np.uint32(k0), np.uint32(k1),
              np.uint32(np.uint32(k0) ^ np.uint32(k1) ^ np.uint32(0x1BD11BDA))]
        rot = [[13, 15, 26, 6], [17, 29, 16, 24]]
        x0 = x0 + ks[0]
        x1 = x1 + ks[1]
        for i in range(5):
            for r in rot[i % 2]:
                x0 = x0 + x1
                x1 = (x1 << np.uint32(r)) | (x1 >> np.uint32(32 - r))
                x1 = x1 ^ x0
            x0 = x0 + ks[(i + 1) % 3]
            x1 = x1 + ks[(i + 2) % 3] + np.uint32(i + 1)
        return x0, x1
    n = 2 * SBT * D
    idx = np.arange(n, dtype=np.uint64)
    hi = (idx >> np.uint64(32)).astype(np.uint32)
    lo = (idx & np.uint64(0xFFFFFFFF)).astype(np.uint32)
    o0, o1 = threefry_pair(0, 42, hi, lo)
    bits = o0 ^ o1
    f = ((bits >> np.uint32(9)) | np.uint32(0x3F800000)).view(np.float32) - np.float32(1.0)
    return f.reshape(2, SBT, D)


def kernel(L0_attn_x, L0_attn_w_qkv, L0_attn_b_qkv, L0_attn_w_out, L0_attn_b_out,
           L0_ffn_W0, L0_ffn_b0, L0_ffn_A, L0_ffn_B):
    x = np.asarray(L0_attn_x, dtype=np.float32)
    w_qkv = np.asarray(L0_attn_w_qkv, dtype=np.float32)
    b_qkv = np.asarray(L0_attn_b_qkv, dtype=np.float32)
    w_out = np.asarray(L0_attn_w_out, dtype=np.float32)
    b_out = np.asarray(L0_attn_b_out, dtype=np.float32)
    W0 = np.asarray(L0_ffn_W0, dtype=np.float32)
    b0 = np.asarray(L0_ffn_b0, dtype=np.float32)
    A = np.asarray(L0_ffn_A, dtype=np.float32)
    B = np.asarray(L0_ffn_B, dtype=np.float32)

    m = np.arange(1, 33, dtype=np.float32) * (8.0 / 32.0)
    slopes = (1.0 / np.power(np.float32(2.0), m)).astype(np.float32)
    u = _mask_u()
    keep = np.full(4, 0.25, np.float32)
    hi = np.cumsum(keep)
    lo = hi - keep

    d_perm = np.array([128 * (i + 4 * jj) + hd
                       for i in range(4) for jj in range(8) for hd in range(128)])
    woutT_perm = np.ascontiguousarray(w_out.T[d_perm])
    w0T_full = np.ascontiguousarray(W0.T)

    mtri_v = np.where(np.arange(128)[:, None] > np.arange(128)[None, :],
                      np.float32(MASKNEG), np.float32(0.0)).astype(np.float32)
    ident_v = np.eye(128, dtype=ml_dtypes.bfloat16)

    wq, wk, wv = w_qkv[0:D], w_qkv[D:2 * D], w_qkv[2 * D:3 * D]
    bq, bk, bv = b_qkv[0:D], b_qkv[D:2 * D], b_qkv[2 * D:3 * D]

    nc = build_nc()
    in_maps = []
    for c in range(8):
        r, beta = c % 4, c // 4
        heads = [r + 4 * j for j in range(NH)]
        cols = slice(1024 * r, 1024 * r + 1024)

        blocks = []
        vb_row = np.zeros(1024, np.float32)
        bqkvqk_v = np.zeros((128, 16), np.float32)
        for s in range(NSW):
            hA, hB = heads[2 * s], heads[2 * s + 1]
            blocks.append(np.concatenate([
                wq[128 * hA:128 * hA + 128], wk[128 * hA:128 * hA + 128],
                wq[128 * hB:128 * hB + 128], wk[128 * hB:128 * hB + 128],
                wv[128 * hA:128 * hA + 128], wv[128 * hB:128 * hB + 128]], axis=0))
            bqkvqk_v[:, 4 * s + 0] = bq[128 * hA:128 * hA + 128]
            bqkvqk_v[:, 4 * s + 1] = bk[128 * hA:128 * hA + 128]
            bqkvqk_v[:, 4 * s + 2] = bq[128 * hB:128 * hB + 128]
            bqkvqk_v[:, 4 * s + 3] = bk[128 * hB:128 * hB + 128]
            vb_row[256 * s:256 * s + 128] = bv[128 * hA:128 * hA + 128]
            vb_row[256 * s + 128:256 * s + 256] = bv[128 * hB:128 * hB + 128]
        wqkvT_v = np.ascontiguousarray(np.concatenate(blocks, axis=0).T)

        aeff_v = np.zeros((D, 128), np.float32)
        beff_v = np.zeros((4, 32, 1024), np.float32)
        for l in range(4):
            aeff_v[:, 32 * l:32 * l + 16] = A[l].T
            beff_v[l, 0:16] = B[l, cols, :].T

        btab_v = np.zeros((128, NV), np.float32)
        i128 = np.arange(128, dtype=np.float32)
        for (hl, off), colx in TAB.items():
            btab_v[:, colx] = slopes[heads[hl]] * (i128 + off) - CEXP

        b2d_v = np.zeros((128, HB2D * 512), np.float32)
        for hl in range(HB2D):
            w = WCH[hl]
            sl = np.float64(slopes[heads[hl]])
            ql = np.arange(512)
            anc = (ql // w) * w + (w - 1)
            b2d_v[:, 512 * hl:512 * hl + 512] = (
                SQ128 * (sl * (i128[:, None] - anc[None, :]) - CEXP)).astype(np.float32)
        ctab_v = np.zeros((128, NCT), np.float32)
        for (hl, dd), colx in CTI.items():
            ctab_v[:, colx] = np.float32(SQ128 * float(slopes[heads[hl]]) * 128.0 * dd)
        mtri4_v = np.zeros((128, 2048), np.float32)
        for j in range(4):
            blk = np.zeros((128, 512), np.float32)
            blk[:, :128 * j] = MASKNEG
            ii = np.arange(128)
            blk[:, 128 * j:128 * j + 128] = np.where(
                ii[:, None] > ii[None, :], np.float32(MASKNEG), np.float32(0.0))
            mtri4_v[:, 512 * j:512 * j + 512] = blk

        masks_v = np.empty((4, SBT, 1024), dtype=ml_dtypes.bfloat16)
        ub = u[beta][:, cols]
        for l in range(4):
            masks_v[l] = ((ub >= lo[l]) & (ub < hi[l])).astype(ml_dtypes.bfloat16)

        in_maps.append({
            "x_full": np.ascontiguousarray(x[beta]),
            "x_cols": np.ascontiguousarray(x[beta][:, cols]),
            "wqkvT": wqkvT_v,
            "bqkv_qk": bqkvqk_v,
            "bqkv_vb": np.ascontiguousarray(
                np.broadcast_to(vb_row[None, :], (128, 1024))).astype(ml_dtypes.bfloat16),
            "woutT": np.ascontiguousarray(woutT_perm[:, cols]),
            "w0T": np.ascontiguousarray(w0T_full[:, cols]),
            "b0bc": np.ascontiguousarray(np.broadcast_to(b0[cols][None, :], (128, 1024))),
            "boutbc": np.ascontiguousarray(np.broadcast_to(b_out[cols][None, :], (128, 1024))),
            "aeff": aeff_v,
            "beff": beff_v,
            "masks": masks_v,
            "btab": btab_v,
            "b2d": b2d_v,
            "ctab": ctab_v,
            "mtri4": mtri4_v,
            "mtri": mtri_v,
            "ident": ident_v,
        })

    res = run_bass_kernel_spmd(nc, in_maps, core_ids=list(range(8)))
    out = np.empty((2, SBT, D), np.float32)
    for c in range(8):
        r, beta = c % 4, c // 4
        out[beta][:, 1024 * r:1024 * r + 1024] = res.results[c]["out_shard"]
    return out


# version 08
# speedup vs baseline: 1.0618x; 1.0618x over previous
import math
from contextlib import ExitStack

import numpy as np
import ml_dtypes

import concourse.bass as bass
import concourse.mybir as mybir
from concourse.bass_utils import run_bass_kernel_spmd

F32 = mybir.dt.float32
BF16 = mybir.dt.bfloat16
AF = mybir.ActivationFunctionType
ALU = mybir.AluOpType
AX = mybir.AxisListType.X

D = 4096
SBT = 4096
TOK = 1024
NDM = 32
NH = 8
NSW = 4
NTC = 16
TCW = 256
NG = 8
NG2 = 16
CEXP = 10.0
SCL = 1.0 / math.sqrt(128.0)
WCH = [64, 128, 256, 512, 512, 512, 512, 512]
MASKNEG = -1.0e9


def _nkt(g):
    return 4 * g + 4


BLOCKS_PER_HEAD = sum(_nkt(g) for g in range(NG))
BLK_BEFORE_G = [sum(_nkt(gg) for gg in range(g)) for g in range(NG)]


def _blk_idx(h, g, kt):
    return h * BLOCKS_PER_HEAD + BLK_BEFORE_G[g] + kt


def _qk_before_g(h, g):
    return h * (BLOCKS_PER_HEAD + NG) + BLK_BEFORE_G[g] + g


def _tab_meta():
    idx = {}
    for hl in range(NH):
        w = WCH[hl]
        for g in range(NG):
            for kt in range(_nkt(g)):
                for qc in range(512 // w):
                    q_hi = 512 * g + qc * w + w - 1
                    if q_hi < 128 * kt:
                        continue
                    key = (hl, 128 * kt - q_hi)
                    if key not in idx:
                        idx[key] = len(idx)
    return idx


TAB = _tab_meta()
NV = len(TAB)


def _exp_pieces(hl, g, kt):
    w = WCH[hl]
    j = kt - 4 * g
    diag = j >= 0
    pieces = []
    for qc in range(512 // w):
        ql, qh = qc * w, qc * w + w
        q_hi_g = 512 * g + qh - 1
        if q_hi_g < 128 * kt:
            continue
        col = TAB[(hl, 128 * kt - q_hi_g)]
        if diag:
            w_lo, w_hi = 128 * j, 128 * j + 128
            a, b = max(ql, w_lo), min(qh, w_hi)
            if a < b:
                pieces.append((a, b, "sm", col))
            if qh > w_hi:
                pieces.append((max(ql, w_hi), qh, "ps", col))
        else:
            pieces.append((ql, qh, "ps", col))
    return pieces


SQ128 = math.sqrt(128.0)


def _slopes_f():
    m = np.arange(1, 33, dtype=np.float32) * 0.25
    return (1.0 / np.power(np.float32(2.0), m)).astype(np.float32)


def _live(hl, g, kt):
    sl = float(_slopes_f()[4 * hl + 3])
    w = WCH[hl]
    bias_max = sl * (128 * kt + 127 - (512 * g + w - 1))
    return kt >= 4 * g or bias_max > -80.0


LIVE = {(hl, g): [kt for kt in range(_nkt(g)) if _live(hl, g, kt)]
        for hl in range(NH) for g in range(NG)}
HB2D = 3

EXPC = {}
SMC = {}
QKC = {}
MSKC = {}
AV_AFTER_HEAD = [0] * (NH + 1)
_exp = _sm = _msk = 0
_qk = 0
for _h in range(NH):
    for _g in range(NG):
        QKC[(_h, _g)] = _qk
        lv = LIVE[(_h, _g)]
        for _kt in lv:
            _exp += 1
            EXPC[(_h, _g, _kt)] = _exp
            if _h < HB2D:
                _sm += 1
                SMC[(_h, _g, _kt)] = _sm
            elif _kt >= 4 * _g:
                _msk += 1
                MSKC[(_h, _g, _kt)] = _msk
        _qk += len(lv) + 1
    AV_AFTER_HEAD[_h + 1] = 2 * _exp
T_EXP_ALL = _exp

FIS = [(0, 0), (0, 1), (1, 0), (1, 1)]
YIS = [(oc2, tt2, l) for oc2 in range(2) for tt2 in range(2) for l in range(4)]


def _cm_h0(g2, k):
    return 80 * g2 + 8 * (k + 1)


def _cm_h1(g2, k):
    return 80 * g2 + 32 + 8 * (k + 1)


def _cm_y(g2, yi):
    return 80 * g2 + 64 + yi + 1


def _ca_F(g2, fo):
    return 20 * g2 + fo + 1


def _ca_y(g2, yi):
    return 20 * g2 + 4 + yi + 1


def _cd_base(g2):
    return 1 + 40 * g2


def _cd_bias(g2, fo):
    return _cd_base(g2) + 2 * fo + 1


def _cd_xc(g2, fo):
    return _cd_base(g2) + 2 * fo + 2


def _cd_mul(g2, yi):
    return _cd_base(g2) + 8 + 2 * yi + 1


def _cd_add(g2, yi):
    return _cd_base(g2) + 8 + 2 * yi + 2


def _cd_final(g2, fo):
    tt2, oc2 = FIS[fo]
    return _cd_add(g2, 8 * oc2 + 4 * tt2 + 3)


CTI = {}
for _h in range(HB2D):
    for _g in range(NG):
        for _kt in LIVE[(_h, _g)]:
            _d = _kt - 4 * _g
            if (_h, _d) not in CTI:
                CTI[(_h, _d)] = len(CTI)
NCT = len(CTI)

LDC = {"mtri": 16, "btab": 32, "ident": 48, "bqkvqk": 64, "vbias": 80,
       "b2d": 96, "mtri4": 112, "ctab": 128, "bias_bc": 144, "boutbc": 176}



def build_nc():
    nc = bass.Bass("TRN2", target_bir_lowering=False)

    ap = lambda t: t.ap()
    x_full = ap(nc.dram_tensor("x_full", [SBT, D], F32, kind="ExternalInput"))
    x_cols = ap(nc.dram_tensor("x_cols", [SBT, TOK], F32, kind="ExternalInput"))
    wqkvT = ap(nc.dram_tensor("wqkvT", [D, 3072], F32, kind="ExternalInput"))
    bqkv_qk = ap(nc.dram_tensor("bqkv_qk", [128, 16], F32, kind="ExternalInput"))
    bqkv_vb = ap(nc.dram_tensor("bqkv_vb", [128, 1024], BF16, kind="ExternalInput"))
    woutT = ap(nc.dram_tensor("woutT", [D, 1024], F32, kind="ExternalInput"))
    w0T = ap(nc.dram_tensor("w0T", [D, 1024], F32, kind="ExternalInput"))
    b0bc = ap(nc.dram_tensor("b0bc", [128, 1024], BF16, kind="ExternalInput"))
    boutbc = ap(nc.dram_tensor("boutbc", [128, 1024], F32, kind="ExternalInput"))
    aeff = ap(nc.dram_tensor("aeff", [D, 64], F32, kind="ExternalInput"))
    beff = ap(nc.dram_tensor("beff", [4, 16, 1024], F32, kind="ExternalInput"))
    masks = ap(nc.dram_tensor("masks", [4, SBT, 1024], BF16, kind="ExternalInput"))
    btab = ap(nc.dram_tensor("btab", [128, NV], F32, kind="ExternalInput"))
    mtri = ap(nc.dram_tensor("mtri", [128, 128], F32, kind="ExternalInput"))
    b2d = ap(nc.dram_tensor("b2d", [128, HB2D * 512], F32, kind="ExternalInput"))
    mtri4 = ap(nc.dram_tensor("mtri4", [128, 2048], F32, kind="ExternalInput"))
    ctab = ap(nc.dram_tensor("ctab", [128, NCT], F32, kind="ExternalInput"))
    ident = ap(nc.dram_tensor("ident", [128, 128], BF16, kind="ExternalInput"))
    out_sh = ap(nc.dram_tensor("out_shard", [SBT, TOK], F32, kind="ExternalOutput"))

    xn_sp = ap(nc.dram_tensor("xn_sp", [D, SBT], BF16, kind="Internal"))
    cc2_in = nc.dram_tensor("cc2_in", [TOK, SBT], BF16, kind="Internal")
    cc2_out = nc.dram_tensor("cc2_out", [D, SBT], BF16, kind="Internal")
    qk_sp = ap(nc.dram_tensor("qk_sp", [2048, SBT], BF16, kind="Internal"))
    v_sp = ap(nc.dram_tensor("v_sp", [SBT, 1024], BF16, kind="Internal"))
    masks_r = masks.rearrange("l s c -> (l s) c")
    beff_r = beff.rearrange("l r c -> (l r) c")

    RG = [[0, 1, 2, 3], [4, 5, 6, 7]]
    SP_ALL = 16 * 2 * NSW * NTC

    es = ExitStack()
    sem = {}
    for name in ["ld_x0", "ld_x1", "ld_c", "ld_w0", "ld_w1", "ld_xn0", "ld_xn1",
                 "ld_kv0", "ld_kv1", "ld_qt0", "ld_qt1", "ld_cw",
                 "ld_ao0", "ld_ao1", "ld_ao2", "ld_ao3",
                 "ld_xnm0", "ld_xnm1", "ld_xnm2", "ld_xnm3",
                 "ld_mk0", "ld_mk1", "ld_mk2", "ld_mk3",
                 "ld_mk4", "ld_mk5", "ld_mk6", "ld_mk7",
                 "st_out0", "st_out1", "st_out2", "st_out3", "ld_xc", "ld_hl",
                 "s_lnA", "s_lnV", "s_tp", "s_ev", "st_xn",
                 "p_mm", "e_qk", "e_v", "st_sp",
                 "t_qk", "t_exp", "t_msk", "t_av", "t_dve", "t_sm", "t_ss", "st_cc2", "cc2s", "c_act", "c_hev", "c_fo",
                 "c_mm", "c_dve", "st_out"]:
        sem[name] = es.enter_context(nc.semaphore(name))

    mtri_sb = es.enter_context(nc.sbuf_tensor("mtri_sb", [128, 128], F32))
    btab_sb = es.enter_context(nc.sbuf_tensor("btab_sb", [128, NV], F32))
    ident_sb = es.enter_context(nc.sbuf_tensor("ident_sb", [128, 128], BF16))
    bqkvqk_sb = es.enter_context(nc.sbuf_tensor("bqkvqk_sb", [128, 16], F32))

    with es, nc.Block() as block:
        with ExitStack() as p01:
            xa = p01.enter_context(nc.sbuf_tensor("xa", [128, 2 * D], F32))
            xsq = p01.enter_context(nc.sbuf_tensor("xsq", [128, 2048], F32))
            xnbf = p01.enter_context(nc.sbuf_tensor("xnbf", [128, 3 * 2048], BF16))
            xnT = p01.enter_context(nc.sbuf_tensor("xnT", [128, NDM * 256], BF16))
            lnv_ = p01.enter_context(nc.sbuf_tensor("lnv", [128, 12], F32))
            pt0 = p01.enter_context(nc.psum_tensor("pt0", [128, 512], BF16))
            pt1 = p01.enter_context(nc.psum_tensor("pt1", [128, 512], BF16))
            ptb = [pt0, pt1]
            wsw = p01.enter_context(nc.sbuf_tensor("wsw", [128, 2 * NDM * 768], BF16))
            xnsb = p01.enter_context(nc.sbuf_tensor("xnsb", [128, 2 * NDM * TCW], BF16))
            vbias = p01.enter_context(nc.sbuf_tensor("vbias", [128, 1024], BF16))
            qkev = p01.enter_context(nc.sbuf_tensor("qkev", [128, 4 * TCW], BF16))
            vev = p01.enter_context(nc.sbuf_tensor("vev", [128, 2 * TCW], BF16))
            qkp = [p01.enter_context(nc.psum_tensor(f"qkp{i}", [128, TCW], F32)) for i in range(2)]
            vp = [p01.enter_context(nc.psum_tensor(f"vp{i}", [128, TCW], F32)) for i in range(2)]
            COL = {k: i for i, k in enumerate(
                ["rs", "rssa", "rssb", "rss", "mean", "ms", "var", "lnvv", "rstd", "nb", "eps"])}

            def lv(k):
                return lnv_[:, COL[k]:COL[k] + 1]

            def wb_xn(eng, j):
                eng.dma_start(
                    out=xn_sp[:, 256 * j:256 * j + 256].rearrange("(dm p) c -> p dm c", p=128),
                    in_=xnT.ap().rearrange("p (dm c) -> p dm c", c=256)).then_inc(sem["st_xn"], 16)

            @block.sync
            def _(s):
                for j in range(16):
                    for t2 in range(2):
                        i = 2 * j + t2
                        if i >= 2:
                            s.wait_ge(sem["s_lnA"], 6 * i - 6)
                        s.dma_start(out=xa[:, (i % 2) * D:(i % 2) * D + D],
                                    in_=x_full[128 * i:128 * i + 128, :]).then_inc(sem[f"ld_x{i % 2}"], 16)
                    if j >= 1:
                        s.wait_ge(sem["s_ev"], 16 * j)
                        wb_xn(s, j - 1)
                s.wait_ge(sem["s_ev"], 16 * 16)
                wb_xn(s, 15)
                s.dma_start(out=vbias[:, :], in_=bqkv_vb).then_inc(sem["ld_c"], 16)
                s.wait_ge(sem["st_xn"], 16 * 16)
                for gi in range(NSW * NTC):
                    sw, tc = divmod(gi, NTC)
                    if gi >= 2:
                        s.wait_ge(sem["p_mm"], 6 * (gi - 1))
                    for q4 in range(4):
                        src = xn_sp[1024 * q4:1024 * q4 + 1024, 256 * tc:256 * tc + 256] \
                            .rearrange("(dd p) c -> p dd c", p=128)
                        dst = xnsb.ap().rearrange("p (dm c) -> p dm c", c=TCW)[
                            :, (gi % 2) * NDM + 8 * q4:(gi % 2) * NDM + 8 * q4 + 8, :]
                        s.dma_start(out=dst, in_=src).then_inc(sem[f"ld_xn{gi % 2}"], 16)

            @block.scalar
            def _(sc):
                sc.dma_start(out=mtri_sb[:, :], in_=mtri).then_inc(sem["ld_c"], 16)
                sc.dma_start(out=btab_sb[:, :], in_=btab).then_inc(sem["ld_c"], 16)
                sc.dma_start(out=ident_sb[:, :], in_=ident).then_inc(sem["ld_c"], 16)
                sc.dma_start(out=bqkvqk_sb[:, :], in_=bqkv_qk).then_inc(sem["ld_c"], 16)
                for i in range(32):
                    xi = xa[:, (i % 2) * D:(i % 2) * D + D]
                    sa = ((2 * i) % 3) * 2048
                    sb = ((2 * i + 1) % 3) * 2048
                    sc.wait_ge(sem[f"ld_x{i % 2}"], 16 * (i // 2 + 1))
                    if i >= 1:
                        sc.wait_ge(sem["s_lnV"], 8 * i - 5)
                    sc.activation(xsq[:, :], xi[:, 0:2048], AF.Square).then_inc(sem["s_lnA"], 1)
                    sc.wait_ge(sem["s_lnV"], 8 * i + 2)
                    sc.activation(xsq[:, :], xi[:, 2048:4096], AF.Square).then_inc(sem["s_lnA"], 1)
                    sc.wait_ge(sem["s_lnV"], 8 * i + 7)
                    sc.activation(lv("lnvv"), lv("var"), AF.Ln, bias=lv("eps")).then_inc(sem["s_lnA"], 1)
                    sc.wait_ge(sem["s_lnA"], 6 * i + 3)
                    sc.activation(lv("rstd"), lv("lnvv"), AF.Exp, scale=-0.5).then_inc(sem["s_lnA"], 1)
                    sc.wait_ge(sem["s_lnV"], 8 * i + 8)
                    if i >= 2:
                        sc.wait_ge(sem["s_tp"], 8 * i - 8)
                    sc.activation(xnbf[:, sa:sa + 2048], xi[:, 0:2048], AF.Identity,
                                  bias=lv("nb"), scale=lv("rstd")).then_inc(sem["s_lnA"], 1)
                    if i >= 1:
                        sc.wait_ge(sem["s_tp"], 8 * i - 4)
                    sc.activation(xnbf[:, sb:sb + 2048], xi[:, 2048:4096], AF.Identity,
                                  bias=lv("nb"), scale=lv("rstd")).then_inc(sem["s_lnA"], 1)

            @block.vector
            def _(v):
                def evac_tile(v, i):
                    if i >= 2 and i % 2 == 0:
                        v.wait_ge(sem["st_xn"], 16 * (i // 2))
                    for b in range(8):
                        gi = 8 * i + b
                        v.wait_ge(sem["s_tp"], gi + 1)
                        dst = xnT.ap().rearrange("p (dm c) -> p dm c", c=256)[
                            :, 4 * b:4 * b + 4, 128 * (i % 2):128 * (i % 2) + 128]
                        src = ptb[gi % 2].ap().rearrange("p (j c) -> p j c", c=128)
                        v.tensor_copy(dst, src).then_inc(sem["s_ev"], 1)

                v.memset(lv("eps"), 1e-5)
                for i in range(32):
                    xi = xa[:, (i % 2) * D:(i % 2) * D + D]
                    v.wait_ge(sem[f"ld_x{i % 2}"], 16 * (i // 2 + 1))
                    v.reduce_sum(lv("rs"), xi, axis=AX).then_inc(sem["s_lnV"], 1)
                    v.wait_ge(sem["s_lnA"], 6 * i + 1)
                    v.reduce_sum(lv("rssa"), xsq[:, :], axis=AX).then_inc(sem["s_lnV"], 1)
                    v.wait_ge(sem["s_lnA"], 6 * i + 2)
                    v.reduce_sum(lv("rssb"), xsq[:, :], axis=AX).then_inc(sem["s_lnV"], 1)
                    v.wait_ge(sem["s_lnV"], 8 * i + 3)
                    v.tensor_add(lv("rss"), lv("rssa"), lv("rssb")).then_inc(sem["s_lnV"], 1)
                    v.wait_ge(sem["s_lnV"], 8 * i + 1)
                    v.tensor_scalar_mul(lv("mean"), lv("rs"), 1.0 / D).then_inc(sem["s_lnV"], 1)
                    v.wait_ge(sem["s_lnV"], 8 * i + 5)
                    v.tensor_mul(lv("ms"), lv("mean"), lv("mean")).then_inc(sem["s_lnV"], 1)
                    v.wait_ge(sem["s_lnV"], 8 * i + 6)
                    if i >= 1:
                        v.wait_ge(sem["s_lnA"], 6 * i - 3)
                    v.scalar_tensor_tensor(lv("var"), lv("rss"), 1.0 / D, lv("ms"),
                                           ALU.mult, ALU.subtract).then_inc(sem["s_lnV"], 1)
                    v.wait_ge(sem["s_lnA"], 6 * i + 4)
                    v.scalar_tensor_tensor(lv("nb"), lv("mean"), -1.0, lv("rstd"),
                                           ALU.mult, ALU.mult).then_inc(sem["s_lnV"], 1)
                    if i >= 1:
                        evac_tile(v, i - 1)
                evac_tile(v, 31)

            @block.tensor
            def _(t):
                t.wait_ge(sem["ld_c"], 16 * 4)
                for i in range(32):
                    sa = ((2 * i) % 3) * 2048
                    sb = ((2 * i + 1) % 3) * 2048
                    for b in range(8):
                        gi = 8 * i + b
                        t.wait_ge(sem["s_lnA"], 6 * i + (5 if b < 4 else 6))
                        if gi >= 2:
                            t.wait_ge(sem["s_ev"], gi - 1)
                        last = None
                        for j in range(4):
                            dmc = 4 * b + j
                            off = (sa + 128 * dmc) if b < 4 else (sb + 128 * (dmc - 16))
                            last = t.transpose(
                                ptb[gi % 2][:, 128 * j:128 * j + 128],
                                xnbf[:, off:off + 128],
                                ident_sb[:, :])
                        last.then_inc(sem["s_tp"], 1)


            @block.gpsimd
            def _(g):
                def load_sweep(s):
                    if s >= 2:
                        g.wait_ge(sem["p_mm"], 6 * NTC * (s - 1))
                    for q4 in range(4):
                        src = wqkvT[128 * 8 * q4:128 * 8 * (q4 + 1), 768 * s:768 * s + 768] \
                            .rearrange("(dd p) c -> p dd c", p=128)
                        dst = wsw.ap().rearrange("p (dm c) -> p dm c", c=768)[
                            :, (s % 2) * NDM + 8 * q4:(s % 2) * NDM + 8 * q4 + 8, :]
                        g.dma_start(out=dst, in_=src).then_inc(sem[f"ld_w{s % 2}"], 16)

                load_sweep(0)
                load_sweep(1)
                for gi in range(NSW * NTC):
                    if gi % NTC == 0 and 2 <= gi // NTC + 1 < NSW:
                        load_sweep(gi // NTC + 1)
                    s, tc = divmod(gi, NTC)
                    g.wait_ge(sem["e_qk"], 4 * (gi + 1))
                    dstq = qk_sp[512 * s:512 * s + 512, TCW * tc:TCW * tc + TCW] \
                        .rearrange("(oc p) c -> p oc c", p=128)
                    g.dma_start(out=dstq, in_=qkev.ap().rearrange("p (oc c) -> p oc c", c=TCW)) \
                        .then_inc(sem["st_sp"], 16)
                    g.wait_ge(sem["e_v"], 2 * (gi + 1))
                    dstv = v_sp[TCW * tc:TCW * tc + TCW, 256 * s:256 * s + 256] \
                        .rearrange("(tt p) c -> p tt c", p=128)
                    g.dma_start(out=dstv, in_=vev.ap().rearrange("p (tt c) -> p tt c", c=TCW)) \
                        .then_inc(sem["st_sp"], 16)

            @block.tensor
            def _(t):
                qk_guard = [0, 0]
                v_guard = [0, 0]
                for gi in range(NSW * NTC):
                    s, tc = divmod(gi, NTC)
                    if tc == 0:
                        t.wait_ge(sem[f"ld_w{s % 2}"], 16 * 4 * (s // 2 + 1))
                    t.wait_ge(sem[f"ld_xn{gi % 2}"], 16 * 4 * (gi // 2 + 1))
                    wof = (s % 2) * NDM * 768
                    xof = (gi % 2) * NDM * TCW
                    for oc in range(4):
                        if qk_guard[oc % 2]:
                            t.wait_ge(sem["e_qk"], qk_guard[oc % 2])
                        last = None
                        for dm in range(NDM):
                            last = t.matmul(qkp[oc % 2][:, :],
                                            wsw[:, wof + 768 * dm + 128 * oc:wof + 768 * dm + 128 * oc + 128],
                                            xnsb[:, xof + TCW * dm:xof + TCW * dm + TCW],
                                            start=(dm == 0), stop=(dm == NDM - 1))
                        last.then_inc(sem["p_mm"], 1)
                        qk_guard[oc % 2] = 4 * gi + oc + 1
                    for tt in range(2):
                        if v_guard[tt % 2]:
                            t.wait_ge(sem["e_v"], v_guard[tt % 2])
                        last = None
                        for dm in range(NDM):
                            last = t.matmul(vp[tt % 2][:, :],
                                            xnsb[:, xof + TCW * dm + 128 * tt:xof + TCW * dm + 128 * tt + 128],
                                            wsw[:, wof + 768 * dm + 512:wof + 768 * dm + 768],
                                            start=(dm == 0), stop=(dm == NDM - 1))
                        last.then_inc(sem["p_mm"], 1)
                        v_guard[tt % 2] = 2 * gi + tt + 1

            @block.scalar
            def _(sc):
                sc.wait_ge(sem["ld_c"], 16 * 4)
                for gi in range(NSW * NTC):
                    s, tc = divmod(gi, NTC)
                    for oc in range(4):
                        sc.wait_ge(sem["p_mm"], 6 * gi + oc + 1)
                        if gi >= 1:
                            sc.wait_ge(sem["st_sp"], 16 * 2 * gi)
                        sc.activation(qkev[:, TCW * oc:TCW * oc + TCW], qkp[oc % 2][:, :],
                                      AF.Identity, bias=bqkvqk_sb[:, 4 * s + oc:4 * s + oc + 1]) \
                            .then_inc(sem["e_qk"], 1)

            @block.vector
            def _(v):
                v.wait_ge(sem["ld_c"], 16 * 5)
                for gi in range(NSW * NTC):
                    s, tc = divmod(gi, NTC)
                    for tt in range(2):
                        v.wait_ge(sem["p_mm"], 6 * gi + 4 + tt + 1)
                        if gi >= 1:
                            v.wait_ge(sem["st_sp"], 16 * 2 * gi)
                        v.tensor_add(vev[:, TCW * tt:TCW * tt + TCW], vp[tt % 2][:, :],
                                     vbias[:, 256 * s:256 * s + 256]).then_inc(sem["e_v"], 1)

        with ExitStack() as p2:
            ktsb = p2.enter_context(nc.sbuf_tensor("ktsb", [128, 2 * SBT], BF16))
            vsb = p2.enter_context(nc.sbuf_tensor("vsb", [128, 2 * SBT], BF16))
            qtsb = p2.enter_context(nc.sbuf_tensor("qtsb", [128, 2 * 512], BF16))
            etc_l = [p2.enter_context(nc.sbuf_tensor(f"etc{j}", [128, 512], BF16)) for j in range(2)]
            etd_l = [p2.enter_context(nc.sbuf_tensor(f"etd{j}", [128, 512], BF16)) for j in range(4)]
            etn_l = [p2.enter_context(nc.sbuf_tensor(f"etn{j}", [128, 512], BF16)) for j in range(2)]
            smb = [p2.enter_context(nc.sbuf_tensor(f"sm{j}", [128, 128], F32)) for j in range(2)]
            smn = [p2.enter_context(nc.sbuf_tensor(f"smn{j}", [128, 512], F32)) for j in range(2)]
            b2d_sb = p2.enter_context(nc.sbuf_tensor("b2d_sb", [128, HB2D * 512], F32))
            ctab_sb = p2.enter_context(nc.sbuf_tensor("ctab_sb", [128, NCT], F32))
            mtri4_sb = p2.enter_context(nc.sbuf_tensor("mtri4_sb", [128, 2048], F32))
            ones_c = p2.enter_context(nc.sbuf_tensor("ones_c", [128, 1], BF16))
            ones_r = p2.enter_context(nc.sbuf_tensor("ones_r", [1, 128], BF16))
            rinv_f = p2.enter_context(nc.sbuf_tensor("rinv_f", [1, 512], F32))
            rinv_b = p2.enter_context(nc.sbuf_tensor("rinv_b", [1, 512], BF16))
            rbsb = p2.enter_context(nc.sbuf_tensor("rbsb", [128, 512], F32))
            osb = p2.enter_context(nc.sbuf_tensor("osb", [128, 2 * 512], BF16))
            stp_ = [p2.enter_context(nc.psum_tensor(f"stp{i}", [128, 512], F32)) for i in range(2)]
            otp = p2.enter_context(nc.psum_tensor("otp", [128, 512], F32))
            lp = p2.enter_context(nc.psum_tensor("lp", [1, 512], F32))
            rbp = p2.enter_context(nc.psum_tensor("rbp", [128, 512], F32))

            def etbuf(h, g, kt):
                j = kt - 4 * g
                if h < HB2D:
                    return etn_l[kt % 2]
                return etd_l[j] if j >= 0 else etc_l[kt % 2]

            @block.sync
            def _(s_):
                s_.wait_ge(sem["st_sp"], SP_ALL)
                s_.dma_start(out=b2d_sb[:, :], in_=b2d).then_inc(sem["ld_c"], 16)
                s_.dma_start(out=mtri4_sb[:, :], in_=mtri4).then_inc(sem["ld_c"], 16)
                s_.dma_start(out=ctab_sb[:, :], in_=ctab).then_inc(sem["ld_c"], 16)
                for h in range(NH):
                    if h >= 2:
                        s_.wait_ge(sem["t_av"], AV_AFTER_HEAD[h - 1])
                    s_.dma_start(out=ktsb[:, (h % 2) * SBT:(h % 2) * SBT + SBT],
                                 in_=qk_sp[256 * h + 128:256 * h + 256, :]).then_inc(sem[f"ld_kv{h % 2}"], 16)
                    for q4 in range(4):
                        src = v_sp[1024 * q4:1024 * (q4 + 1), 128 * h:128 * h + 128] \
                            .rearrange("(kt p) c -> p kt c", p=128)
                        dst = vsb.ap().rearrange("p (kt c) -> p kt c", c=128)[
                            :, (h % 2) * NDM + 8 * q4:(h % 2) * NDM + 8 * q4 + 8, :]
                        s_.dma_start(out=dst, in_=src).then_inc(sem[f"ld_kv{h % 2}"], 16)
                    for g in range(NG):
                        hg = h * NG + g
                        if hg >= 2:
                            h2, g2_ = divmod(hg - 2, NG)
                            s_.wait_ge(sem["t_qk"], QKC[(h2, g2_)] + len(LIVE[(h2, g2_)]))
                        s_.dma_start(out=qtsb[:, (g % 2) * 512:(g % 2) * 512 + 512],
                                     in_=qk_sp[256 * h:256 * h + 128, 512 * g:512 * g + 512]) \
                            .then_inc(sem[f"ld_qt{g % 2}"], 16)

            @block.tensor
            def _(t):
                t.wait_ge(sem["st_sp"], SP_ALL)
                t.wait_ge(sem["t_dve"], 1)
                bank_guard = [None, None]
                for h in range(NH):
                    t.wait_ge(sem[f"ld_kv{h % 2}"], 16 * 5 * (h // 2 + 1))
                    kof = (h % 2) * SBT
                    vof = (h % 2) * SBT
                    for g in range(NG):
                        hg = h * NG + g
                        lv = LIVE[(h, g)]
                        qb = QKC[(h, g)]
                        t.wait_ge(sem[f"ld_qt{g % 2}"], 16 * (h * 4 + g // 2 + 1))
                        qt_ap = qtsb[:, (g % 2) * 512:(g % 2) * 512 + 512]

                        def st_mm(i, h=h, g=g, qt_ap=qt_ap, qb=qb, lv=lv):
                            kt = lv[i]
                            if bank_guard[kt % 2] is not None:
                                sname, v = bank_guard[kt % 2]
                                t.wait_ge(sem[sname], v)
                            t.matmul(stp_[kt % 2][:, :],
                                     ktsb[:, kof + 128 * kt:kof + 128 * kt + 128],
                                     qt_ap, start=True, stop=True).then_inc(sem["t_qk"], 1)
                            if h < HB2D:
                                bank_guard[kt % 2] = ("t_sm", SMC[(h, g, kt)])
                            else:
                                bank_guard[kt % 2] = ("t_exp", EXPC[(h, g, kt)])

                        st_mm(0)
                        if len(lv) > 1:
                            st_mm(1)
                        for i, kt in enumerate(lv):
                            t.wait_ge(sem["t_exp"], EXPC[(h, g, kt)])
                            if i == 0 and hg > 0:
                                t.wait_ge(sem["t_dve"], 4 * hg + 1)
                            eb = etbuf(h, g, kt)
                            t.matmul(otp[:, :], vsb[:, vof + 128 * kt:vof + 128 * kt + 128],
                                     eb[:, :], start=(i == 0), stop=(i == len(lv) - 1),
                                     skip_group_check=True).then_inc(sem["t_av"], 1)
                            t.matmul(lp[:, :], ones_c[:, :], eb[:, :],
                                     start=(i == 0), stop=(i == len(lv) - 1),
                                     skip_group_check=True).then_inc(sem["t_av"], 1)
                            if i + 2 < len(lv):
                                st_mm(i + 2)
                        t.wait_ge(sem["t_dve"], 4 * hg + 3)
                        t.matmul(rbp[:, :], ones_r[:, :], rinv_b[:, :],
                                 start=True, stop=True).then_inc(sem["t_qk"], 1)

            @block.vector
            def _(v):
                v.wait_ge(sem["st_sp"], SP_ALL)
                v.wait_ge(sem["ld_c"], 16 * 8)
                v.memset(ones_c[:, :], 1.0)
                v.memset(ones_r[:, :], 1.0)
                v.memset(etd_l[1][:, 0:128], 0.0)
                v.memset(etd_l[2][:, 0:256], 0.0)
                v.memset(etd_l[3][:, 0:384], 0.0).then_inc(sem["t_dve"], 1)
                dve_cum = 1
                ss_cum = 0
                sm_guard = [0, 0]
                smn_guard = [0, 0]
                for h in range(NH):
                    for g in range(NG):
                        hg = h * NG + g
                        lv = LIVE[(h, g)]
                        qb = QKC[(h, g)]
                        if h < HB2D:
                            for i, kt in enumerate(lv):
                                j = kt - 4 * g
                                v.wait_ge(sem["t_qk"], qb + i + 1)
                                if smn_guard[kt % 2]:
                                    v.wait_ge(sem["t_exp"], smn_guard[kt % 2])
                                ci = CTI[(h, kt - 4 * g)]
                                last = v.scalar_tensor_tensor(
                                    smn[kt % 2][:, :], stp_[kt % 2][:, :],
                                    ctab_sb[:, ci:ci + 1],
                                    b2d_sb[:, 512 * h:512 * h + 512], ALU.add, ALU.add)
                                if j >= 0:
                                    last.then_inc(sem["t_ss"], 1)
                                    ss_cum += 1
                                    v.wait_ge(sem["t_ss"], ss_cum)
                                    last = v.tensor_add(
                                        smn[kt % 2][:, 0:128 * (j + 1)],
                                        smn[kt % 2][:, 0:128 * (j + 1)],
                                        mtri4_sb[:, 512 * j:512 * j + 128 * (j + 1)])
                                last.then_inc(sem["t_sm"], 1)
                                smn_guard[kt % 2] = EXPC[(h, g, kt)]
                        else:
                            for kt in range(4 * g, 4 * g + 4):
                                j = kt - 4 * g
                                i = lv.index(kt)
                                v.wait_ge(sem["t_qk"], qb + i + 1)
                                if sm_guard[j % 2]:
                                    v.wait_ge(sem["t_exp"], sm_guard[j % 2])
                                v.tensor_add(smb[j % 2][:, :], stp_[kt % 2][:, 128 * j:128 * j + 128],
                                             mtri_sb[:, :]).then_inc(sem["t_msk"], 1)
                                sm_guard[j % 2] = EXPC[(h, g, kt)]
                        v.wait_ge(sem["t_av"], 2 * EXPC[(h, g, lv[-1])])
                        v.reciprocal(rinv_f[:, :], lp[:, :]).then_inc(sem["t_dve"], 1)
                        dve_cum += 1
                        v.wait_ge(sem["t_dve"], dve_cum)
                        v.tensor_copy(rinv_b[:, :], rinv_f[:, :]).then_inc(sem["t_dve"], 1)
                        dve_cum += 1
                        v.wait_ge(sem["t_qk"], qb + len(lv) + 1)
                        v.tensor_copy(rbsb[:, :], rbp[:, :]).then_inc(sem["t_dve"], 1)
                        dve_cum += 1
                        v.wait_ge(sem["t_dve"], dve_cum)
                        if hg >= 2:
                            v.wait_ge(sem["st_cc2"], 16 * hg)
                        v.tensor_mul(osb[:, (hg % 2) * 512:(hg % 2) * 512 + 512], otp[:, :],
                                     rbsb[:, :]).then_inc(sem["t_dve"], 1)
                        dve_cum += 1

            @block.scalar
            def _(sc):
                sc.wait_ge(sem["ld_c"], 16 * 4)
                sc.wait_ge(sem["t_dve"], 1)
                etc_guard = [0, 0]
                etd_guard = [0, 0, 0, 0]
                etn_guard = [0, 0]
                for h in range(NH):
                    for g in range(NG):
                        hg = h * NG + g
                        lv = LIVE[(h, g)]
                        qb = QKC[(h, g)]
                        for i, kt in enumerate(lv):
                            j = kt - 4 * g
                            if h < HB2D:
                                sc.wait_ge(sem["t_sm"], SMC[(h, g, kt)])
                                if etn_guard[kt % 2]:
                                    sc.wait_ge(sem["t_av"], etn_guard[kt % 2])
                                dest = etn_l[kt % 2]
                                etn_guard[kt % 2] = 2 * EXPC[(h, g, kt)]
                                sc.activation(dest[:, :], smn[kt % 2][:, :], AF.Exp,
                                              scale=SCL).then_inc(sem["t_exp"], 1)
                                continue
                            sc.wait_ge(sem["t_qk"], qb + i + 1)
                            if j >= 0:
                                sc.wait_ge(sem["t_msk"], MSKC[(h, g, kt)])
                                if etd_guard[j]:
                                    sc.wait_ge(sem["t_av"], etd_guard[j])
                                dest = etd_l[j]
                                etd_guard[j] = 2 * EXPC[(h, g, kt)]
                            else:
                                if etc_guard[kt % 2]:
                                    sc.wait_ge(sem["t_av"], etc_guard[kt % 2])
                                dest = etc_l[kt % 2]
                                etc_guard[kt % 2] = 2 * EXPC[(h, g, kt)]
                            last = None
                            for (a, b, srck, col) in _exp_pieces(h, g, kt):
                                if srck == "sm":
                                    srcp = smb[j % 2][:, a - 128 * j:b - 128 * j]
                                else:
                                    srcp = stp_[kt % 2][:, a:b]
                                last = sc.activation(dest[:, a:b], srcp, AF.Exp,
                                                     bias=btab_sb[:, col:col + 1], scale=SCL)
                            last.then_inc(sem["t_exp"], 1)

            @block.gpsimd
            def _(g_):
                for h in range(NH):
                    for g in range(NG):
                        hg = h * NG + g
                        g_.wait_ge(sem["t_dve"], 1 + 4 * (hg + 1))
                        g_.dma_start(out=cc2_in.ap()[128 * h:128 * h + 128, 512 * g:512 * g + 512],
                                     in_=osb[:, (hg % 2) * 512:(hg % 2) * 512 + 512]) \
                            .then_inc(sem["st_cc2"], 16)
                g_.wait_ge(sem["st_cc2"], 16 * NH * NG)
                g_.collective_compute("AllGather", ALU.bypass, replica_groups=RG,
                                      ins=[cc2_in.ap()], outs=[cc2_out.ap()]).then_inc(sem["cc2s"])
        with ExitStack() as p3:
            wout_sb = p3.enter_context(nc.sbuf_tensor("wout_sb", [128, NDM * 1024], BF16))
            w0_sb = p3.enter_context(nc.sbuf_tensor("w0_sb", [128, NDM * 1024], BF16))
            aeff_sb = p3.enter_context(nc.sbuf_tensor("aeff_sb", [128, NDM * 64], BF16))
            beff_sb = p3.enter_context(nc.sbuf_tensor("beff_sb", [16, 4 * 1024], BF16))
            bias_bc = p3.enter_context(nc.sbuf_tensor("bias_bc", [128, 1024], BF16))
            aorng = p3.enter_context(nc.sbuf_tensor("aorng", [128, 4 * 8 * TCW], BF16))
            xnrng = p3.enter_context(nc.sbuf_tensor("xnrng", [128, 4 * 8 * TCW], BF16))
            mkrng = p3.enter_context(nc.sbuf_tensor("mkrng", [128, 4 * 1024], BF16))
            xcs = p3.enter_context(nc.sbuf_tensor("xcs", [128, 2048], F32))
            hsb = p3.enter_context(nc.sbuf_tensor("hsb", [64, TCW], BF16))
            hlsb = p3.enter_context(nc.sbuf_tensor("hlsb", [16, 1024], BF16))
            ysb = p3.enter_context(nc.sbuf_tensor("ysb", [128, 4 * 512], BF16))
            fsb = [p3.enter_context(nc.sbuf_tensor(f"fsb{i}", [128, 512], F32)) for i in range(4)]
            tmp_sb = p3.enter_context(nc.sbuf_tensor("tmp_sb", [128, 512], BF16))
            hp = p3.enter_context(nc.psum_tensor("hp", [64, TCW], F32))
            Pp = [[p3.enter_context(nc.psum_tensor(f"Pp{t_}{o_}", [128, 512], F32))
                   for o_ in range(2)] for t_ in range(2)]
            yp = [p3.enter_context(nc.psum_tensor(f"yp{i}", [128, 512], F32)) for i in range(2)]

            @block.gpsimd
            def _(g_):
                g_.wait_ge(sem["t_av"], 2 * T_EXP_ALL)
                g_.wait_ge(sem["t_dve"], 1 + 4 * NH * NG)
                for (srcT, dstT, cw) in ((woutT, wout_sb, 1024), (w0T, w0_sb, 1024),
                                         (aeff, aeff_sb, 64)):
                    for q4 in range(4):
                        srcq = srcT[128 * 8 * q4:128 * 8 * (q4 + 1), :] \
                            .rearrange("(dd p) c -> p dd c", p=128)
                        dst = dstT.ap().rearrange("p (dm c) -> p dm c", c=cw)[
                            :, 8 * q4:8 * q4 + 8, :]
                        g_.dma_start(out=dst, in_=srcq).then_inc(sem["ld_cw"], 16)
                for l in range(4):
                    g_.dma_start(out=beff_sb[:, 1024 * l:1024 * l + 1024],
                                 in_=beff_r[16 * l:16 * l + 16, :]).then_inc(sem["ld_cw"], 16)
                mk_guard = [0, 0, 0, 0]
                for g2 in range(NG2):
                    g_.wait_ge(sem["c_hev"], g2 + 1)
                    if g2 >= 1:
                        g_.wait_ge(sem["c_mm"], 80 * g2)
                    for l in range(4):
                        g_.dma_start(out=hlsb[:, TCW * l:TCW * l + TCW],
                                     in_=hsb[16 * l:16 * l + 16, :]).then_inc(sem["ld_hl"], 16)
                    for oc2 in range(2):
                        for l in range(4):
                            if mk_guard[l]:
                                g_.wait_ge(sem["c_dve"], mk_guard[l])
                            srcm = masks_r[SBT * l + TCW * g2:SBT * l + TCW * g2 + TCW,
                                           512 * oc2:512 * oc2 + 512] \
                                .rearrange("(tt p) c -> p tt c", p=128)
                            dst = mkrng.ap()[:, 1024 * l:1024 * l + 1024] \
                                .rearrange("p (tt c) -> p tt c", c=512)
                            g_.dma_start(out=dst, in_=srcm).then_inc(sem[f"ld_mk{l}"], 16)
                            mk_guard[l] = _cd_mul(g2, 8 * oc2 + 4 + l)

            @block.sync
            def _(s_):
                s_.wait_ge(sem["cc2s"], 1)
                s_.dma_start(out=bias_bc[:, :], in_=b0bc).then_inc(sem["ld_c"], 16)
                s_.dma_start(out=fsb[0][:, :], in_=boutbc[:, 0:512]).then_inc(sem["ld_c"], 16)
                s_.dma_start(out=fsb[1][:, :], in_=boutbc[:, 512:1024]).then_inc(sem["ld_c"], 16)
                for k in range(4):
                    srcx = xn_sp[1024 * k:1024 * k + 1024, 0:TCW] \
                        .rearrange("(dd p) c -> p dd c", p=128)
                    dst = xnrng.ap().rearrange("p (dm c) -> p dm c", c=TCW)[
                        :, 8 * k:8 * k + 8, :]
                    s_.dma_start(out=dst, in_=srcx).then_inc(sem[f"ld_xnm{k}"], 16)
                for g2 in range(NG2):
                    if g2 >= 1:
                        for fo, (tt2, oc2) in enumerate(FIS):
                            s_.wait_ge(sem["c_dve"], _cd_final(g2 - 1, fo))
                            s_.dma_start(
                                out=out_sh[TCW * (g2 - 1) + 128 * tt2:TCW * (g2 - 1) + 128 * tt2 + 128,
                                           512 * oc2:512 * oc2 + 512],
                                in_=fsb[fo][:, :]).then_inc(sem[f"st_out{fo}"], 16)
                    if g2 < NG2 - 1:
                        for k in range(4):
                            u = 4 * (g2 + 1) + k
                            s_.wait_ge(sem["c_mm"], _cm_h1(g2, k))
                            srcx = xn_sp[1024 * k:1024 * k + 1024,
                                         TCW * (g2 + 1):TCW * (g2 + 1) + TCW] \
                                .rearrange("(dd p) c -> p dd c", p=128)
                            dst = xnrng.ap().rearrange("p (dm c) -> p dm c", c=TCW)[
                                :, 8 * k:8 * k + 8, :]
                            s_.dma_start(out=dst, in_=srcx).then_inc(sem[f"ld_xnm{k}"], 16)
                    if g2 >= 1:
                        s_.wait_ge(sem["c_dve"], _cd_xc(g2 - 1, 3))
                    srcx = x_cols[TCW * g2:TCW * g2 + TCW, :].rearrange("(tt p) c -> p tt c", p=128)
                    s_.dma_start(out=xcs.ap().rearrange("p (tt c) -> p tt c", c=1024),
                                 in_=srcx).then_inc(sem["ld_xc"], 16)
                g2 = NG2 - 1
                for fo, (tt2, oc2) in enumerate(FIS):
                    s_.wait_ge(sem["c_dve"], _cd_final(g2, fo))
                    s_.dma_start(out=out_sh[TCW * g2 + 128 * tt2:TCW * g2 + 128 * tt2 + 128,
                                            512 * oc2:512 * oc2 + 512],
                                 in_=fsb[fo][:, :]).then_inc(sem[f"st_out{fo}"], 16)
                for fo in range(4):
                    s_.wait_ge(sem[f"st_out{fo}"], 16 * NG2)

            @block.scalar
            def _(sc):
                sc.wait_ge(sem["cc2s"], 1)
                fs_guard = [0, 0, 0, 0]
                for k in range(4):
                    srca = cc2_out.ap()[128 * 8 * k:128 * 8 * (k + 1), 0:TCW] \
                        .rearrange("(dd p) c -> p dd c", p=128)
                    dst = aorng.ap().rearrange("p (dm c) -> p dm c", c=TCW)[
                        :, 8 * k:8 * k + 8, :]
                    sc.dma_start(out=dst, in_=srca).then_inc(sem[f"ld_ao{k}"], 16)
                for g2 in range(NG2):
                    sc.wait_ge(sem["c_mm"], 80 * g2 + 32)
                    if g2 >= 1:
                        sc.wait_ge(sem["ld_hl"], 16 * 4 * g2)
                    sc.copy(hsb[:, :], hp[:, :]).then_inc(sem["c_hev"], 1)
                    for fo in (0, 1):
                        tt2, oc2 = FIS[fo]
                        if g2 == 0:
                            sc.wait_ge(sem["c_dve"], 1)
                        if fs_guard[fo]:
                            sc.wait_ge(sem[f"st_out{fo}"], fs_guard[fo])
                        sc.copy(fsb[fo][:, :], Pp[tt2][oc2][:, :]).then_inc(sem["c_act"], 1)
                        fs_guard[fo] = 16 * (g2 + 1)
                    if g2 < NG2 - 1:
                        for k in range(4):
                            sc.wait_ge(sem["c_mm"], _cm_h1(g2, k))
                            srca = cc2_out.ap()[128 * 8 * k:128 * 8 * (k + 1),
                                                TCW * (g2 + 1):TCW * (g2 + 1) + TCW] \
                                .rearrange("(dd p) c -> p dd c", p=128)
                            dst = aorng.ap().rearrange("p (dm c) -> p dm c", c=TCW)[
                                :, 8 * k:8 * k + 8, :]
                            sc.dma_start(out=dst, in_=srca).then_inc(sem[f"ld_ao{k}"], 16)
                    else:
                        sc.wait_ge(sem["c_mm"], 80 * g2 + 64)
                    for fo in (2, 3):
                        tt2, oc2 = FIS[fo]
                        if fs_guard[fo]:
                            sc.wait_ge(sem[f"st_out{fo}"], fs_guard[fo])
                        sc.copy(fsb[fo][:, :], Pp[tt2][oc2][:, :]).then_inc(sem["c_act"], 1)
                        fs_guard[fo] = 16 * (g2 + 1)
                    for yi in range(16):
                        sc.wait_ge(sem["c_mm"], _cm_y(g2, yi))
                        if g2 > 0 or yi >= 4:
                            gy, yy = (g2, yi - 4) if yi >= 4 else (g2 - 1, 12 + yi)
                            sc.wait_ge(sem["c_dve"], _cd_mul(gy, yy))
                        sc.activation(ysb[:, 512 * (yi % 4):512 * (yi % 4) + 512], yp[yi % 2][:, :],
                                      AF.Identity, scale=8.0).then_inc(sem["c_act"], 1)

            @block.tensor
            def _(t):
                t.wait_ge(sem["cc2s"], 1)
                t.wait_ge(sem["ld_cw"], 16 * 16)
                for g2 in range(NG2):
                    for t2 in range(2):
                        for k in range(4):
                            t.wait_ge(sem[f"ld_xnm{k}"], 16 * (g2 + 1))
                            t.wait_ge(sem[f"ld_ao{k}"], 16 * (g2 + 1))
                            for dd in range(8):
                                dm = 8 * k + dd
                                if dm == 0 and g2 >= 1:
                                    if t2 == 0:
                                        t.wait_ge(sem["c_act"], _ca_F(g2 - 1, 1))
                                        t.wait_ge(sem["c_hev"], g2)
                                    else:
                                        t.wait_ge(sem["c_act"], _ca_F(g2 - 1, 3))
                                xof = k * 8 * TCW + TCW * dd
                                for oc2 in range(2):
                                    t.matmul(Pp[t2][oc2][:, :],
                                             aorng[:, xof + 128 * t2:xof + 128 * t2 + 128],
                                             wout_sb[:, 1024 * dm + 512 * oc2:1024 * dm + 512 * oc2 + 512],
                                             start=(dm == 0), stop=False,
                                             skip_group_check=True)
                                last = None
                                for oc2 in range(2):
                                    last = t.matmul(Pp[t2][oc2][:, :],
                                                    xnrng[:, xof + 128 * t2:xof + 128 * t2 + 128],
                                                    w0_sb[:, 1024 * dm + 512 * oc2:1024 * dm + 512 * oc2 + 512],
                                                    start=False, stop=(dm == NDM - 1),
                                                    skip_group_check=True)
                                if t2 == 0:
                                    last = t.matmul(hp[:, :], aeff_sb[:, 64 * dm:64 * dm + 64],
                                                    xnrng[:, xof:xof + TCW],
                                                    start=(dm == 0), stop=(dm == NDM - 1),
                                                    skip_group_check=True)
                                last.then_inc(sem["c_mm"], 1)
                    t.wait_ge(sem["ld_hl"], 16 * 4 * (g2 + 1))
                    for yi, (oc2, tt2, l) in enumerate(YIS):
                        if yi >= 2:
                            t.wait_ge(sem["c_act"], _ca_y(g2, yi - 2))
                        elif g2 >= 1:
                            t.wait_ge(sem["c_act"], _ca_y(g2 - 1, 14 + yi))
                        t.matmul(yp[yi % 2][:, :],
                                 hlsb[:, TCW * l + 128 * tt2:TCW * l + 128 * tt2 + 128],
                                 beff_sb[:, 1024 * l + 512 * oc2:1024 * l + 512 * oc2 + 512],
                                 start=True, stop=True).then_inc(sem["c_mm"], 1)

            @block.vector
            def _(v):
                v.wait_ge(sem["ld_c"], 16 * 11)
                v.tensor_add(bias_bc[:, 0:512], bias_bc[:, 0:512], fsb[0][:, :])
                v.tensor_add(bias_bc[:, 512:1024], bias_bc[:, 512:1024],
                             fsb[1][:, :]).then_inc(sem["c_dve"], 1)
                cd = 1
                for g2 in range(NG2):
                    for fo, (tt2, oc2) in enumerate(FIS):
                        v.wait_ge(sem["c_act"], _ca_F(g2, fo))
                        v.wait_ge(sem["c_dve"], cd)
                        v.tensor_add(fsb[fo][:, :], fsb[fo][:, :],
                                     bias_bc[:, 512 * oc2:512 * oc2 + 512]).then_inc(sem["c_dve"], 1)
                        cd += 1
                        v.wait_ge(sem["ld_xc"], 16 * (g2 + 1))
                        v.wait_ge(sem["c_dve"], cd)
                        v.tensor_add(fsb[fo][:, :], fsb[fo][:, :],
                                     xcs[:, 1024 * tt2 + 512 * oc2:1024 * tt2 + 512 * oc2 + 512]) \
                            .then_inc(sem["c_dve"], 1)
                        cd += 1
                    for yi, (oc2, tt2, l) in enumerate(YIS):
                        fo = 2 * tt2 + oc2
                        v.wait_ge(sem["c_act"], _ca_y(g2, yi))
                        v.wait_ge(sem[f"ld_mk{l}"], 16 * (2 * g2 + oc2 + 1))
                        v.wait_ge(sem["c_dve"], cd)
                        v.tensor_mul(tmp_sb[:, :], ysb[:, 512 * (yi % 4):512 * (yi % 4) + 512],
                                     mkrng[:, 1024 * l + 512 * tt2:1024 * l + 512 * tt2 + 512]) \
                            .then_inc(sem["c_dve"], 1)
                        cd += 1
                        v.wait_ge(sem["c_dve"], cd)
                        v.tensor_add(fsb[fo][:, :], fsb[fo][:, :], tmp_sb[:, :]) \
                            .then_inc(sem["c_dve"], 1)
                        cd += 1

    return nc



def _mask_u():
    def threefry_pair(k0, k1, x0, x1):
        ks = [np.uint32(k0), np.uint32(k1),
              np.uint32(np.uint32(k0) ^ np.uint32(k1) ^ np.uint32(0x1BD11BDA))]
        rot = [[13, 15, 26, 6], [17, 29, 16, 24]]
        x0 = x0 + ks[0]
        x1 = x1 + ks[1]
        for i in range(5):
            for r in rot[i % 2]:
                x0 = x0 + x1
                x1 = (x1 << np.uint32(r)) | (x1 >> np.uint32(32 - r))
                x1 = x1 ^ x0
            x0 = x0 + ks[(i + 1) % 3]
            x1 = x1 + ks[(i + 2) % 3] + np.uint32(i + 1)
        return x0, x1
    n = 2 * SBT * D
    idx = np.arange(n, dtype=np.uint64)
    hi = (idx >> np.uint64(32)).astype(np.uint32)
    lo = (idx & np.uint64(0xFFFFFFFF)).astype(np.uint32)
    o0, o1 = threefry_pair(0, 42, hi, lo)
    bits = o0 ^ o1
    f = ((bits >> np.uint32(9)) | np.uint32(0x3F800000)).view(np.float32) - np.float32(1.0)
    return f.reshape(2, SBT, D)


def kernel(L0_attn_x, L0_attn_w_qkv, L0_attn_b_qkv, L0_attn_w_out, L0_attn_b_out,
           L0_ffn_W0, L0_ffn_b0, L0_ffn_A, L0_ffn_B):
    x = np.asarray(L0_attn_x, dtype=np.float32)
    w_qkv = np.asarray(L0_attn_w_qkv, dtype=np.float32)
    b_qkv = np.asarray(L0_attn_b_qkv, dtype=np.float32)
    w_out = np.asarray(L0_attn_w_out, dtype=np.float32)
    b_out = np.asarray(L0_attn_b_out, dtype=np.float32)
    W0 = np.asarray(L0_ffn_W0, dtype=np.float32)
    b0 = np.asarray(L0_ffn_b0, dtype=np.float32)
    A = np.asarray(L0_ffn_A, dtype=np.float32)
    B = np.asarray(L0_ffn_B, dtype=np.float32)

    m = np.arange(1, 33, dtype=np.float32) * (8.0 / 32.0)
    slopes = (1.0 / np.power(np.float32(2.0), m)).astype(np.float32)
    u = _mask_u()
    keep = np.full(4, 0.25, np.float32)
    hi = np.cumsum(keep)
    lo = hi - keep

    d_perm = np.array([128 * (i + 4 * jj) + hd
                       for i in range(4) for jj in range(8) for hd in range(128)])
    woutT_perm = np.ascontiguousarray(w_out.T[d_perm])
    w0T_full = np.ascontiguousarray(W0.T)

    mtri_v = np.where(np.arange(128)[:, None] > np.arange(128)[None, :],
                      np.float32(MASKNEG), np.float32(0.0)).astype(np.float32)
    ident_v = np.eye(128, dtype=ml_dtypes.bfloat16)

    wq, wk, wv = w_qkv[0:D], w_qkv[D:2 * D], w_qkv[2 * D:3 * D]
    bq, bk, bv = b_qkv[0:D], b_qkv[D:2 * D], b_qkv[2 * D:3 * D]

    nc = build_nc()
    in_maps = []
    for c in range(8):
        r, beta = c % 4, c // 4
        heads = [r + 4 * j for j in range(NH)]
        cols = slice(1024 * r, 1024 * r + 1024)

        blocks = []
        vb_row = np.zeros(1024, np.float32)
        bqkvqk_v = np.zeros((128, 16), np.float32)
        for s in range(NSW):
            hA, hB = heads[2 * s], heads[2 * s + 1]
            blocks.append(np.concatenate([
                wq[128 * hA:128 * hA + 128], wk[128 * hA:128 * hA + 128],
                wq[128 * hB:128 * hB + 128], wk[128 * hB:128 * hB + 128],
                wv[128 * hA:128 * hA + 128], wv[128 * hB:128 * hB + 128]], axis=0))
            bqkvqk_v[:, 4 * s + 0] = bq[128 * hA:128 * hA + 128]
            bqkvqk_v[:, 4 * s + 1] = bk[128 * hA:128 * hA + 128]
            bqkvqk_v[:, 4 * s + 2] = bq[128 * hB:128 * hB + 128]
            bqkvqk_v[:, 4 * s + 3] = bk[128 * hB:128 * hB + 128]
            vb_row[256 * s:256 * s + 128] = bv[128 * hA:128 * hA + 128]
            vb_row[256 * s + 128:256 * s + 256] = bv[128 * hB:128 * hB + 128]
        wqkvT_v = np.ascontiguousarray(np.concatenate(blocks, axis=0).T)

        aeff_v = np.zeros((D, 64), np.float32)
        beff_v = np.zeros((4, 16, 1024), np.float32)
        for l in range(4):
            aeff_v[:, 16 * l:16 * l + 16] = A[l].T
            beff_v[l] = B[l, cols, :].T

        btab_v = np.zeros((128, NV), np.float32)
        i128 = np.arange(128, dtype=np.float32)
        for (hl, off), colx in TAB.items():
            btab_v[:, colx] = slopes[heads[hl]] * (i128 + off) - CEXP

        b2d_v = np.zeros((128, HB2D * 512), np.float32)
        for hl in range(HB2D):
            w = WCH[hl]
            sl = np.float64(slopes[heads[hl]])
            ql = np.arange(512)
            anc = (ql // w) * w + (w - 1)
            b2d_v[:, 512 * hl:512 * hl + 512] = (
                SQ128 * (sl * (i128[:, None] - anc[None, :]) - CEXP)).astype(np.float32)
        ctab_v = np.zeros((128, NCT), np.float32)
        for (hl, dd), colx in CTI.items():
            ctab_v[:, colx] = np.float32(SQ128 * float(slopes[heads[hl]]) * 128.0 * dd)
        mtri4_v = np.zeros((128, 2048), np.float32)
        for j in range(4):
            blk = np.zeros((128, 512), np.float32)
            blk[:, :128 * j] = MASKNEG
            ii = np.arange(128)
            blk[:, 128 * j:128 * j + 128] = np.where(
                ii[:, None] > ii[None, :], np.float32(MASKNEG), np.float32(0.0))
            mtri4_v[:, 512 * j:512 * j + 512] = blk

        masks_v = np.empty((4, SBT, 1024), dtype=ml_dtypes.bfloat16)
        ub = u[beta][:, cols]
        for l in range(4):
            masks_v[l] = ((ub >= lo[l]) & (ub < hi[l])).astype(ml_dtypes.bfloat16)

        in_maps.append({
            "x_full": np.ascontiguousarray(x[beta]),
            "x_cols": np.ascontiguousarray(x[beta][:, cols]),
            "wqkvT": wqkvT_v,
            "bqkv_qk": bqkvqk_v,
            "bqkv_vb": np.ascontiguousarray(
                np.broadcast_to(vb_row[None, :], (128, 1024))).astype(ml_dtypes.bfloat16),
            "woutT": np.ascontiguousarray(woutT_perm[:, cols]),
            "w0T": np.ascontiguousarray(w0T_full[:, cols]),
            "b0bc": np.ascontiguousarray(np.broadcast_to(b0[cols][None, :], (128, 1024))).astype(ml_dtypes.bfloat16),
            "boutbc": np.ascontiguousarray(np.broadcast_to(b_out[cols][None, :], (128, 1024))),
            "aeff": aeff_v,
            "beff": beff_v,
            "masks": masks_v,
            "btab": btab_v,
            "b2d": b2d_v,
            "ctab": ctab_v,
            "mtri4": mtri4_v,
            "mtri": mtri_v,
            "ident": ident_v,
        })

    res = run_bass_kernel_spmd(nc, in_maps, core_ids=list(range(8)))
    out = np.empty((2, SBT, D), np.float32)
    for c in range(8):
        r, beta = c % 4, c // 4
        out[beta][:, 1024 * r:1024 * r + 1024] = res.results[c]["out_shard"]
    return out


# version 09
# speedup vs baseline: 1.1338x; 1.0678x over previous
import math
from contextlib import ExitStack

import numpy as np
import ml_dtypes

import concourse.bass as bass
import concourse.mybir as mybir
from concourse.bass_utils import run_bass_kernel_spmd

F32 = mybir.dt.float32
BF16 = mybir.dt.bfloat16
AF = mybir.ActivationFunctionType
ALU = mybir.AluOpType
AX = mybir.AxisListType.X

D = 4096
SBT = 4096
TOK = 1024
NDM = 32
NH = 8
NSW = 4
NTC = 16
TCW = 256
NG = 8
NG2 = 16
CEXP = 10.0
SCL = 1.0 / math.sqrt(128.0)
WCH = [64, 128, 256, 512, 512, 512, 512, 512]
MASKNEG = -1.0e9


def _nkt(g):
    return 4 * g + 4


BLOCKS_PER_HEAD = sum(_nkt(g) for g in range(NG))
BLK_BEFORE_G = [sum(_nkt(gg) for gg in range(g)) for g in range(NG)]


def _blk_idx(h, g, kt):
    return h * BLOCKS_PER_HEAD + BLK_BEFORE_G[g] + kt


def _qk_before_g(h, g):
    return h * (BLOCKS_PER_HEAD + NG) + BLK_BEFORE_G[g] + g


def _tab_meta():
    idx = {}
    for hl in range(NH):
        w = WCH[hl]
        for g in range(NG):
            for kt in range(_nkt(g)):
                for qc in range(512 // w):
                    q_hi = 512 * g + qc * w + w - 1
                    if q_hi < 128 * kt:
                        continue
                    key = (hl, 128 * kt - q_hi)
                    if key not in idx:
                        idx[key] = len(idx)
    return idx


TAB = _tab_meta()
NV = len(TAB)


def _exp_pieces(hl, g, kt):
    w = WCH[hl]
    j = kt - 4 * g
    diag = j >= 0
    pieces = []
    for qc in range(512 // w):
        ql, qh = qc * w, qc * w + w
        q_hi_g = 512 * g + qh - 1
        if q_hi_g < 128 * kt:
            continue
        col = TAB[(hl, 128 * kt - q_hi_g)]
        if diag:
            w_lo, w_hi = 128 * j, 128 * j + 128
            a, b = max(ql, w_lo), min(qh, w_hi)
            if a < b:
                pieces.append((a, b, "sm", col))
            if qh > w_hi:
                pieces.append((max(ql, w_hi), qh, "ps", col))
        else:
            pieces.append((ql, qh, "ps", col))
    return pieces


SQ128 = math.sqrt(128.0)


def _slopes_f():
    m = np.arange(1, 33, dtype=np.float32) * 0.25
    return (1.0 / np.power(np.float32(2.0), m)).astype(np.float32)


def _live(hl, g, kt):
    sl = float(_slopes_f()[4 * hl + 3])
    w = WCH[hl]
    bias_max = sl * (128 * kt + 127 - (512 * g + w - 1))
    return kt >= 4 * g or bias_max > -80.0


LIVE = {(hl, g): [kt for kt in range(_nkt(g)) if _live(hl, g, kt)]
        for hl in range(NH) for g in range(NG)}
HB2D = 3

EXPC = {}
SMC = {}
QKC = {}
MSKC = {}
AV_AFTER_HEAD = [0] * (NH + 1)
_exp = _sm = _msk = 0
_qk = 0
for _h in range(NH):
    for _g in range(NG):
        QKC[(_h, _g)] = _qk
        lv = LIVE[(_h, _g)]
        for _kt in lv:
            _exp += 1
            EXPC[(_h, _g, _kt)] = _exp
            if _h < HB2D:
                _sm += 1
                SMC[(_h, _g, _kt)] = _sm
            elif _kt >= 4 * _g:
                _msk += 1
                MSKC[(_h, _g, _kt)] = _msk
        _qk += len(lv) + 1
    AV_AFTER_HEAD[_h + 1] = 2 * _exp
T_EXP_ALL = _exp

FIS = [(0, 0), (0, 1), (1, 0), (1, 1)]
YIS = [(oc2, tt2, l) for oc2 in range(2) for tt2 in range(2) for l in range(4)]


def _cm_h0(g2, k):
    return 80 * g2 + 8 * (k + 1)


def _cm_h1(g2, k):
    return 80 * g2 + 32 + 8 * (k + 1)


def _cm_y(g2, yi):
    return 80 * g2 + 64 + yi + 1


def _ca_F(g2, fo):
    return 20 * g2 + fo + 1


def _ca_y(g2, yi):
    return 20 * g2 + 4 + yi + 1


def _cd_base(g2):
    return 1 + 40 * g2


def _cd_bias(g2, fo):
    return _cd_base(g2) + 2 * fo + 1


def _cd_xc(g2, fo):
    return _cd_base(g2) + 2 * fo + 2


def _cd_mul(g2, yi):
    return _cd_base(g2) + 8 + 2 * yi + 1


def _cd_add(g2, yi):
    return _cd_base(g2) + 8 + 2 * yi + 2


def _cd_final(g2, fo):
    tt2, oc2 = FIS[fo]
    return _cd_add(g2, 8 * oc2 + 4 * tt2 + 3)


CTI = {}
for _h in range(HB2D):
    for _g in range(NG):
        for _kt in LIVE[(_h, _g)]:
            _d = _kt - 4 * _g
            if (_h, _d) not in CTI:
                CTI[(_h, _d)] = len(CTI)
NCT = len(CTI)

LDC = {"mtri": 16, "btab": 32, "ident": 48, "bqkvqk": 64, "vbias": 80,
       "b2d": 96, "mtri4": 112, "ctab": 128, "bias_bc": 144, "boutbc": 176}



def build_nc():
    nc = bass.Bass("TRN2", target_bir_lowering=False)

    ap = lambda t: t.ap()
    x_full = ap(nc.dram_tensor("x_full", [SBT, D], F32, kind="ExternalInput"))
    x_cols = ap(nc.dram_tensor("x_cols", [SBT, TOK], F32, kind="ExternalInput"))
    wqkvT = ap(nc.dram_tensor("wqkvT", [D, 3072], F32, kind="ExternalInput"))
    bqkv_qk = ap(nc.dram_tensor("bqkv_qk", [128, 16], F32, kind="ExternalInput"))
    bqkv_vb = ap(nc.dram_tensor("bqkv_vb", [128, 1024], BF16, kind="ExternalInput"))
    woutT = ap(nc.dram_tensor("woutT", [D, 1024], F32, kind="ExternalInput"))
    w0T = ap(nc.dram_tensor("w0T", [D, 1024], F32, kind="ExternalInput"))
    b0bc = ap(nc.dram_tensor("b0bc", [128, 1024], BF16, kind="ExternalInput"))
    boutbc = ap(nc.dram_tensor("boutbc", [128, 1024], F32, kind="ExternalInput"))
    aeff = ap(nc.dram_tensor("aeff", [D, 64], F32, kind="ExternalInput"))
    beff = ap(nc.dram_tensor("beff", [4, 16, 1024], F32, kind="ExternalInput"))
    masks = ap(nc.dram_tensor("masks", [4, SBT, 1024], BF16, kind="ExternalInput"))
    btab = ap(nc.dram_tensor("btab", [128, NV], F32, kind="ExternalInput"))
    mtri = ap(nc.dram_tensor("mtri", [128, 128], F32, kind="ExternalInput"))
    b2d = ap(nc.dram_tensor("b2d", [128, HB2D * 512], F32, kind="ExternalInput"))
    mtri4 = ap(nc.dram_tensor("mtri4", [128, 2048], F32, kind="ExternalInput"))
    ctab = ap(nc.dram_tensor("ctab", [128, NCT], F32, kind="ExternalInput"))
    ident = ap(nc.dram_tensor("ident", [128, 128], BF16, kind="ExternalInput"))
    out_sh = ap(nc.dram_tensor("out_shard", [SBT, TOK], F32, kind="ExternalOutput"))

    xn_sp = ap(nc.dram_tensor("xn_sp", [D, SBT], BF16, kind="Internal"))
    cc2_in = nc.dram_tensor("cc2_in", [TOK, SBT], BF16, kind="Internal")
    cc2o = [nc.dram_tensor(f"cc2o{h}", [512, SBT], BF16, kind="Internal") for h in range(NH)]
    qk_sp = ap(nc.dram_tensor("qk_sp", [2048, SBT], BF16, kind="Internal"))
    v_sp = ap(nc.dram_tensor("v_sp", [SBT, 1024], BF16, kind="Internal"))
    masks_r = masks.rearrange("l s c -> (l s) c")
    beff_r = beff.rearrange("l r c -> (l r) c")

    RG = [[0, 1, 2, 3], [4, 5, 6, 7]]
    SP_ALL = 16 * 2 * NSW * NTC

    es = ExitStack()
    sem = {}
    for name in ["ld_x0", "ld_x1", "ld_c", "ld_w0", "ld_w1", "ld_xn0", "ld_xn1",
                 "ld_kv0", "ld_kv1", "ld_qt0", "ld_qt1", "ld_cw",
                 "ld_ao0", "ld_ao1", "ld_ao2", "ld_ao3",
                 "ld_xnm0", "ld_xnm1", "ld_xnm2", "ld_xnm3",
                 "ld_mk0", "ld_mk1", "ld_mk2", "ld_mk3",
                 "ld_mk4", "ld_mk5", "ld_mk6", "ld_mk7",
                 "st_out0", "st_out1", "st_out2", "st_out3", "ld_xc", "ld_hl",
                 "s_lnA", "s_lnV", "s_tp", "s_ev", "st_xn",
                 "p_mm", "e_qk", "e_v", "st_sp",
                 "t_qk", "t_exp", "t_msk", "t_av", "t_dve", "t_sm", "t_ss", "st_cc2", "cc2s", "c_act", "c_hev", "c_fo",
                 "c_mm", "c_dve", "st_out"]:
        sem[name] = es.enter_context(nc.semaphore(name))

    mtri_sb = es.enter_context(nc.sbuf_tensor("mtri_sb", [128, 128], F32))
    btab_sb = es.enter_context(nc.sbuf_tensor("btab_sb", [128, NV], F32))
    ident_sb = es.enter_context(nc.sbuf_tensor("ident_sb", [128, 128], BF16))
    bqkvqk_sb = es.enter_context(nc.sbuf_tensor("bqkvqk_sb", [128, 16], F32))

    with es, nc.Block() as block:
        with ExitStack() as p01:
            xa = p01.enter_context(nc.sbuf_tensor("xa", [128, 2 * D], F32))
            xsq = p01.enter_context(nc.sbuf_tensor("xsq", [128, 2048], F32))
            xnbf = p01.enter_context(nc.sbuf_tensor("xnbf", [128, 3 * 2048], BF16))
            xnT = p01.enter_context(nc.sbuf_tensor("xnT", [128, NDM * 256], BF16))
            lnv_ = p01.enter_context(nc.sbuf_tensor("lnv", [128, 12], F32))
            pt0 = p01.enter_context(nc.psum_tensor("pt0", [128, 512], BF16))
            pt1 = p01.enter_context(nc.psum_tensor("pt1", [128, 512], BF16))
            ptb = [pt0, pt1]
            wsw = p01.enter_context(nc.sbuf_tensor("wsw", [128, 2 * NDM * 768], BF16))
            xnsb = p01.enter_context(nc.sbuf_tensor("xnsb", [128, 2 * NDM * TCW], BF16))
            vbias = p01.enter_context(nc.sbuf_tensor("vbias", [128, 1024], BF16))
            qkev = p01.enter_context(nc.sbuf_tensor("qkev", [128, 4 * TCW], BF16))
            vev = p01.enter_context(nc.sbuf_tensor("vev", [128, 2 * TCW], BF16))
            qkp = [p01.enter_context(nc.psum_tensor(f"qkp{i}", [128, TCW], F32)) for i in range(2)]
            vp = [p01.enter_context(nc.psum_tensor(f"vp{i}", [128, TCW], F32)) for i in range(2)]
            COL = {k: i for i, k in enumerate(
                ["rs", "rssa", "rssb", "rss", "mean", "ms", "var", "lnvv", "rstd", "nb", "eps"])}

            def lv(k):
                return lnv_[:, COL[k]:COL[k] + 1]

            def wb_xn(eng, j):
                eng.dma_start(
                    out=xn_sp[:, 256 * j:256 * j + 256].rearrange("(dm p) c -> p dm c", p=128),
                    in_=xnT.ap().rearrange("p (dm c) -> p dm c", c=256)).then_inc(sem["st_xn"], 16)

            @block.sync
            def _(s):
                for j in range(16):
                    for t2 in range(2):
                        i = 2 * j + t2
                        if i >= 2:
                            s.wait_ge(sem["s_lnA"], 6 * i - 6)
                        s.dma_start(out=xa[:, (i % 2) * D:(i % 2) * D + D],
                                    in_=x_full[128 * i:128 * i + 128, :]).then_inc(sem[f"ld_x{i % 2}"], 16)
                    if j >= 1:
                        s.wait_ge(sem["s_ev"], 16 * j)
                        wb_xn(s, j - 1)
                s.wait_ge(sem["s_ev"], 16 * 16)
                wb_xn(s, 15)
                s.dma_start(out=vbias[:, :], in_=bqkv_vb).then_inc(sem["ld_c"], 16)
                s.wait_ge(sem["st_xn"], 16 * 16)
                for gi in range(NSW * NTC):
                    sw, tc = divmod(gi, NTC)
                    if gi >= 2:
                        s.wait_ge(sem["p_mm"], 6 * (gi - 1))
                    for q4 in range(4):
                        src = xn_sp[1024 * q4:1024 * q4 + 1024, 256 * tc:256 * tc + 256] \
                            .rearrange("(dd p) c -> p dd c", p=128)
                        dst = xnsb.ap().rearrange("p (dm c) -> p dm c", c=TCW)[
                            :, (gi % 2) * NDM + 8 * q4:(gi % 2) * NDM + 8 * q4 + 8, :]
                        s.dma_start(out=dst, in_=src).then_inc(sem[f"ld_xn{gi % 2}"], 16)

            @block.scalar
            def _(sc):
                sc.dma_start(out=mtri_sb[:, :], in_=mtri).then_inc(sem["ld_c"], 16)
                sc.dma_start(out=btab_sb[:, :], in_=btab).then_inc(sem["ld_c"], 16)
                sc.dma_start(out=ident_sb[:, :], in_=ident).then_inc(sem["ld_c"], 16)
                sc.dma_start(out=bqkvqk_sb[:, :], in_=bqkv_qk).then_inc(sem["ld_c"], 16)
                for i in range(32):
                    xi = xa[:, (i % 2) * D:(i % 2) * D + D]
                    sa = ((2 * i) % 3) * 2048
                    sb = ((2 * i + 1) % 3) * 2048
                    sc.wait_ge(sem[f"ld_x{i % 2}"], 16 * (i // 2 + 1))
                    if i >= 1:
                        sc.wait_ge(sem["s_lnV"], 8 * i - 5)
                    sc.activation(xsq[:, :], xi[:, 0:2048], AF.Square).then_inc(sem["s_lnA"], 1)
                    sc.wait_ge(sem["s_lnV"], 8 * i + 2)
                    sc.activation(xsq[:, :], xi[:, 2048:4096], AF.Square).then_inc(sem["s_lnA"], 1)
                    sc.wait_ge(sem["s_lnV"], 8 * i + 7)
                    sc.activation(lv("lnvv"), lv("var"), AF.Ln, bias=lv("eps")).then_inc(sem["s_lnA"], 1)
                    sc.wait_ge(sem["s_lnA"], 6 * i + 3)
                    sc.activation(lv("rstd"), lv("lnvv"), AF.Exp, scale=-0.5).then_inc(sem["s_lnA"], 1)
                    sc.wait_ge(sem["s_lnV"], 8 * i + 8)
                    if i >= 2:
                        sc.wait_ge(sem["s_tp"], 8 * i - 8)
                    sc.activation(xnbf[:, sa:sa + 2048], xi[:, 0:2048], AF.Identity,
                                  bias=lv("nb"), scale=lv("rstd")).then_inc(sem["s_lnA"], 1)
                    if i >= 1:
                        sc.wait_ge(sem["s_tp"], 8 * i - 4)
                    sc.activation(xnbf[:, sb:sb + 2048], xi[:, 2048:4096], AF.Identity,
                                  bias=lv("nb"), scale=lv("rstd")).then_inc(sem["s_lnA"], 1)

            @block.vector
            def _(v):
                def evac_tile(v, i):
                    if i >= 2 and i % 2 == 0:
                        v.wait_ge(sem["st_xn"], 16 * (i // 2))
                    for b in range(8):
                        gi = 8 * i + b
                        v.wait_ge(sem["s_tp"], gi + 1)
                        dst = xnT.ap().rearrange("p (dm c) -> p dm c", c=256)[
                            :, 4 * b:4 * b + 4, 128 * (i % 2):128 * (i % 2) + 128]
                        src = ptb[gi % 2].ap().rearrange("p (j c) -> p j c", c=128)
                        v.tensor_copy(dst, src).then_inc(sem["s_ev"], 1)

                v.memset(lv("eps"), 1e-5)
                for i in range(32):
                    xi = xa[:, (i % 2) * D:(i % 2) * D + D]
                    v.wait_ge(sem[f"ld_x{i % 2}"], 16 * (i // 2 + 1))
                    v.reduce_sum(lv("rs"), xi, axis=AX).then_inc(sem["s_lnV"], 1)
                    v.wait_ge(sem["s_lnA"], 6 * i + 1)
                    v.reduce_sum(lv("rssa"), xsq[:, :], axis=AX).then_inc(sem["s_lnV"], 1)
                    v.wait_ge(sem["s_lnA"], 6 * i + 2)
                    v.reduce_sum(lv("rssb"), xsq[:, :], axis=AX).then_inc(sem["s_lnV"], 1)
                    v.wait_ge(sem["s_lnV"], 8 * i + 3)
                    v.tensor_add(lv("rss"), lv("rssa"), lv("rssb")).then_inc(sem["s_lnV"], 1)
                    v.wait_ge(sem["s_lnV"], 8 * i + 1)
                    v.tensor_scalar_mul(lv("mean"), lv("rs"), 1.0 / D).then_inc(sem["s_lnV"], 1)
                    v.wait_ge(sem["s_lnV"], 8 * i + 5)
                    v.tensor_mul(lv("ms"), lv("mean"), lv("mean")).then_inc(sem["s_lnV"], 1)
                    v.wait_ge(sem["s_lnV"], 8 * i + 6)
                    if i >= 1:
                        v.wait_ge(sem["s_lnA"], 6 * i - 3)
                    v.scalar_tensor_tensor(lv("var"), lv("rss"), 1.0 / D, lv("ms"),
                                           ALU.mult, ALU.subtract).then_inc(sem["s_lnV"], 1)
                    v.wait_ge(sem["s_lnA"], 6 * i + 4)
                    v.scalar_tensor_tensor(lv("nb"), lv("mean"), -1.0, lv("rstd"),
                                           ALU.mult, ALU.mult).then_inc(sem["s_lnV"], 1)
                    if i >= 1:
                        evac_tile(v, i - 1)
                evac_tile(v, 31)

            @block.tensor
            def _(t):
                t.wait_ge(sem["ld_c"], 16 * 4)
                for i in range(32):
                    sa = ((2 * i) % 3) * 2048
                    sb = ((2 * i + 1) % 3) * 2048
                    for b in range(8):
                        gi = 8 * i + b
                        t.wait_ge(sem["s_lnA"], 6 * i + (5 if b < 4 else 6))
                        if gi >= 2:
                            t.wait_ge(sem["s_ev"], gi - 1)
                        last = None
                        for j in range(4):
                            dmc = 4 * b + j
                            off = (sa + 128 * dmc) if b < 4 else (sb + 128 * (dmc - 16))
                            last = t.transpose(
                                ptb[gi % 2][:, 128 * j:128 * j + 128],
                                xnbf[:, off:off + 128],
                                ident_sb[:, :])
                        last.then_inc(sem["s_tp"], 1)


            @block.gpsimd
            def _(g):
                def load_sweep(s):
                    if s >= 2:
                        g.wait_ge(sem["p_mm"], 6 * NTC * (s - 1))
                    for q4 in range(4):
                        src = wqkvT[128 * 8 * q4:128 * 8 * (q4 + 1), 768 * s:768 * s + 768] \
                            .rearrange("(dd p) c -> p dd c", p=128)
                        dst = wsw.ap().rearrange("p (dm c) -> p dm c", c=768)[
                            :, (s % 2) * NDM + 8 * q4:(s % 2) * NDM + 8 * q4 + 8, :]
                        g.dma_start(out=dst, in_=src).then_inc(sem[f"ld_w{s % 2}"], 16)

                load_sweep(0)
                load_sweep(1)
                for gi in range(NSW * NTC):
                    if gi % NTC == 0 and 2 <= gi // NTC + 1 < NSW:
                        load_sweep(gi // NTC + 1)
                    s, tc = divmod(gi, NTC)
                    g.wait_ge(sem["e_qk"], 4 * (gi + 1))
                    dstq = qk_sp[512 * s:512 * s + 512, TCW * tc:TCW * tc + TCW] \
                        .rearrange("(oc p) c -> p oc c", p=128)
                    g.dma_start(out=dstq, in_=qkev.ap().rearrange("p (oc c) -> p oc c", c=TCW)) \
                        .then_inc(sem["st_sp"], 16)
                    g.wait_ge(sem["e_v"], 2 * (gi + 1))
                    dstv = v_sp[TCW * tc:TCW * tc + TCW, 256 * s:256 * s + 256] \
                        .rearrange("(tt p) c -> p tt c", p=128)
                    g.dma_start(out=dstv, in_=vev.ap().rearrange("p (tt c) -> p tt c", c=TCW)) \
                        .then_inc(sem["st_sp"], 16)

            @block.tensor
            def _(t):
                qk_guard = [0, 0]
                v_guard = [0, 0]
                for gi in range(NSW * NTC):
                    s, tc = divmod(gi, NTC)
                    if tc == 0:
                        t.wait_ge(sem[f"ld_w{s % 2}"], 16 * 4 * (s // 2 + 1))
                    t.wait_ge(sem[f"ld_xn{gi % 2}"], 16 * 4 * (gi // 2 + 1))
                    wof = (s % 2) * NDM * 768
                    xof = (gi % 2) * NDM * TCW
                    for oc in range(4):
                        if qk_guard[oc % 2]:
                            t.wait_ge(sem["e_qk"], qk_guard[oc % 2])
                        last = None
                        for dm in range(NDM):
                            last = t.matmul(qkp[oc % 2][:, :],
                                            wsw[:, wof + 768 * dm + 128 * oc:wof + 768 * dm + 128 * oc + 128],
                                            xnsb[:, xof + TCW * dm:xof + TCW * dm + TCW],
                                            start=(dm == 0), stop=(dm == NDM - 1))
                        last.then_inc(sem["p_mm"], 1)
                        qk_guard[oc % 2] = 4 * gi + oc + 1
                    for tt in range(2):
                        if v_guard[tt % 2]:
                            t.wait_ge(sem["e_v"], v_guard[tt % 2])
                        last = None
                        for dm in range(NDM):
                            last = t.matmul(vp[tt % 2][:, :],
                                            xnsb[:, xof + TCW * dm + 128 * tt:xof + TCW * dm + 128 * tt + 128],
                                            wsw[:, wof + 768 * dm + 512:wof + 768 * dm + 768],
                                            start=(dm == 0), stop=(dm == NDM - 1))
                        last.then_inc(sem["p_mm"], 1)
                        v_guard[tt % 2] = 2 * gi + tt + 1

            @block.scalar
            def _(sc):
                sc.wait_ge(sem["ld_c"], 16 * 4)
                for gi in range(NSW * NTC):
                    s, tc = divmod(gi, NTC)
                    for oc in range(4):
                        sc.wait_ge(sem["p_mm"], 6 * gi + oc + 1)
                        if gi >= 1:
                            sc.wait_ge(sem["st_sp"], 16 * 2 * gi)
                        sc.activation(qkev[:, TCW * oc:TCW * oc + TCW], qkp[oc % 2][:, :],
                                      AF.Identity, bias=bqkvqk_sb[:, 4 * s + oc:4 * s + oc + 1]) \
                            .then_inc(sem["e_qk"], 1)

            @block.vector
            def _(v):
                v.wait_ge(sem["ld_c"], 16 * 5)
                for gi in range(NSW * NTC):
                    s, tc = divmod(gi, NTC)
                    for tt in range(2):
                        v.wait_ge(sem["p_mm"], 6 * gi + 4 + tt + 1)
                        if gi >= 1:
                            v.wait_ge(sem["st_sp"], 16 * 2 * gi)
                        v.tensor_add(vev[:, TCW * tt:TCW * tt + TCW], vp[tt % 2][:, :],
                                     vbias[:, 256 * s:256 * s + 256]).then_inc(sem["e_v"], 1)

        with ExitStack() as p2:
            ktsb = p2.enter_context(nc.sbuf_tensor("ktsb", [128, 2 * SBT], BF16))
            vsb = p2.enter_context(nc.sbuf_tensor("vsb", [128, 2 * SBT], BF16))
            qtsb = p2.enter_context(nc.sbuf_tensor("qtsb", [128, 2 * 512], BF16))
            etc_l = [p2.enter_context(nc.sbuf_tensor(f"etc{j}", [128, 512], BF16)) for j in range(2)]
            etd_l = [p2.enter_context(nc.sbuf_tensor(f"etd{j}", [128, 512], BF16)) for j in range(4)]
            etn_l = [p2.enter_context(nc.sbuf_tensor(f"etn{j}", [128, 512], BF16)) for j in range(2)]
            smb = [p2.enter_context(nc.sbuf_tensor(f"sm{j}", [128, 128], F32)) for j in range(2)]
            smn = [p2.enter_context(nc.sbuf_tensor(f"smn{j}", [128, 512], F32)) for j in range(2)]
            b2d_sb = p2.enter_context(nc.sbuf_tensor("b2d_sb", [128, HB2D * 512], F32))
            ctab_sb = p2.enter_context(nc.sbuf_tensor("ctab_sb", [128, NCT], F32))
            mtri4_sb = p2.enter_context(nc.sbuf_tensor("mtri4_sb", [128, 2048], F32))
            ones_c = p2.enter_context(nc.sbuf_tensor("ones_c", [128, 1], BF16))
            ones_r = p2.enter_context(nc.sbuf_tensor("ones_r", [1, 128], BF16))
            rinv_b = p2.enter_context(nc.sbuf_tensor("rinv_b", [1, 512], BF16))
            rbsb = p2.enter_context(nc.sbuf_tensor("rbsb", [128, 512], F32))
            osb = p2.enter_context(nc.sbuf_tensor("osb", [128, 2 * 512], BF16))
            stp_ = [p2.enter_context(nc.psum_tensor(f"stp{i}", [128, 512], F32)) for i in range(2)]
            otp = [p2.enter_context(nc.psum_tensor(f"otp{i}", [128, 512], F32)) for i in range(2)]
            lpb = [p2.enter_context(nc.psum_tensor(f"lp{i}", [1, 512], F32)) for i in range(2)]
            rbp = p2.enter_context(nc.psum_tensor("rbp", [128, 512], F32))

            def etbuf(h, g, kt):
                j = kt - 4 * g
                if h < HB2D:
                    return etn_l[kt % 2]
                return etd_l[j] if j >= 0 else etc_l[kt % 2]

            @block.sync
            def _(s_):
                s_.wait_ge(sem["st_sp"], SP_ALL)
                s_.dma_start(out=b2d_sb[:, :], in_=b2d).then_inc(sem["ld_c"], 16)
                s_.dma_start(out=mtri4_sb[:, :], in_=mtri4).then_inc(sem["ld_c"], 16)
                s_.dma_start(out=ctab_sb[:, :], in_=ctab).then_inc(sem["ld_c"], 16)
                for h in range(NH):
                    if h >= 2:
                        s_.wait_ge(sem["t_av"], AV_AFTER_HEAD[h - 1])
                    s_.dma_start(out=ktsb[:, (h % 2) * SBT:(h % 2) * SBT + SBT],
                                 in_=qk_sp[256 * h + 128:256 * h + 256, :]).then_inc(sem[f"ld_kv{h % 2}"], 16)
                    for q4 in range(4):
                        src = v_sp[1024 * q4:1024 * (q4 + 1), 128 * h:128 * h + 128] \
                            .rearrange("(kt p) c -> p kt c", p=128)
                        dst = vsb.ap().rearrange("p (kt c) -> p kt c", c=128)[
                            :, (h % 2) * NDM + 8 * q4:(h % 2) * NDM + 8 * q4 + 8, :]
                        s_.dma_start(out=dst, in_=src).then_inc(sem[f"ld_kv{h % 2}"], 16)
                    for g in range(NG):
                        hg = h * NG + g
                        if hg >= 2:
                            h2, g2_ = divmod(hg - 2, NG)
                            s_.wait_ge(sem["t_qk"], QKC[(h2, g2_)] + len(LIVE[(h2, g2_)]))
                        s_.dma_start(out=qtsb[:, (g % 2) * 512:(g % 2) * 512 + 512],
                                     in_=qk_sp[256 * h:256 * h + 128, 512 * g:512 * g + 512]) \
                            .then_inc(sem[f"ld_qt{g % 2}"], 16)

            @block.tensor
            def _(t):
                t.wait_ge(sem["st_sp"], SP_ALL)
                t.wait_ge(sem["t_dve"], 1)
                bank_guard = [None, None]
                for h in range(NH):
                    t.wait_ge(sem[f"ld_kv{h % 2}"], 16 * 5 * (h // 2 + 1))
                    kof = (h % 2) * SBT
                    vof = (h % 2) * SBT
                    for g in range(NG):
                        hg = h * NG + g
                        lv = LIVE[(h, g)]
                        qb = QKC[(h, g)]
                        t.wait_ge(sem[f"ld_qt{g % 2}"], 16 * (h * 4 + g // 2 + 1))
                        qt_ap = qtsb[:, (g % 2) * 512:(g % 2) * 512 + 512]

                        def st_mm(i, h=h, g=g, qt_ap=qt_ap, qb=qb, lv=lv):
                            kt = lv[i]
                            if bank_guard[kt % 2] is not None:
                                sname, v = bank_guard[kt % 2]
                                t.wait_ge(sem[sname], v)
                            t.matmul(stp_[kt % 2][:, :],
                                     ktsb[:, kof + 128 * kt:kof + 128 * kt + 128],
                                     qt_ap, start=True, stop=True).then_inc(sem["t_qk"], 1)
                            if h < HB2D:
                                bank_guard[kt % 2] = ("t_sm", SMC[(h, g, kt)])
                            else:
                                bank_guard[kt % 2] = ("t_exp", EXPC[(h, g, kt)])

                        st_mm(0)
                        if len(lv) > 1:
                            st_mm(1)
                        for i, kt in enumerate(lv):
                            t.wait_ge(sem["t_exp"], EXPC[(h, g, kt)])
                            if i == 0 and hg >= 2:
                                t.wait_ge(sem["t_dve"], 3 * hg - 2)
                            eb = etbuf(h, g, kt)
                            t.matmul(otp[hg % 2][:, :], vsb[:, vof + 128 * kt:vof + 128 * kt + 128],
                                     eb[:, :], start=(i == 0), stop=(i == len(lv) - 1),
                                     skip_group_check=True).then_inc(sem["t_av"], 1)
                            t.matmul(lpb[hg % 2][:, :], ones_c[:, :], eb[:, :],
                                     start=(i == 0), stop=(i == len(lv) - 1),
                                     skip_group_check=True).then_inc(sem["t_av"], 1)
                            if i + 2 < len(lv):
                                st_mm(i + 2)
                        t.wait_ge(sem["t_dve"], 3 * hg + 2)
                        t.matmul(rbp[:, :], ones_r[:, :], rinv_b[:, :],
                                 start=True, stop=True).then_inc(sem["t_qk"], 1)

            @block.vector
            def _(v):
                v.wait_ge(sem["st_sp"], SP_ALL)
                v.wait_ge(sem["ld_c"], 16 * 8)
                v.memset(ones_c[:, :], 1.0)
                v.memset(ones_r[:, :], 1.0)
                v.memset(etd_l[1][:, 0:128], 0.0)
                v.memset(etd_l[2][:, 0:256], 0.0)
                v.memset(etd_l[3][:, 0:384], 0.0).then_inc(sem["t_dve"], 1)
                dve_cum = 1
                ss_cum = 0
                sm_guard = [0, 0]
                smn_guard = [0, 0]
                for h in range(NH):
                    for g in range(NG):
                        hg = h * NG + g
                        lv = LIVE[(h, g)]
                        qb = QKC[(h, g)]
                        if h < HB2D:
                            for i, kt in enumerate(lv):
                                j = kt - 4 * g
                                v.wait_ge(sem["t_qk"], qb + i + 1)
                                if smn_guard[kt % 2]:
                                    v.wait_ge(sem["t_exp"], smn_guard[kt % 2])
                                ci = CTI[(h, kt - 4 * g)]
                                last = v.scalar_tensor_tensor(
                                    smn[kt % 2][:, :], stp_[kt % 2][:, :],
                                    ctab_sb[:, ci:ci + 1],
                                    b2d_sb[:, 512 * h:512 * h + 512], ALU.add, ALU.add)
                                if j >= 0:
                                    last.then_inc(sem["t_ss"], 1)
                                    ss_cum += 1
                                    v.wait_ge(sem["t_ss"], ss_cum)
                                    last = v.tensor_add(
                                        smn[kt % 2][:, 0:128 * (j + 1)],
                                        smn[kt % 2][:, 0:128 * (j + 1)],
                                        mtri4_sb[:, 512 * j:512 * j + 128 * (j + 1)])
                                last.then_inc(sem["t_sm"], 1)
                                smn_guard[kt % 2] = EXPC[(h, g, kt)]
                        else:
                            for kt in range(4 * g, 4 * g + 4):
                                j = kt - 4 * g
                                i = lv.index(kt)
                                v.wait_ge(sem["t_qk"], qb + i + 1)
                                if sm_guard[j % 2]:
                                    v.wait_ge(sem["t_exp"], sm_guard[j % 2])
                                v.tensor_add(smb[j % 2][:, :], stp_[kt % 2][:, 128 * j:128 * j + 128],
                                             mtri_sb[:, :]).then_inc(sem["t_msk"], 1)
                                sm_guard[j % 2] = EXPC[(h, g, kt)]
                        v.wait_ge(sem["t_av"], 2 * EXPC[(h, g, lv[-1])])
                        v.tensor_copy(rinv_b[:, :], lpb[hg % 2][:, :]).then_inc(sem["t_dve"], 1)
                        dve_cum += 1
                        v.wait_ge(sem["t_qk"], qb + len(lv) + 1)
                        v.reciprocal(rbsb[:, :], rbp[:, :]).then_inc(sem["t_dve"], 1)
                        dve_cum += 1
                        v.wait_ge(sem["t_dve"], dve_cum)
                        if hg >= 2:
                            v.wait_ge(sem["st_cc2"], 16 * hg)
                        v.tensor_mul(osb[:, (hg % 2) * 512:(hg % 2) * 512 + 512], otp[hg % 2][:, :],
                                     rbsb[:, :]).then_inc(sem["t_dve"], 1)
                        dve_cum += 1

            @block.scalar
            def _(sc):
                sc.wait_ge(sem["ld_c"], 16 * 4)
                sc.wait_ge(sem["t_dve"], 1)
                etc_guard = [0, 0]
                etd_guard = [0, 0, 0, 0]
                etn_guard = [0, 0]
                for h in range(NH):
                    for g in range(NG):
                        hg = h * NG + g
                        lv = LIVE[(h, g)]
                        qb = QKC[(h, g)]
                        for i, kt in enumerate(lv):
                            j = kt - 4 * g
                            if h < HB2D:
                                sc.wait_ge(sem["t_sm"], SMC[(h, g, kt)])
                                if etn_guard[kt % 2]:
                                    sc.wait_ge(sem["t_av"], etn_guard[kt % 2])
                                dest = etn_l[kt % 2]
                                etn_guard[kt % 2] = 2 * EXPC[(h, g, kt)]
                                sc.activation(dest[:, :], smn[kt % 2][:, :], AF.Exp,
                                              scale=SCL).then_inc(sem["t_exp"], 1)
                                continue
                            sc.wait_ge(sem["t_qk"], qb + i + 1)
                            if j >= 0:
                                sc.wait_ge(sem["t_msk"], MSKC[(h, g, kt)])
                                if etd_guard[j]:
                                    sc.wait_ge(sem["t_av"], etd_guard[j])
                                dest = etd_l[j]
                                etd_guard[j] = 2 * EXPC[(h, g, kt)]
                            else:
                                if etc_guard[kt % 2]:
                                    sc.wait_ge(sem["t_av"], etc_guard[kt % 2])
                                dest = etc_l[kt % 2]
                                etc_guard[kt % 2] = 2 * EXPC[(h, g, kt)]
                            last = None
                            for (a, b, srck, col) in _exp_pieces(h, g, kt):
                                if srck == "sm":
                                    srcp = smb[j % 2][:, a - 128 * j:b - 128 * j]
                                else:
                                    srcp = stp_[kt % 2][:, a:b]
                                last = sc.activation(dest[:, a:b], srcp, AF.Exp,
                                                     bias=btab_sb[:, col:col + 1], scale=SCL)
                            last.then_inc(sem["t_exp"], 1)

            @block.gpsimd
            def _(g_):
                for h in range(NH):
                    for g in range(NG):
                        hg = h * NG + g
                        g_.wait_ge(sem["t_dve"], 1 + 3 * (hg + 1))
                        g_.dma_start(out=cc2_in.ap()[128 * h:128 * h + 128, 512 * g:512 * g + 512],
                                     in_=osb[:, (hg % 2) * 512:(hg % 2) * 512 + 512]) \
                            .then_inc(sem["st_cc2"], 16)
                    g_.wait_ge(sem["st_cc2"], 16 * NG * (h + 1))
                    g_.collective_compute("AllGather", ALU.bypass, replica_groups=RG,
                                          ins=[cc2_in.ap()[128 * h:128 * h + 128, :]],
                                          outs=[cc2o[h].ap()]).then_inc(sem["cc2s"])
        with ExitStack() as p3:
            wout_sb = p3.enter_context(nc.sbuf_tensor("wout_sb", [128, NDM * 1024], BF16))
            w0_sb = p3.enter_context(nc.sbuf_tensor("w0_sb", [128, NDM * 1024], BF16))
            aeff_sb = p3.enter_context(nc.sbuf_tensor("aeff_sb", [128, NDM * 64], BF16))
            beff_sb = p3.enter_context(nc.sbuf_tensor("beff_sb", [16, 4 * 1024], BF16))
            bias_bc = p3.enter_context(nc.sbuf_tensor("bias_bc", [128, 1024], BF16))
            aorng = p3.enter_context(nc.sbuf_tensor("aorng", [128, 4 * 8 * TCW], BF16))
            xnrng = p3.enter_context(nc.sbuf_tensor("xnrng", [128, 4 * 8 * TCW], BF16))
            mkrng = p3.enter_context(nc.sbuf_tensor("mkrng", [128, 4 * 1024], BF16))
            xcs = p3.enter_context(nc.sbuf_tensor("xcs", [128, 2048], F32))
            hsb = p3.enter_context(nc.sbuf_tensor("hsb", [64, TCW], BF16))
            hlsb = p3.enter_context(nc.sbuf_tensor("hlsb", [16, 1024], BF16))
            ysb = p3.enter_context(nc.sbuf_tensor("ysb", [128, 4 * 512], BF16))
            fsb = [p3.enter_context(nc.sbuf_tensor(f"fsb{i}", [128, 512], F32)) for i in range(4)]
            tmp_sb = p3.enter_context(nc.sbuf_tensor("tmp_sb", [128, 512], BF16))
            hp = p3.enter_context(nc.psum_tensor("hp", [64, TCW], F32))
            Pp = [[p3.enter_context(nc.psum_tensor(f"Pp{t_}{o_}", [128, 512], F32))
                   for o_ in range(2)] for t_ in range(2)]
            yp = [p3.enter_context(nc.psum_tensor(f"yp{i}", [128, 512], F32)) for i in range(2)]

            @block.gpsimd
            def _(g_):
                g_.wait_ge(sem["t_av"], 2 * T_EXP_ALL)
                g_.wait_ge(sem["t_dve"], 1 + 3 * NH * NG)
                for (srcT, dstT, cw) in ((woutT, wout_sb, 1024), (w0T, w0_sb, 1024),
                                         (aeff, aeff_sb, 64)):
                    for q4 in range(4):
                        srcq = srcT[128 * 8 * q4:128 * 8 * (q4 + 1), :] \
                            .rearrange("(dd p) c -> p dd c", p=128)
                        dst = dstT.ap().rearrange("p (dm c) -> p dm c", c=cw)[
                            :, 8 * q4:8 * q4 + 8, :]
                        g_.dma_start(out=dst, in_=srcq).then_inc(sem["ld_cw"], 16)
                for l in range(4):
                    g_.dma_start(out=beff_sb[:, 1024 * l:1024 * l + 1024],
                                 in_=beff_r[16 * l:16 * l + 16, :]).then_inc(sem["ld_cw"], 16)
                mk_guard = [0, 0, 0, 0]
                for g2 in range(NG2):
                    g_.wait_ge(sem["c_hev"], g2 + 1)
                    if g2 >= 1:
                        g_.wait_ge(sem["c_mm"], 80 * g2)
                    for l in range(4):
                        g_.dma_start(out=hlsb[:, TCW * l:TCW * l + TCW],
                                     in_=hsb[16 * l:16 * l + 16, :]).then_inc(sem["ld_hl"], 16)
                    for oc2 in range(2):
                        for l in range(4):
                            if mk_guard[l]:
                                g_.wait_ge(sem["c_dve"], mk_guard[l])
                            srcm = masks_r[SBT * l + TCW * g2:SBT * l + TCW * g2 + TCW,
                                           512 * oc2:512 * oc2 + 512] \
                                .rearrange("(tt p) c -> p tt c", p=128)
                            dst = mkrng.ap()[:, 1024 * l:1024 * l + 1024] \
                                .rearrange("p (tt c) -> p tt c", c=512)
                            g_.dma_start(out=dst, in_=srcm).then_inc(sem[f"ld_mk{l}"], 16)
                            mk_guard[l] = _cd_mul(g2, 8 * oc2 + 4 + l)

            @block.sync
            def _(s_):
                s_.wait_ge(sem["cc2s"], 1)
                s_.dma_start(out=bias_bc[:, :], in_=b0bc).then_inc(sem["ld_c"], 16)
                s_.dma_start(out=fsb[0][:, :], in_=boutbc[:, 0:512]).then_inc(sem["ld_c"], 16)
                s_.dma_start(out=fsb[1][:, :], in_=boutbc[:, 512:1024]).then_inc(sem["ld_c"], 16)
                for k in range(4):
                    srcx = xn_sp[1024 * k:1024 * k + 1024, 0:TCW] \
                        .rearrange("(dd p) c -> p dd c", p=128)
                    dst = xnrng.ap().rearrange("p (dm c) -> p dm c", c=TCW)[
                        :, 8 * k:8 * k + 8, :]
                    s_.dma_start(out=dst, in_=srcx).then_inc(sem[f"ld_xnm{k}"], 16)
                for g2 in range(NG2):
                    if g2 >= 1:
                        for fo, (tt2, oc2) in enumerate(FIS):
                            s_.wait_ge(sem["c_dve"], _cd_final(g2 - 1, fo))
                            s_.dma_start(
                                out=out_sh[TCW * (g2 - 1) + 128 * tt2:TCW * (g2 - 1) + 128 * tt2 + 128,
                                           512 * oc2:512 * oc2 + 512],
                                in_=fsb[fo][:, :]).then_inc(sem[f"st_out{fo}"], 16)
                    if g2 < NG2 - 1:
                        for k in range(4):
                            u = 4 * (g2 + 1) + k
                            s_.wait_ge(sem["c_mm"], _cm_h1(g2, k))
                            srcx = xn_sp[1024 * k:1024 * k + 1024,
                                         TCW * (g2 + 1):TCW * (g2 + 1) + TCW] \
                                .rearrange("(dd p) c -> p dd c", p=128)
                            dst = xnrng.ap().rearrange("p (dm c) -> p dm c", c=TCW)[
                                :, 8 * k:8 * k + 8, :]
                            s_.dma_start(out=dst, in_=srcx).then_inc(sem[f"ld_xnm{k}"], 16)
                    if g2 >= 1:
                        s_.wait_ge(sem["c_dve"], _cd_xc(g2 - 1, 3))
                    srcx = x_cols[TCW * g2:TCW * g2 + TCW, :].rearrange("(tt p) c -> p tt c", p=128)
                    s_.dma_start(out=xcs.ap().rearrange("p (tt c) -> p tt c", c=1024),
                                 in_=srcx).then_inc(sem["ld_xc"], 16)
                g2 = NG2 - 1
                for fo, (tt2, oc2) in enumerate(FIS):
                    s_.wait_ge(sem["c_dve"], _cd_final(g2, fo))
                    s_.dma_start(out=out_sh[TCW * g2 + 128 * tt2:TCW * g2 + 128 * tt2 + 128,
                                            512 * oc2:512 * oc2 + 512],
                                 in_=fsb[fo][:, :]).then_inc(sem[f"st_out{fo}"], 16)
                for fo in range(4):
                    s_.wait_ge(sem[f"st_out{fo}"], 16 * NG2)

            @block.scalar
            def _(sc):
                sc.wait_ge(sem["cc2s"], 1)
                fs_guard = [0, 0, 0, 0]
                for k in range(4):
                    sc.wait_ge(sem["cc2s"], 2 * k + 2)
                    for hh in range(2):
                        srca = cc2o[2 * k + hh].ap()[:, 0:TCW] \
                            .rearrange("(dd p) c -> p dd c", p=128)
                        dst = aorng.ap().rearrange("p (dm c) -> p dm c", c=TCW)[
                            :, 8 * k + 4 * hh:8 * k + 4 * hh + 4, :]
                        sc.dma_start(out=dst, in_=srca).then_inc(sem[f"ld_ao{k}"], 16)
                for g2 in range(NG2):
                    sc.wait_ge(sem["c_mm"], 80 * g2 + 32)
                    if g2 >= 1:
                        sc.wait_ge(sem["ld_hl"], 16 * 4 * g2)
                    sc.copy(hsb[:, :], hp[:, :]).then_inc(sem["c_hev"], 1)
                    for fo in (0, 1):
                        tt2, oc2 = FIS[fo]
                        if g2 == 0:
                            sc.wait_ge(sem["c_dve"], 1)
                        if fs_guard[fo]:
                            sc.wait_ge(sem[f"st_out{fo}"], fs_guard[fo])
                        sc.copy(fsb[fo][:, :], Pp[tt2][oc2][:, :]).then_inc(sem["c_act"], 1)
                        fs_guard[fo] = 16 * (g2 + 1)
                    if g2 < NG2 - 1:
                        for k in range(4):
                            sc.wait_ge(sem["c_mm"], _cm_h1(g2, k))
                            for hh in range(2):
                                srca = cc2o[2 * k + hh].ap()[:, TCW * (g2 + 1):TCW * (g2 + 1) + TCW] \
                                    .rearrange("(dd p) c -> p dd c", p=128)
                                dst = aorng.ap().rearrange("p (dm c) -> p dm c", c=TCW)[
                                    :, 8 * k + 4 * hh:8 * k + 4 * hh + 4, :]
                                sc.dma_start(out=dst, in_=srca).then_inc(sem[f"ld_ao{k}"], 16)
                    else:
                        sc.wait_ge(sem["c_mm"], 80 * g2 + 64)
                    for fo in (2, 3):
                        tt2, oc2 = FIS[fo]
                        if fs_guard[fo]:
                            sc.wait_ge(sem[f"st_out{fo}"], fs_guard[fo])
                        sc.copy(fsb[fo][:, :], Pp[tt2][oc2][:, :]).then_inc(sem["c_act"], 1)
                        fs_guard[fo] = 16 * (g2 + 1)
                    for yi in range(16):
                        sc.wait_ge(sem["c_mm"], _cm_y(g2, yi))
                        if g2 > 0 or yi >= 4:
                            gy, yy = (g2, yi - 4) if yi >= 4 else (g2 - 1, 12 + yi)
                            sc.wait_ge(sem["c_dve"], _cd_mul(gy, yy))
                        sc.activation(ysb[:, 512 * (yi % 4):512 * (yi % 4) + 512], yp[yi % 2][:, :],
                                      AF.Identity, scale=8.0).then_inc(sem["c_act"], 1)

            @block.tensor
            def _(t):
                t.wait_ge(sem["cc2s"], 1)
                t.wait_ge(sem["ld_cw"], 16 * 16)
                for g2 in range(NG2):
                    for t2 in range(2):
                        for k in range(4):
                            t.wait_ge(sem[f"ld_xnm{k}"], 16 * (g2 + 1))
                            t.wait_ge(sem[f"ld_ao{k}"], 32 * (g2 + 1))
                            for dd in range(8):
                                dm = 8 * k + dd
                                if dm == 0 and g2 >= 1:
                                    if t2 == 0:
                                        t.wait_ge(sem["c_act"], _ca_F(g2 - 1, 1))
                                        t.wait_ge(sem["c_hev"], g2)
                                    else:
                                        t.wait_ge(sem["c_act"], _ca_F(g2 - 1, 3))
                                xof = k * 8 * TCW + TCW * dd
                                for oc2 in range(2):
                                    t.matmul(Pp[t2][oc2][:, :],
                                             aorng[:, xof + 128 * t2:xof + 128 * t2 + 128],
                                             wout_sb[:, 1024 * dm + 512 * oc2:1024 * dm + 512 * oc2 + 512],
                                             start=(dm == 0), stop=False,
                                             skip_group_check=True)
                                last = None
                                for oc2 in range(2):
                                    last = t.matmul(Pp[t2][oc2][:, :],
                                                    xnrng[:, xof + 128 * t2:xof + 128 * t2 + 128],
                                                    w0_sb[:, 1024 * dm + 512 * oc2:1024 * dm + 512 * oc2 + 512],
                                                    start=False, stop=(dm == NDM - 1),
                                                    skip_group_check=True)
                                if t2 == 0:
                                    last = t.matmul(hp[:, :], aeff_sb[:, 64 * dm:64 * dm + 64],
                                                    xnrng[:, xof:xof + TCW],
                                                    start=(dm == 0), stop=(dm == NDM - 1),
                                                    skip_group_check=True)
                                last.then_inc(sem["c_mm"], 1)
                    t.wait_ge(sem["ld_hl"], 16 * 4 * (g2 + 1))
                    for yi, (oc2, tt2, l) in enumerate(YIS):
                        if yi >= 2:
                            t.wait_ge(sem["c_act"], _ca_y(g2, yi - 2))
                        elif g2 >= 1:
                            t.wait_ge(sem["c_act"], _ca_y(g2 - 1, 14 + yi))
                        t.matmul(yp[yi % 2][:, :],
                                 hlsb[:, TCW * l + 128 * tt2:TCW * l + 128 * tt2 + 128],
                                 beff_sb[:, 1024 * l + 512 * oc2:1024 * l + 512 * oc2 + 512],
                                 start=True, stop=True).then_inc(sem["c_mm"], 1)

            @block.vector
            def _(v):
                v.wait_ge(sem["ld_c"], 16 * 11)
                v.tensor_add(bias_bc[:, 0:512], bias_bc[:, 0:512], fsb[0][:, :])
                v.tensor_add(bias_bc[:, 512:1024], bias_bc[:, 512:1024],
                             fsb[1][:, :]).then_inc(sem["c_dve"], 1)
                cd = 1
                for g2 in range(NG2):
                    for fo, (tt2, oc2) in enumerate(FIS):
                        v.wait_ge(sem["c_act"], _ca_F(g2, fo))
                        v.wait_ge(sem["c_dve"], cd)
                        v.tensor_add(fsb[fo][:, :], fsb[fo][:, :],
                                     bias_bc[:, 512 * oc2:512 * oc2 + 512]).then_inc(sem["c_dve"], 1)
                        cd += 1
                        v.wait_ge(sem["ld_xc"], 16 * (g2 + 1))
                        v.wait_ge(sem["c_dve"], cd)
                        v.tensor_add(fsb[fo][:, :], fsb[fo][:, :],
                                     xcs[:, 1024 * tt2 + 512 * oc2:1024 * tt2 + 512 * oc2 + 512]) \
                            .then_inc(sem["c_dve"], 1)
                        cd += 1
                    for yi, (oc2, tt2, l) in enumerate(YIS):
                        fo = 2 * tt2 + oc2
                        v.wait_ge(sem["c_act"], _ca_y(g2, yi))
                        v.wait_ge(sem[f"ld_mk{l}"], 16 * (2 * g2 + oc2 + 1))
                        v.wait_ge(sem["c_dve"], cd)
                        v.tensor_mul(tmp_sb[:, :], ysb[:, 512 * (yi % 4):512 * (yi % 4) + 512],
                                     mkrng[:, 1024 * l + 512 * tt2:1024 * l + 512 * tt2 + 512]) \
                            .then_inc(sem["c_dve"], 1)
                        cd += 1
                        v.wait_ge(sem["c_dve"], cd)
                        v.tensor_add(fsb[fo][:, :], fsb[fo][:, :], tmp_sb[:, :]) \
                            .then_inc(sem["c_dve"], 1)
                        cd += 1

    return nc



def _mask_u():
    def threefry_pair(k0, k1, x0, x1):
        ks = [np.uint32(k0), np.uint32(k1),
              np.uint32(np.uint32(k0) ^ np.uint32(k1) ^ np.uint32(0x1BD11BDA))]
        rot = [[13, 15, 26, 6], [17, 29, 16, 24]]
        x0 = x0 + ks[0]
        x1 = x1 + ks[1]
        for i in range(5):
            for r in rot[i % 2]:
                x0 = x0 + x1
                x1 = (x1 << np.uint32(r)) | (x1 >> np.uint32(32 - r))
                x1 = x1 ^ x0
            x0 = x0 + ks[(i + 1) % 3]
            x1 = x1 + ks[(i + 2) % 3] + np.uint32(i + 1)
        return x0, x1
    n = 2 * SBT * D
    idx = np.arange(n, dtype=np.uint64)
    hi = (idx >> np.uint64(32)).astype(np.uint32)
    lo = (idx & np.uint64(0xFFFFFFFF)).astype(np.uint32)
    o0, o1 = threefry_pair(0, 42, hi, lo)
    bits = o0 ^ o1
    f = ((bits >> np.uint32(9)) | np.uint32(0x3F800000)).view(np.float32) - np.float32(1.0)
    return f.reshape(2, SBT, D)


def kernel(L0_attn_x, L0_attn_w_qkv, L0_attn_b_qkv, L0_attn_w_out, L0_attn_b_out,
           L0_ffn_W0, L0_ffn_b0, L0_ffn_A, L0_ffn_B):
    x = np.asarray(L0_attn_x, dtype=np.float32)
    w_qkv = np.asarray(L0_attn_w_qkv, dtype=np.float32)
    b_qkv = np.asarray(L0_attn_b_qkv, dtype=np.float32)
    w_out = np.asarray(L0_attn_w_out, dtype=np.float32)
    b_out = np.asarray(L0_attn_b_out, dtype=np.float32)
    W0 = np.asarray(L0_ffn_W0, dtype=np.float32)
    b0 = np.asarray(L0_ffn_b0, dtype=np.float32)
    A = np.asarray(L0_ffn_A, dtype=np.float32)
    B = np.asarray(L0_ffn_B, dtype=np.float32)

    m = np.arange(1, 33, dtype=np.float32) * (8.0 / 32.0)
    slopes = (1.0 / np.power(np.float32(2.0), m)).astype(np.float32)
    u = _mask_u()
    keep = np.full(4, 0.25, np.float32)
    hi = np.cumsum(keep)
    lo = hi - keep

    d_perm = np.array([128 * (rk + 4 * hl) + hd
                       for hl in range(8) for rk in range(4) for hd in range(128)])
    woutT_perm = np.ascontiguousarray(w_out.T[d_perm])
    w0T_full = np.ascontiguousarray(W0.T)

    mtri_v = np.where(np.arange(128)[:, None] > np.arange(128)[None, :],
                      np.float32(MASKNEG), np.float32(0.0)).astype(np.float32)
    ident_v = np.eye(128, dtype=ml_dtypes.bfloat16)

    wq, wk, wv = w_qkv[0:D], w_qkv[D:2 * D], w_qkv[2 * D:3 * D]
    bq, bk, bv = b_qkv[0:D], b_qkv[D:2 * D], b_qkv[2 * D:3 * D]

    nc = build_nc()
    in_maps = []
    for c in range(8):
        r, beta = c % 4, c // 4
        heads = [r + 4 * j for j in range(NH)]
        cols = slice(1024 * r, 1024 * r + 1024)

        blocks = []
        vb_row = np.zeros(1024, np.float32)
        bqkvqk_v = np.zeros((128, 16), np.float32)
        for s in range(NSW):
            hA, hB = heads[2 * s], heads[2 * s + 1]
            blocks.append(np.concatenate([
                wq[128 * hA:128 * hA + 128], wk[128 * hA:128 * hA + 128],
                wq[128 * hB:128 * hB + 128], wk[128 * hB:128 * hB + 128],
                wv[128 * hA:128 * hA + 128], wv[128 * hB:128 * hB + 128]], axis=0))
            bqkvqk_v[:, 4 * s + 0] = bq[128 * hA:128 * hA + 128]
            bqkvqk_v[:, 4 * s + 1] = bk[128 * hA:128 * hA + 128]
            bqkvqk_v[:, 4 * s + 2] = bq[128 * hB:128 * hB + 128]
            bqkvqk_v[:, 4 * s + 3] = bk[128 * hB:128 * hB + 128]
            vb_row[256 * s:256 * s + 128] = bv[128 * hA:128 * hA + 128]
            vb_row[256 * s + 128:256 * s + 256] = bv[128 * hB:128 * hB + 128]
        wqkvT_v = np.ascontiguousarray(np.concatenate(blocks, axis=0).T)

        aeff_v = np.zeros((D, 64), np.float32)
        beff_v = np.zeros((4, 16, 1024), np.float32)
        for l in range(4):
            aeff_v[:, 16 * l:16 * l + 16] = A[l].T
            beff_v[l] = B[l, cols, :].T

        btab_v = np.zeros((128, NV), np.float32)
        i128 = np.arange(128, dtype=np.float32)
        for (hl, off), colx in TAB.items():
            btab_v[:, colx] = slopes[heads[hl]] * (i128 + off) - CEXP

        b2d_v = np.zeros((128, HB2D * 512), np.float32)
        for hl in range(HB2D):
            w = WCH[hl]
            sl = np.float64(slopes[heads[hl]])
            ql = np.arange(512)
            anc = (ql // w) * w + (w - 1)
            b2d_v[:, 512 * hl:512 * hl + 512] = (
                SQ128 * (sl * (i128[:, None] - anc[None, :]) - CEXP)).astype(np.float32)
        ctab_v = np.zeros((128, NCT), np.float32)
        for (hl, dd), colx in CTI.items():
            ctab_v[:, colx] = np.float32(SQ128 * float(slopes[heads[hl]]) * 128.0 * dd)
        mtri4_v = np.zeros((128, 2048), np.float32)
        for j in range(4):
            blk = np.zeros((128, 512), np.float32)
            blk[:, :128 * j] = MASKNEG
            ii = np.arange(128)
            blk[:, 128 * j:128 * j + 128] = np.where(
                ii[:, None] > ii[None, :], np.float32(MASKNEG), np.float32(0.0))
            mtri4_v[:, 512 * j:512 * j + 512] = blk

        masks_v = np.empty((4, SBT, 1024), dtype=ml_dtypes.bfloat16)
        ub = u[beta][:, cols]
        for l in range(4):
            masks_v[l] = ((ub >= lo[l]) & (ub < hi[l])).astype(ml_dtypes.bfloat16)

        in_maps.append({
            "x_full": np.ascontiguousarray(x[beta]),
            "x_cols": np.ascontiguousarray(x[beta][:, cols]),
            "wqkvT": wqkvT_v,
            "bqkv_qk": bqkvqk_v,
            "bqkv_vb": np.ascontiguousarray(
                np.broadcast_to(vb_row[None, :], (128, 1024))).astype(ml_dtypes.bfloat16),
            "woutT": np.ascontiguousarray(woutT_perm[:, cols]),
            "w0T": np.ascontiguousarray(w0T_full[:, cols]),
            "b0bc": np.ascontiguousarray(np.broadcast_to(b0[cols][None, :], (128, 1024))).astype(ml_dtypes.bfloat16),
            "boutbc": np.ascontiguousarray(np.broadcast_to(b_out[cols][None, :], (128, 1024))),
            "aeff": aeff_v,
            "beff": beff_v,
            "masks": masks_v,
            "btab": btab_v,
            "b2d": b2d_v,
            "ctab": ctab_v,
            "mtri4": mtri4_v,
            "mtri": mtri_v,
            "ident": ident_v,
        })

    res = run_bass_kernel_spmd(nc, in_maps, core_ids=list(range(8)))
    out = np.empty((2, SBT, D), np.float32)
    for c in range(8):
        r, beta = c % 4, c // 4
        out[beta][:, 1024 * r:1024 * r + 1024] = res.results[c]["out_shard"]
    return out


# version 10
# speedup vs baseline: 1.1651x; 1.0276x over previous
import math
from contextlib import ExitStack

import numpy as np
import ml_dtypes

import concourse.bass as bass
import concourse.mybir as mybir
from concourse.bass_utils import run_bass_kernel_spmd

F32 = mybir.dt.float32
BF16 = mybir.dt.bfloat16
AF = mybir.ActivationFunctionType
ALU = mybir.AluOpType
AX = mybir.AxisListType.X

D = 4096
SBT = 4096
TOK = 1024
NDM = 32
NH = 8
NSW = 4
NTC = 16
TCW = 256
NG = 8
NG2 = 16
CEXP = 10.0
SCL = 1.0 / math.sqrt(128.0)
WCH = [64, 128, 256, 512, 512, 512, 512, 512]
MASKNEG = -1.0e9


def _nkt(g):
    return 4 * g + 4


BLOCKS_PER_HEAD = sum(_nkt(g) for g in range(NG))
BLK_BEFORE_G = [sum(_nkt(gg) for gg in range(g)) for g in range(NG)]


def _blk_idx(h, g, kt):
    return h * BLOCKS_PER_HEAD + BLK_BEFORE_G[g] + kt


def _qk_before_g(h, g):
    return h * (BLOCKS_PER_HEAD + NG) + BLK_BEFORE_G[g] + g


def _tab_meta():
    idx = {}
    for hl in range(NH):
        w = WCH[hl]
        for g in range(NG):
            for kt in range(_nkt(g)):
                for qc in range(512 // w):
                    q_hi = 512 * g + qc * w + w - 1
                    if q_hi < 128 * kt:
                        continue
                    key = (hl, 128 * kt - q_hi)
                    if key not in idx:
                        idx[key] = len(idx)
    return idx


TAB = _tab_meta()
NV = len(TAB)


def _exp_pieces(hl, g, kt):
    w = WCH[hl]
    j = kt - 4 * g
    diag = j >= 0
    pieces = []
    for qc in range(512 // w):
        ql, qh = qc * w, qc * w + w
        q_hi_g = 512 * g + qh - 1
        if q_hi_g < 128 * kt:
            continue
        col = TAB[(hl, 128 * kt - q_hi_g)]
        if diag:
            w_lo, w_hi = 128 * j, 128 * j + 128
            a, b = max(ql, w_lo), min(qh, w_hi)
            if a < b:
                pieces.append((a, b, "sm", col))
            if qh > w_hi:
                pieces.append((max(ql, w_hi), qh, "ps", col))
        else:
            pieces.append((ql, qh, "ps", col))
    return pieces


SQ128 = math.sqrt(128.0)


def _slopes_f():
    m = np.arange(1, 33, dtype=np.float32) * 0.25
    return (1.0 / np.power(np.float32(2.0), m)).astype(np.float32)


def _live(hl, g, kt):
    sl = float(_slopes_f()[4 * hl + 3])
    w = WCH[hl]
    bias_max = sl * (128 * kt + 127 - (512 * g + w - 1))
    return kt >= 4 * g or bias_max > -80.0


LIVE = {(hl, g): [kt for kt in range(_nkt(g)) if _live(hl, g, kt)]
        for hl in range(NH) for g in range(NG)}
HB2D = 3

EXPC = {}
SMC = {}
QKC = {}
MSKC = {}
AV_AFTER_HEAD = [0] * (NH + 1)
_exp = _sm = _msk = 0
_qk = 0
for _h in range(NH):
    for _g in range(NG):
        QKC[(_h, _g)] = _qk
        lv = LIVE[(_h, _g)]
        for _kt in lv:
            _exp += 1
            EXPC[(_h, _g, _kt)] = _exp
            if _h < HB2D:
                _sm += 1
                SMC[(_h, _g, _kt)] = _sm
            elif _kt >= 4 * _g:
                _msk += 1
                MSKC[(_h, _g, _kt)] = _msk
        _qk += len(lv) + 1
    AV_AFTER_HEAD[_h + 1] = 2 * _exp
T_EXP_ALL = _exp

FIS = [(0, 0), (0, 1), (1, 0), (1, 1)]
YIS = [(oc2, tt2, l) for oc2 in range(2) for tt2 in range(2) for l in range(4)]


def _cm_h0(g2, k):
    return 80 * g2 + 8 * (k + 1)


def _cm_h1(g2, k):
    return 80 * g2 + 32 + 8 * (k + 1)


def _cm_y(g2, yi):
    return 80 * g2 + 64 + yi + 1


def _ca_F(g2, fo):
    return 20 * g2 + fo + 1


def _ca_y(g2, yi):
    return 20 * g2 + 4 + yi + 1


def _cd_base(g2):
    return 1 + 40 * g2


def _cd_bias(g2, fo):
    return _cd_base(g2) + 2 * fo + 1


def _cd_xc(g2, fo):
    return _cd_base(g2) + 2 * fo + 2


def _cd_mul(g2, yi):
    return _cd_base(g2) + 8 + 2 * yi + 1


def _cd_add(g2, yi):
    return _cd_base(g2) + 8 + 2 * yi + 2


def _cd_final(g2, fo):
    tt2, oc2 = FIS[fo]
    return _cd_add(g2, 8 * oc2 + 4 * tt2 + 3)


CTI = {}
for _h in range(HB2D):
    for _g in range(NG):
        for _kt in LIVE[(_h, _g)]:
            _d = _kt - 4 * _g
            if (_h, _d) not in CTI:
                CTI[(_h, _d)] = len(CTI)
NCT = len(CTI)

LDC = {"mtri": 16, "btab": 32, "ident": 48, "bqkvqk": 64, "vbias": 80,
       "b2d": 96, "mtri4": 112, "ctab": 128, "bias_bc": 144, "boutbc": 176}



def build_nc():
    nc = bass.Bass("TRN2", target_bir_lowering=False)

    ap = lambda t: t.ap()
    x_full = ap(nc.dram_tensor("x_full", [SBT, D], F32, kind="ExternalInput"))
    x_cols = ap(nc.dram_tensor("x_cols", [SBT, TOK], F32, kind="ExternalInput"))
    wqkvT = ap(nc.dram_tensor("wqkvT", [D, 3072], F32, kind="ExternalInput"))
    bqkv_qk = ap(nc.dram_tensor("bqkv_qk", [128, 16], F32, kind="ExternalInput"))
    bqkv_vb = ap(nc.dram_tensor("bqkv_vb", [128, 1024], BF16, kind="ExternalInput"))
    woutT = ap(nc.dram_tensor("woutT", [D, 1024], F32, kind="ExternalInput"))
    w0T = ap(nc.dram_tensor("w0T", [D, 1024], F32, kind="ExternalInput"))
    b0bc = ap(nc.dram_tensor("b0bc", [128, 1024], BF16, kind="ExternalInput"))
    boutbc = ap(nc.dram_tensor("boutbc", [128, 1024], F32, kind="ExternalInput"))
    aeff = ap(nc.dram_tensor("aeff", [D, 64], F32, kind="ExternalInput"))
    beff = ap(nc.dram_tensor("beff", [4, 16, 1024], F32, kind="ExternalInput"))
    masks = ap(nc.dram_tensor("masks", [4, SBT, 1024], BF16, kind="ExternalInput"))
    btab = ap(nc.dram_tensor("btab", [128, NV], F32, kind="ExternalInput"))
    mtri = ap(nc.dram_tensor("mtri", [128, 128], F32, kind="ExternalInput"))
    b2d = ap(nc.dram_tensor("b2d", [128, HB2D * 512], F32, kind="ExternalInput"))
    mtri4 = ap(nc.dram_tensor("mtri4", [128, 2048], F32, kind="ExternalInput"))
    ctab = ap(nc.dram_tensor("ctab", [128, NCT], F32, kind="ExternalInput"))
    ident = ap(nc.dram_tensor("ident", [128, 128], BF16, kind="ExternalInput"))
    out_sh = ap(nc.dram_tensor("out_shard", [SBT, TOK], F32, kind="ExternalOutput"))

    xn_sp = ap(nc.dram_tensor("xn_sp", [D, SBT], BF16, kind="Internal"))
    cc2_in = nc.dram_tensor("cc2_in", [TOK, SBT], BF16, kind="Internal")
    cc2o = [nc.dram_tensor(f"cc2o{h}", [512, SBT], BF16, kind="Internal") for h in range(NH)]
    qk_sp = ap(nc.dram_tensor("qk_sp", [2048, SBT], BF16, kind="Internal"))
    v_sp = ap(nc.dram_tensor("v_sp", [SBT, 1024], BF16, kind="Internal"))
    masks_r = masks.rearrange("l s c -> (l s) c")
    beff_r = beff.rearrange("l r c -> (l r) c")

    RG = [[0, 1, 2, 3], [4, 5, 6, 7]]
    SP_ALL = 16 * 2 * NSW * NTC

    es = ExitStack()
    sem = {}
    for name in ["ld_x0", "ld_x1", "ld_c", "ld_w0", "ld_w1", "ld_xn0", "ld_xn1",
                 "ld_kv0", "ld_kv1", "ld_qt0", "ld_qt1", "ld_cw",
                 "ld_ao0", "ld_ao1", "ld_ao2", "ld_ao3",
                 "ld_xnm0", "ld_xnm1", "ld_xnm2", "ld_xnm3",
                 "ld_mk0", "ld_mk1", "ld_mk2", "ld_mk3",
                 "ld_mk4", "ld_mk5", "ld_mk6", "ld_mk7",
                 "st_out0", "st_out1", "st_out2", "st_out3", "ld_xc", "ld_hl",
                 "s_lnA", "s_lnV", "s_tp", "s_ev", "st_xn",
                 "p_mm", "e_qk", "e_v", "st_sp",
                 "t_qk", "t_exp", "t_msk", "t_av", "t_dve", "t_sm", "t_ss", "st_cc2", "cc2s", "c_act", "c_hev", "c_fo",
                 "c_mm", "c_dve", "st_out"]:
        sem[name] = es.enter_context(nc.semaphore(name))

    mtri_sb = es.enter_context(nc.sbuf_tensor("mtri_sb", [128, 128], F32))
    btab_sb = es.enter_context(nc.sbuf_tensor("btab_sb", [128, NV], F32))
    ident_sb = es.enter_context(nc.sbuf_tensor("ident_sb", [128, 128], BF16))
    bqkvqk_sb = es.enter_context(nc.sbuf_tensor("bqkvqk_sb", [128, 16], F32))

    with es, nc.Block() as block:
        with ExitStack() as p01:
            xa = p01.enter_context(nc.sbuf_tensor("xa", [128, 2 * D], F32))
            xsq = p01.enter_context(nc.sbuf_tensor("xsq", [128, 2048], F32))
            xnbf = p01.enter_context(nc.sbuf_tensor("xnbf", [128, 3 * 2048], BF16))
            xnT = p01.enter_context(nc.sbuf_tensor("xnT", [128, NDM * 256], BF16))
            lnv_ = p01.enter_context(nc.sbuf_tensor("lnv", [128, 12], F32))
            pt0 = p01.enter_context(nc.psum_tensor("pt0", [128, 512], BF16))
            pt1 = p01.enter_context(nc.psum_tensor("pt1", [128, 512], BF16))
            ptb = [pt0, pt1]
            wsw = p01.enter_context(nc.sbuf_tensor("wsw", [128, 2 * NDM * 768], BF16))
            xnsb = p01.enter_context(nc.sbuf_tensor("xnsb", [128, 2 * NDM * TCW], BF16))
            vbias = p01.enter_context(nc.sbuf_tensor("vbias", [128, 1024], BF16))
            qkev = p01.enter_context(nc.sbuf_tensor("qkev", [128, 4 * TCW], BF16))
            vev = p01.enter_context(nc.sbuf_tensor("vev", [128, 2 * TCW], BF16))
            qkp = [p01.enter_context(nc.psum_tensor(f"qkp{i}", [128, TCW], F32)) for i in range(2)]
            vp = [p01.enter_context(nc.psum_tensor(f"vp{i}", [128, TCW], F32)) for i in range(2)]
            COL = {k: i for i, k in enumerate(
                ["rs", "rssa", "rssb", "rss", "mean", "ms", "var", "lnvv", "rstd", "nb", "eps"])}

            def lv(k):
                return lnv_[:, COL[k]:COL[k] + 1]

            def wb_xn(eng, j):
                eng.dma_start(
                    out=xn_sp[:, 256 * j:256 * j + 256].rearrange("(dm p) c -> p dm c", p=128),
                    in_=xnT.ap().rearrange("p (dm c) -> p dm c", c=256)).then_inc(sem["st_xn"], 16)

            @block.sync
            def _(s):
                for j in range(16):
                    for t2 in range(2):
                        i = 2 * j + t2
                        if i >= 2:
                            s.wait_ge(sem["s_lnA"], 6 * i - 6)
                        s.dma_start(out=xa[:, (i % 2) * D:(i % 2) * D + D],
                                    in_=x_full[128 * i:128 * i + 128, :]).then_inc(sem[f"ld_x{i % 2}"], 16)
                    if j >= 1:
                        s.wait_ge(sem["s_ev"], 16 * j)
                        wb_xn(s, j - 1)
                s.wait_ge(sem["s_ev"], 16 * 16)
                wb_xn(s, 15)
                s.dma_start(out=vbias[:, :], in_=bqkv_vb).then_inc(sem["ld_c"], 16)
                s.wait_ge(sem["st_xn"], 16 * 16)
                for gi in range(NSW * NTC):
                    sw, tc = divmod(gi, NTC)
                    if gi >= 2:
                        s.wait_ge(sem["p_mm"], 6 * (gi - 1))
                    for q4 in range(4):
                        src = xn_sp[1024 * q4:1024 * q4 + 1024, 256 * tc:256 * tc + 256] \
                            .rearrange("(dd p) c -> p dd c", p=128)
                        dst = xnsb.ap().rearrange("p (dm c) -> p dm c", c=TCW)[
                            :, (gi % 2) * NDM + 8 * q4:(gi % 2) * NDM + 8 * q4 + 8, :]
                        s.dma_start(out=dst, in_=src).then_inc(sem[f"ld_xn{gi % 2}"], 16)

            @block.scalar
            def _(sc):
                sc.dma_start(out=mtri_sb[:, :], in_=mtri).then_inc(sem["ld_c"], 16)
                sc.dma_start(out=btab_sb[:, :], in_=btab).then_inc(sem["ld_c"], 16)
                sc.dma_start(out=ident_sb[:, :], in_=ident).then_inc(sem["ld_c"], 16)
                sc.dma_start(out=bqkvqk_sb[:, :], in_=bqkv_qk).then_inc(sem["ld_c"], 16)
                for i in range(32):
                    xi = xa[:, (i % 2) * D:(i % 2) * D + D]
                    sa = ((2 * i) % 3) * 2048
                    sb = ((2 * i + 1) % 3) * 2048
                    sc.wait_ge(sem[f"ld_x{i % 2}"], 16 * (i // 2 + 1))
                    if i >= 1:
                        sc.wait_ge(sem["s_lnV"], 8 * i - 5)
                    sc.activation(xsq[:, :], xi[:, 0:2048], AF.Square).then_inc(sem["s_lnA"], 1)
                    sc.wait_ge(sem["s_lnV"], 8 * i + 2)
                    sc.activation(xsq[:, :], xi[:, 2048:4096], AF.Square).then_inc(sem["s_lnA"], 1)
                    sc.wait_ge(sem["s_lnV"], 8 * i + 7)
                    sc.activation(lv("lnvv"), lv("var"), AF.Ln, bias=lv("eps")).then_inc(sem["s_lnA"], 1)
                    sc.wait_ge(sem["s_lnA"], 6 * i + 3)
                    sc.activation(lv("rstd"), lv("lnvv"), AF.Exp, scale=-0.5).then_inc(sem["s_lnA"], 1)
                    sc.wait_ge(sem["s_lnV"], 8 * i + 8)
                    if i >= 2:
                        sc.wait_ge(sem["s_tp"], 8 * i - 8)
                    sc.activation(xnbf[:, sa:sa + 2048], xi[:, 0:2048], AF.Identity,
                                  bias=lv("nb"), scale=lv("rstd")).then_inc(sem["s_lnA"], 1)
                    if i >= 1:
                        sc.wait_ge(sem["s_tp"], 8 * i - 4)
                    sc.activation(xnbf[:, sb:sb + 2048], xi[:, 2048:4096], AF.Identity,
                                  bias=lv("nb"), scale=lv("rstd")).then_inc(sem["s_lnA"], 1)

            @block.vector
            def _(v):
                def evac_tile(v, i):
                    if i >= 2 and i % 2 == 0:
                        v.wait_ge(sem["st_xn"], 16 * (i // 2))
                    for b in range(8):
                        gi = 8 * i + b
                        v.wait_ge(sem["s_tp"], gi + 1)
                        dst = xnT.ap().rearrange("p (dm c) -> p dm c", c=256)[
                            :, 4 * b:4 * b + 4, 128 * (i % 2):128 * (i % 2) + 128]
                        src = ptb[gi % 2].ap().rearrange("p (j c) -> p j c", c=128)
                        v.tensor_copy(dst, src).then_inc(sem["s_ev"], 1)

                v.memset(lv("eps"), 1e-5)
                for i in range(32):
                    xi = xa[:, (i % 2) * D:(i % 2) * D + D]
                    v.wait_ge(sem[f"ld_x{i % 2}"], 16 * (i // 2 + 1))
                    v.reduce_sum(lv("rs"), xi, axis=AX).then_inc(sem["s_lnV"], 1)
                    v.wait_ge(sem["s_lnA"], 6 * i + 1)
                    v.reduce_sum(lv("rssa"), xsq[:, :], axis=AX).then_inc(sem["s_lnV"], 1)
                    v.wait_ge(sem["s_lnA"], 6 * i + 2)
                    v.reduce_sum(lv("rssb"), xsq[:, :], axis=AX).then_inc(sem["s_lnV"], 1)
                    v.wait_ge(sem["s_lnV"], 8 * i + 3)
                    v.tensor_add(lv("rss"), lv("rssa"), lv("rssb")).then_inc(sem["s_lnV"], 1)
                    v.wait_ge(sem["s_lnV"], 8 * i + 1)
                    v.tensor_scalar_mul(lv("mean"), lv("rs"), 1.0 / D).then_inc(sem["s_lnV"], 1)
                    v.wait_ge(sem["s_lnV"], 8 * i + 5)
                    v.tensor_mul(lv("ms"), lv("mean"), lv("mean")).then_inc(sem["s_lnV"], 1)
                    v.wait_ge(sem["s_lnV"], 8 * i + 6)
                    if i >= 1:
                        v.wait_ge(sem["s_lnA"], 6 * i - 3)
                    v.scalar_tensor_tensor(lv("var"), lv("rss"), 1.0 / D, lv("ms"),
                                           ALU.mult, ALU.subtract).then_inc(sem["s_lnV"], 1)
                    v.wait_ge(sem["s_lnA"], 6 * i + 4)
                    v.scalar_tensor_tensor(lv("nb"), lv("mean"), -1.0, lv("rstd"),
                                           ALU.mult, ALU.mult).then_inc(sem["s_lnV"], 1)
                    if i >= 1:
                        evac_tile(v, i - 1)
                evac_tile(v, 31)

            @block.tensor
            def _(t):
                t.wait_ge(sem["ld_c"], 16 * 4)
                for i in range(32):
                    sa = ((2 * i) % 3) * 2048
                    sb = ((2 * i + 1) % 3) * 2048
                    for b in range(8):
                        gi = 8 * i + b
                        t.wait_ge(sem["s_lnA"], 6 * i + (5 if b < 4 else 6))
                        if gi >= 2:
                            t.wait_ge(sem["s_ev"], gi - 1)
                        last = None
                        for j in range(4):
                            dmc = 4 * b + j
                            off = (sa + 128 * dmc) if b < 4 else (sb + 128 * (dmc - 16))
                            last = t.transpose(
                                ptb[gi % 2][:, 128 * j:128 * j + 128],
                                xnbf[:, off:off + 128],
                                ident_sb[:, :])
                        last.then_inc(sem["s_tp"], 1)


            @block.gpsimd
            def _(g):
                def load_sweep(s):
                    if s >= 2:
                        g.wait_ge(sem["p_mm"], 6 * NTC * (s - 1))
                    for q4 in range(4):
                        src = wqkvT[128 * 8 * q4:128 * 8 * (q4 + 1), 768 * s:768 * s + 768] \
                            .rearrange("(dd p) c -> p dd c", p=128)
                        dst = wsw.ap().rearrange("p (dm c) -> p dm c", c=768)[
                            :, (s % 2) * NDM + 8 * q4:(s % 2) * NDM + 8 * q4 + 8, :]
                        g.dma_start(out=dst, in_=src).then_inc(sem[f"ld_w{s % 2}"], 16)

                load_sweep(0)
                load_sweep(1)
                for gi in range(NSW * NTC):
                    if gi % NTC == 0 and 2 <= gi // NTC + 1 < NSW:
                        load_sweep(gi // NTC + 1)
                    s, tc = divmod(gi, NTC)
                    g.wait_ge(sem["e_qk"], 4 * (gi + 1))
                    dstq = qk_sp[512 * s:512 * s + 512, TCW * tc:TCW * tc + TCW] \
                        .rearrange("(oc p) c -> p oc c", p=128)
                    g.dma_start(out=dstq, in_=qkev.ap().rearrange("p (oc c) -> p oc c", c=TCW)) \
                        .then_inc(sem["st_sp"], 16)
                    g.wait_ge(sem["e_v"], 2 * (gi + 1))
                    dstv = v_sp[TCW * tc:TCW * tc + TCW, 256 * s:256 * s + 256] \
                        .rearrange("(tt p) c -> p tt c", p=128)
                    g.dma_start(out=dstv, in_=vev.ap().rearrange("p (tt c) -> p tt c", c=TCW)) \
                        .then_inc(sem["st_sp"], 16)

            @block.tensor
            def _(t):
                qk_guard = [0, 0]
                v_guard = [0, 0]
                for gi in range(NSW * NTC):
                    s, tc = divmod(gi, NTC)
                    if tc == 0:
                        t.wait_ge(sem[f"ld_w{s % 2}"], 16 * 4 * (s // 2 + 1))
                    t.wait_ge(sem[f"ld_xn{gi % 2}"], 16 * 4 * (gi // 2 + 1))
                    wof = (s % 2) * NDM * 768
                    xof = (gi % 2) * NDM * TCW
                    for oc in range(4):
                        if qk_guard[oc % 2]:
                            t.wait_ge(sem["e_qk"], qk_guard[oc % 2])
                        last = None
                        for dm in range(NDM):
                            last = t.matmul(qkp[oc % 2][:, :],
                                            wsw[:, wof + 768 * dm + 128 * oc:wof + 768 * dm + 128 * oc + 128],
                                            xnsb[:, xof + TCW * dm:xof + TCW * dm + TCW],
                                            start=(dm == 0), stop=(dm == NDM - 1))
                        last.then_inc(sem["p_mm"], 1)
                        qk_guard[oc % 2] = 4 * gi + oc + 1
                    for tt in range(2):
                        if v_guard[tt % 2]:
                            t.wait_ge(sem["e_v"], v_guard[tt % 2])
                        last = None
                        for dm in range(NDM):
                            last = t.matmul(vp[tt % 2][:, :],
                                            xnsb[:, xof + TCW * dm + 128 * tt:xof + TCW * dm + 128 * tt + 128],
                                            wsw[:, wof + 768 * dm + 512:wof + 768 * dm + 768],
                                            start=(dm == 0), stop=(dm == NDM - 1))
                        last.then_inc(sem["p_mm"], 1)
                        v_guard[tt % 2] = 2 * gi + tt + 1

            @block.scalar
            def _(sc):
                sc.wait_ge(sem["ld_c"], 16 * 4)
                for gi in range(NSW * NTC):
                    s, tc = divmod(gi, NTC)
                    for oc in range(4):
                        sc.wait_ge(sem["p_mm"], 6 * gi + oc + 1)
                        if gi >= 1:
                            sc.wait_ge(sem["st_sp"], 16 * 2 * gi)
                        sc.activation(qkev[:, TCW * oc:TCW * oc + TCW], qkp[oc % 2][:, :],
                                      AF.Identity, bias=bqkvqk_sb[:, 4 * s + oc:4 * s + oc + 1]) \
                            .then_inc(sem["e_qk"], 1)

            @block.vector
            def _(v):
                v.wait_ge(sem["ld_c"], 16 * 5)
                for gi in range(NSW * NTC):
                    s, tc = divmod(gi, NTC)
                    for tt in range(2):
                        v.wait_ge(sem["p_mm"], 6 * gi + 4 + tt + 1)
                        if gi >= 1:
                            v.wait_ge(sem["st_sp"], 16 * 2 * gi)
                        v.tensor_add(vev[:, TCW * tt:TCW * tt + TCW], vp[tt % 2][:, :],
                                     vbias[:, 256 * s:256 * s + 256]).then_inc(sem["e_v"], 1)

        with ExitStack() as p2:
            ktsb = p2.enter_context(nc.sbuf_tensor("ktsb", [128, 2 * SBT], BF16))
            vsb = p2.enter_context(nc.sbuf_tensor("vsb", [128, 2 * SBT], BF16))
            qtsb = p2.enter_context(nc.sbuf_tensor("qtsb", [128, 2 * 512], BF16))
            etc_l = [p2.enter_context(nc.sbuf_tensor(f"etc{j}", [128, 512], BF16)) for j in range(3)]
            etd_l = [p2.enter_context(nc.sbuf_tensor(f"etd{j}", [128, 512], BF16)) for j in range(4)]
            etn_l = [p2.enter_context(nc.sbuf_tensor(f"etn{j}", [128, 512], BF16)) for j in range(3)]
            smb = [p2.enter_context(nc.sbuf_tensor(f"sm{j}", [128, 128], F32)) for j in range(2)]
            smn = [p2.enter_context(nc.sbuf_tensor(f"smn{j}", [128, 512], F32)) for j in range(3)]
            b2d_sb = p2.enter_context(nc.sbuf_tensor("b2d_sb", [128, HB2D * 512], F32))
            ctab_sb = p2.enter_context(nc.sbuf_tensor("ctab_sb", [128, NCT], F32))
            mtri4_sb = p2.enter_context(nc.sbuf_tensor("mtri4_sb", [128, 2048], F32))
            ones_c = p2.enter_context(nc.sbuf_tensor("ones_c", [128, 1], BF16))
            ones_r = p2.enter_context(nc.sbuf_tensor("ones_r", [1, 128], BF16))
            rinv_b = p2.enter_context(nc.sbuf_tensor("rinv_b", [1, 512], BF16))
            rbsb = p2.enter_context(nc.sbuf_tensor("rbsb", [128, 512], F32))
            osb = p2.enter_context(nc.sbuf_tensor("osb", [128, 2 * 512], BF16))
            stp_ = [p2.enter_context(nc.psum_tensor(f"stp{i}", [128, 512], F32)) for i in range(3)]
            otp = [p2.enter_context(nc.psum_tensor(f"otp{i}", [128, 512], F32)) for i in range(2)]
            lpb = [p2.enter_context(nc.psum_tensor(f"lp{i}", [1, 512], F32)) for i in range(2)]
            rbp = p2.enter_context(nc.psum_tensor("rbp", [128, 512], F32))

            def etbuf(h, g, kt):
                j = kt - 4 * g
                if h < HB2D:
                    return etn_l[kt % 3]
                return etd_l[j] if j >= 0 else etc_l[kt % 3]

            @block.sync
            def _(s_):
                s_.wait_ge(sem["st_sp"], SP_ALL)
                s_.dma_start(out=b2d_sb[:, :], in_=b2d).then_inc(sem["ld_c"], 16)
                s_.dma_start(out=mtri4_sb[:, :], in_=mtri4).then_inc(sem["ld_c"], 16)
                s_.dma_start(out=ctab_sb[:, :], in_=ctab).then_inc(sem["ld_c"], 16)
                for h in range(NH):
                    if h >= 2:
                        s_.wait_ge(sem["t_av"], AV_AFTER_HEAD[h - 1])
                    s_.dma_start(out=ktsb[:, (h % 2) * SBT:(h % 2) * SBT + SBT],
                                 in_=qk_sp[256 * h + 128:256 * h + 256, :]).then_inc(sem[f"ld_kv{h % 2}"], 16)
                    for q4 in range(4):
                        src = v_sp[1024 * q4:1024 * (q4 + 1), 128 * h:128 * h + 128] \
                            .rearrange("(kt p) c -> p kt c", p=128)
                        dst = vsb.ap().rearrange("p (kt c) -> p kt c", c=128)[
                            :, (h % 2) * NDM + 8 * q4:(h % 2) * NDM + 8 * q4 + 8, :]
                        s_.dma_start(out=dst, in_=src).then_inc(sem[f"ld_kv{h % 2}"], 16)
                    for g in range(NG):
                        hg = h * NG + g
                        if hg >= 2:
                            h2, g2_ = divmod(hg - 2, NG)
                            s_.wait_ge(sem["t_qk"], QKC[(h2, g2_)] + len(LIVE[(h2, g2_)]))
                        s_.dma_start(out=qtsb[:, (g % 2) * 512:(g % 2) * 512 + 512],
                                     in_=qk_sp[256 * h:256 * h + 128, 512 * g:512 * g + 512]) \
                            .then_inc(sem[f"ld_qt{g % 2}"], 16)

            @block.tensor
            def _(t):
                t.wait_ge(sem["st_sp"], SP_ALL)
                t.wait_ge(sem["t_dve"], 1)
                bank_guard = [None, None, None]
                for h in range(NH):
                    t.wait_ge(sem[f"ld_kv{h % 2}"], 16 * 5 * (h // 2 + 1))
                    kof = (h % 2) * SBT
                    vof = (h % 2) * SBT
                    for g in range(NG):
                        hg = h * NG + g
                        lv = LIVE[(h, g)]
                        qb = QKC[(h, g)]
                        t.wait_ge(sem[f"ld_qt{g % 2}"], 16 * (h * 4 + g // 2 + 1))
                        qt_ap = qtsb[:, (g % 2) * 512:(g % 2) * 512 + 512]

                        def st_mm(i, h=h, g=g, qt_ap=qt_ap, qb=qb, lv=lv):
                            kt = lv[i]
                            if bank_guard[kt % 3] is not None:
                                sname, v = bank_guard[kt % 3]
                                t.wait_ge(sem[sname], v)
                            t.matmul(stp_[kt % 3][:, :],
                                     ktsb[:, kof + 128 * kt:kof + 128 * kt + 128],
                                     qt_ap, start=True, stop=True).then_inc(sem["t_qk"], 1)
                            if h < HB2D:
                                bank_guard[kt % 3] = ("t_sm", SMC[(h, g, kt)])
                            else:
                                bank_guard[kt % 3] = ("t_exp", EXPC[(h, g, kt)])

                        for ii in range(min(3, len(lv))):
                            st_mm(ii)
                        for i, kt in enumerate(lv):
                            t.wait_ge(sem["t_exp"], EXPC[(h, g, kt)])
                            if i == 0 and hg >= 2:
                                t.wait_ge(sem["t_dve"], 3 * hg - 2)
                            eb = etbuf(h, g, kt)
                            t.matmul(otp[hg % 2][:, :], vsb[:, vof + 128 * kt:vof + 128 * kt + 128],
                                     eb[:, :], start=(i == 0), stop=(i == len(lv) - 1),
                                     skip_group_check=True).then_inc(sem["t_av"], 1)
                            t.matmul(lpb[hg % 2][:, :], ones_c[:, :], eb[:, :],
                                     start=(i == 0), stop=(i == len(lv) - 1),
                                     skip_group_check=True).then_inc(sem["t_av"], 1)
                            if i + 3 < len(lv):
                                st_mm(i + 3)
                        t.wait_ge(sem["t_dve"], 3 * hg + 2)
                        t.matmul(rbp[:, :], ones_r[:, :], rinv_b[:, :],
                                 start=True, stop=True).then_inc(sem["t_qk"], 1)

            @block.vector
            def _(v):
                v.wait_ge(sem["st_sp"], SP_ALL)
                v.wait_ge(sem["ld_c"], 16 * 8)
                v.memset(ones_c[:, :], 1.0)
                v.memset(ones_r[:, :], 1.0)
                v.memset(etd_l[1][:, 0:128], 0.0)
                v.memset(etd_l[2][:, 0:256], 0.0)
                v.memset(etd_l[3][:, 0:384], 0.0).then_inc(sem["t_dve"], 1)
                dve_cum = 1
                ss_cum = 0
                sm_guard = [0, 0]
                smn_guard = [0, 0, 0]
                for h in range(NH):
                    for g in range(NG):
                        hg = h * NG + g
                        lv = LIVE[(h, g)]
                        qb = QKC[(h, g)]
                        if h < HB2D:
                            for i, kt in enumerate(lv):
                                j = kt - 4 * g
                                v.wait_ge(sem["t_qk"], qb + i + 1)
                                if smn_guard[kt % 3]:
                                    v.wait_ge(sem["t_exp"], smn_guard[kt % 3])
                                ci = CTI[(h, kt - 4 * g)]
                                last = v.scalar_tensor_tensor(
                                    smn[kt % 3][:, :], stp_[kt % 3][:, :],
                                    ctab_sb[:, ci:ci + 1],
                                    b2d_sb[:, 512 * h:512 * h + 512], ALU.add, ALU.add)
                                if j >= 0:
                                    last.then_inc(sem["t_ss"], 1)
                                    ss_cum += 1
                                    v.wait_ge(sem["t_ss"], ss_cum)
                                    last = v.tensor_add(
                                        smn[kt % 3][:, 0:128 * (j + 1)],
                                        smn[kt % 3][:, 0:128 * (j + 1)],
                                        mtri4_sb[:, 512 * j:512 * j + 128 * (j + 1)])
                                last.then_inc(sem["t_sm"], 1)
                                smn_guard[kt % 3] = EXPC[(h, g, kt)]
                        else:
                            for kt in range(4 * g, 4 * g + 4):
                                j = kt - 4 * g
                                i = lv.index(kt)
                                v.wait_ge(sem["t_qk"], qb + i + 1)
                                if sm_guard[j % 2]:
                                    v.wait_ge(sem["t_exp"], sm_guard[j % 2])
                                v.tensor_add(smb[j % 2][:, :], stp_[kt % 3][:, 128 * j:128 * j + 128],
                                             mtri_sb[:, :]).then_inc(sem["t_msk"], 1)
                                sm_guard[j % 2] = EXPC[(h, g, kt)]
                        v.wait_ge(sem["t_av"], 2 * EXPC[(h, g, lv[-1])])
                        v.tensor_copy(rinv_b[:, :], lpb[hg % 2][:, :]).then_inc(sem["t_dve"], 1)
                        dve_cum += 1
                        v.wait_ge(sem["t_qk"], qb + len(lv) + 1)
                        v.reciprocal(rbsb[:, :], rbp[:, :]).then_inc(sem["t_dve"], 1)
                        dve_cum += 1
                        v.wait_ge(sem["t_dve"], dve_cum)
                        if hg >= 2:
                            v.wait_ge(sem["st_cc2"], 16 * hg)
                        v.tensor_mul(osb[:, (hg % 2) * 512:(hg % 2) * 512 + 512], otp[hg % 2][:, :],
                                     rbsb[:, :]).then_inc(sem["t_dve"], 1)
                        dve_cum += 1

            @block.scalar
            def _(sc):
                sc.wait_ge(sem["ld_c"], 16 * 4)
                sc.wait_ge(sem["t_dve"], 1)
                etc_guard = [0, 0, 0]
                etd_guard = [0, 0, 0, 0]
                etn_guard = [0, 0, 0]
                for h in range(NH):
                    for g in range(NG):
                        hg = h * NG + g
                        lv = LIVE[(h, g)]
                        qb = QKC[(h, g)]
                        for i, kt in enumerate(lv):
                            j = kt - 4 * g
                            if h < HB2D:
                                sc.wait_ge(sem["t_sm"], SMC[(h, g, kt)])
                                if etn_guard[kt % 3]:
                                    sc.wait_ge(sem["t_av"], etn_guard[kt % 3])
                                dest = etn_l[kt % 3]
                                etn_guard[kt % 3] = 2 * EXPC[(h, g, kt)]
                                sc.activation(dest[:, :], smn[kt % 3][:, :], AF.Exp,
                                              scale=SCL).then_inc(sem["t_exp"], 1)
                                continue
                            sc.wait_ge(sem["t_qk"], qb + i + 1)
                            if j >= 0:
                                sc.wait_ge(sem["t_msk"], MSKC[(h, g, kt)])
                                if etd_guard[j]:
                                    sc.wait_ge(sem["t_av"], etd_guard[j])
                                dest = etd_l[j]
                                etd_guard[j] = 2 * EXPC[(h, g, kt)]
                            else:
                                if etc_guard[kt % 3]:
                                    sc.wait_ge(sem["t_av"], etc_guard[kt % 3])
                                dest = etc_l[kt % 3]
                                etc_guard[kt % 3] = 2 * EXPC[(h, g, kt)]
                            last = None
                            for (a, b, srck, col) in _exp_pieces(h, g, kt):
                                if srck == "sm":
                                    srcp = smb[j % 2][:, a - 128 * j:b - 128 * j]
                                else:
                                    srcp = stp_[kt % 3][:, a:b]
                                last = sc.activation(dest[:, a:b], srcp, AF.Exp,
                                                     bias=btab_sb[:, col:col + 1], scale=SCL)
                            last.then_inc(sem["t_exp"], 1)

            @block.gpsimd
            def _(g_):
                for h in range(NH):
                    for g in range(NG):
                        hg = h * NG + g
                        g_.wait_ge(sem["t_dve"], 1 + 3 * (hg + 1))
                        g_.dma_start(out=cc2_in.ap()[128 * h:128 * h + 128, 512 * g:512 * g + 512],
                                     in_=osb[:, (hg % 2) * 512:(hg % 2) * 512 + 512]) \
                            .then_inc(sem["st_cc2"], 16)
                    g_.wait_ge(sem["st_cc2"], 16 * NG * (h + 1))
                    g_.collective_compute("AllGather", ALU.bypass, replica_groups=RG,
                                          ins=[cc2_in.ap()[128 * h:128 * h + 128, :]],
                                          outs=[cc2o[h].ap()]).then_inc(sem["cc2s"])
        with ExitStack() as p3:
            wout_sb = p3.enter_context(nc.sbuf_tensor("wout_sb", [128, NDM * 1024], BF16))
            w0_sb = p3.enter_context(nc.sbuf_tensor("w0_sb", [128, NDM * 1024], BF16))
            aeff_sb = p3.enter_context(nc.sbuf_tensor("aeff_sb", [128, NDM * 64], BF16))
            beff_sb = p3.enter_context(nc.sbuf_tensor("beff_sb", [16, 4 * 1024], BF16))
            bias_bc = p3.enter_context(nc.sbuf_tensor("bias_bc", [128, 1024], BF16))
            aorng = p3.enter_context(nc.sbuf_tensor("aorng", [128, 4 * 8 * TCW], BF16))
            xnrng = p3.enter_context(nc.sbuf_tensor("xnrng", [128, 4 * 8 * TCW], BF16))
            mkrng = p3.enter_context(nc.sbuf_tensor("mkrng", [128, 4 * 1024], BF16))
            xcs = p3.enter_context(nc.sbuf_tensor("xcs", [128, 2048], F32))
            hsb = p3.enter_context(nc.sbuf_tensor("hsb", [64, TCW], BF16))
            hlsb = p3.enter_context(nc.sbuf_tensor("hlsb", [16, 1024], BF16))
            ysb = p3.enter_context(nc.sbuf_tensor("ysb", [128, 4 * 512], BF16))
            fsb = [p3.enter_context(nc.sbuf_tensor(f"fsb{i}", [128, 512], F32)) for i in range(4)]
            tmp_sb = p3.enter_context(nc.sbuf_tensor("tmp_sb", [128, 512], BF16))
            hp = p3.enter_context(nc.psum_tensor("hp", [64, TCW], F32))
            Pp = [[p3.enter_context(nc.psum_tensor(f"Pp{t_}{o_}", [128, 512], F32))
                   for o_ in range(2)] for t_ in range(2)]
            yp = [p3.enter_context(nc.psum_tensor(f"yp{i}", [128, 512], F32)) for i in range(2)]

            @block.gpsimd
            def _(g_):
                g_.wait_ge(sem["t_av"], 2 * T_EXP_ALL)
                g_.wait_ge(sem["t_dve"], 1 + 3 * NH * NG)
                for (srcT, dstT, cw) in ((woutT, wout_sb, 1024), (w0T, w0_sb, 1024),
                                         (aeff, aeff_sb, 64)):
                    for q4 in range(4):
                        srcq = srcT[128 * 8 * q4:128 * 8 * (q4 + 1), :] \
                            .rearrange("(dd p) c -> p dd c", p=128)
                        dst = dstT.ap().rearrange("p (dm c) -> p dm c", c=cw)[
                            :, 8 * q4:8 * q4 + 8, :]
                        g_.dma_start(out=dst, in_=srcq).then_inc(sem["ld_cw"], 16)
                for l in range(4):
                    g_.dma_start(out=beff_sb[:, 1024 * l:1024 * l + 1024],
                                 in_=beff_r[16 * l:16 * l + 16, :]).then_inc(sem["ld_cw"], 16)
                mk_guard = [0, 0, 0, 0]
                for g2 in range(NG2):
                    g_.wait_ge(sem["c_hev"], g2 + 1)
                    if g2 >= 1:
                        g_.wait_ge(sem["c_mm"], 80 * g2)
                    for l in range(4):
                        g_.dma_start(out=hlsb[:, TCW * l:TCW * l + TCW],
                                     in_=hsb[16 * l:16 * l + 16, :]).then_inc(sem["ld_hl"], 16)
                    for oc2 in range(2):
                        for l in range(4):
                            if mk_guard[l]:
                                g_.wait_ge(sem["c_dve"], mk_guard[l])
                            srcm = masks_r[SBT * l + TCW * g2:SBT * l + TCW * g2 + TCW,
                                           512 * oc2:512 * oc2 + 512] \
                                .rearrange("(tt p) c -> p tt c", p=128)
                            dst = mkrng.ap()[:, 1024 * l:1024 * l + 1024] \
                                .rearrange("p (tt c) -> p tt c", c=512)
                            g_.dma_start(out=dst, in_=srcm).then_inc(sem[f"ld_mk{l}"], 16)
                            mk_guard[l] = _cd_mul(g2, 8 * oc2 + 4 + l)

            @block.sync
            def _(s_):
                s_.wait_ge(sem["cc2s"], 1)
                s_.dma_start(out=bias_bc[:, :], in_=b0bc).then_inc(sem["ld_c"], 16)
                s_.dma_start(out=fsb[0][:, :], in_=boutbc[:, 0:512]).then_inc(sem["ld_c"], 16)
                s_.dma_start(out=fsb[1][:, :], in_=boutbc[:, 512:1024]).then_inc(sem["ld_c"], 16)
                for k in range(4):
                    srcx = xn_sp[1024 * k:1024 * k + 1024, 0:TCW] \
                        .rearrange("(dd p) c -> p dd c", p=128)
                    dst = xnrng.ap().rearrange("p (dm c) -> p dm c", c=TCW)[
                        :, 8 * k:8 * k + 8, :]
                    s_.dma_start(out=dst, in_=srcx).then_inc(sem[f"ld_xnm{k}"], 16)
                for g2 in range(NG2):
                    if g2 >= 1:
                        for fo, (tt2, oc2) in enumerate(FIS):
                            s_.wait_ge(sem["c_dve"], _cd_final(g2 - 1, fo))
                            s_.dma_start(
                                out=out_sh[TCW * (g2 - 1) + 128 * tt2:TCW * (g2 - 1) + 128 * tt2 + 128,
                                           512 * oc2:512 * oc2 + 512],
                                in_=fsb[fo][:, :]).then_inc(sem[f"st_out{fo}"], 16)
                    if g2 < NG2 - 1:
                        for k in range(4):
                            u = 4 * (g2 + 1) + k
                            s_.wait_ge(sem["c_mm"], _cm_h1(g2, k))
                            srcx = xn_sp[1024 * k:1024 * k + 1024,
                                         TCW * (g2 + 1):TCW * (g2 + 1) + TCW] \
                                .rearrange("(dd p) c -> p dd c", p=128)
                            dst = xnrng.ap().rearrange("p (dm c) -> p dm c", c=TCW)[
                                :, 8 * k:8 * k + 8, :]
                            s_.dma_start(out=dst, in_=srcx).then_inc(sem[f"ld_xnm{k}"], 16)
                    if g2 >= 1:
                        s_.wait_ge(sem["c_dve"], _cd_xc(g2 - 1, 3))
                    srcx = x_cols[TCW * g2:TCW * g2 + TCW, :].rearrange("(tt p) c -> p tt c", p=128)
                    s_.dma_start(out=xcs.ap().rearrange("p (tt c) -> p tt c", c=1024),
                                 in_=srcx).then_inc(sem["ld_xc"], 16)
                g2 = NG2 - 1
                for fo, (tt2, oc2) in enumerate(FIS):
                    s_.wait_ge(sem["c_dve"], _cd_final(g2, fo))
                    s_.dma_start(out=out_sh[TCW * g2 + 128 * tt2:TCW * g2 + 128 * tt2 + 128,
                                            512 * oc2:512 * oc2 + 512],
                                 in_=fsb[fo][:, :]).then_inc(sem[f"st_out{fo}"], 16)
                for fo in range(4):
                    s_.wait_ge(sem[f"st_out{fo}"], 16 * NG2)

            @block.scalar
            def _(sc):
                sc.wait_ge(sem["cc2s"], 1)
                fs_guard = [0, 0, 0, 0]
                for k in range(4):
                    sc.wait_ge(sem["cc2s"], 2 * k + 2)
                    for hh in range(2):
                        srca = cc2o[2 * k + hh].ap()[:, 0:TCW] \
                            .rearrange("(dd p) c -> p dd c", p=128)
                        dst = aorng.ap().rearrange("p (dm c) -> p dm c", c=TCW)[
                            :, 8 * k + 4 * hh:8 * k + 4 * hh + 4, :]
                        sc.dma_start(out=dst, in_=srca).then_inc(sem[f"ld_ao{k}"], 16)
                for g2 in range(NG2):
                    sc.wait_ge(sem["c_mm"], 80 * g2 + 32)
                    if g2 >= 1:
                        sc.wait_ge(sem["ld_hl"], 16 * 4 * g2)
                    sc.copy(hsb[:, :], hp[:, :]).then_inc(sem["c_hev"], 1)
                    for fo in (0, 1):
                        tt2, oc2 = FIS[fo]
                        if g2 == 0:
                            sc.wait_ge(sem["c_dve"], 1)
                        if fs_guard[fo]:
                            sc.wait_ge(sem[f"st_out{fo}"], fs_guard[fo])
                        sc.copy(fsb[fo][:, :], Pp[tt2][oc2][:, :]).then_inc(sem["c_act"], 1)
                        fs_guard[fo] = 16 * (g2 + 1)
                    if g2 < NG2 - 1:
                        for k in range(4):
                            sc.wait_ge(sem["c_mm"], _cm_h1(g2, k))
                            for hh in range(2):
                                srca = cc2o[2 * k + hh].ap()[:, TCW * (g2 + 1):TCW * (g2 + 1) + TCW] \
                                    .rearrange("(dd p) c -> p dd c", p=128)
                                dst = aorng.ap().rearrange("p (dm c) -> p dm c", c=TCW)[
                                    :, 8 * k + 4 * hh:8 * k + 4 * hh + 4, :]
                                sc.dma_start(out=dst, in_=srca).then_inc(sem[f"ld_ao{k}"], 16)
                    else:
                        sc.wait_ge(sem["c_mm"], 80 * g2 + 64)
                    for fo in (2, 3):
                        tt2, oc2 = FIS[fo]
                        if fs_guard[fo]:
                            sc.wait_ge(sem[f"st_out{fo}"], fs_guard[fo])
                        sc.copy(fsb[fo][:, :], Pp[tt2][oc2][:, :]).then_inc(sem["c_act"], 1)
                        fs_guard[fo] = 16 * (g2 + 1)
                    for yi in range(16):
                        sc.wait_ge(sem["c_mm"], _cm_y(g2, yi))
                        if g2 > 0 or yi >= 4:
                            gy, yy = (g2, yi - 4) if yi >= 4 else (g2 - 1, 12 + yi)
                            sc.wait_ge(sem["c_dve"], _cd_mul(gy, yy))
                        sc.activation(ysb[:, 512 * (yi % 4):512 * (yi % 4) + 512], yp[yi % 2][:, :],
                                      AF.Identity, scale=8.0).then_inc(sem["c_act"], 1)

            @block.tensor
            def _(t):
                t.wait_ge(sem["cc2s"], 1)
                t.wait_ge(sem["ld_cw"], 16 * 16)
                for g2 in range(NG2):
                    for t2 in range(2):
                        for k in range(4):
                            t.wait_ge(sem[f"ld_xnm{k}"], 16 * (g2 + 1))
                            t.wait_ge(sem[f"ld_ao{k}"], 32 * (g2 + 1))
                            for dd in range(8):
                                dm = 8 * k + dd
                                if dm == 0 and g2 >= 1:
                                    if t2 == 0:
                                        t.wait_ge(sem["c_act"], _ca_F(g2 - 1, 1))
                                        t.wait_ge(sem["c_hev"], g2)
                                    else:
                                        t.wait_ge(sem["c_act"], _ca_F(g2 - 1, 3))
                                xof = k * 8 * TCW + TCW * dd
                                for oc2 in range(2):
                                    t.matmul(Pp[t2][oc2][:, :],
                                             aorng[:, xof + 128 * t2:xof + 128 * t2 + 128],
                                             wout_sb[:, 1024 * dm + 512 * oc2:1024 * dm + 512 * oc2 + 512],
                                             start=(dm == 0), stop=False,
                                             skip_group_check=True)
                                last = None
                                for oc2 in range(2):
                                    last = t.matmul(Pp[t2][oc2][:, :],
                                                    xnrng[:, xof + 128 * t2:xof + 128 * t2 + 128],
                                                    w0_sb[:, 1024 * dm + 512 * oc2:1024 * dm + 512 * oc2 + 512],
                                                    start=False, stop=(dm == NDM - 1),
                                                    skip_group_check=True)
                                if t2 == 0:
                                    last = t.matmul(hp[:, :], aeff_sb[:, 64 * dm:64 * dm + 64],
                                                    xnrng[:, xof:xof + TCW],
                                                    start=(dm == 0), stop=(dm == NDM - 1),
                                                    skip_group_check=True)
                                last.then_inc(sem["c_mm"], 1)
                    t.wait_ge(sem["ld_hl"], 16 * 4 * (g2 + 1))
                    for yi, (oc2, tt2, l) in enumerate(YIS):
                        if yi >= 2:
                            t.wait_ge(sem["c_act"], _ca_y(g2, yi - 2))
                        elif g2 >= 1:
                            t.wait_ge(sem["c_act"], _ca_y(g2 - 1, 14 + yi))
                        t.matmul(yp[yi % 2][:, :],
                                 hlsb[:, TCW * l + 128 * tt2:TCW * l + 128 * tt2 + 128],
                                 beff_sb[:, 1024 * l + 512 * oc2:1024 * l + 512 * oc2 + 512],
                                 start=True, stop=True).then_inc(sem["c_mm"], 1)

            @block.vector
            def _(v):
                v.wait_ge(sem["ld_c"], 16 * 11)
                v.tensor_add(bias_bc[:, 0:512], bias_bc[:, 0:512], fsb[0][:, :])
                v.tensor_add(bias_bc[:, 512:1024], bias_bc[:, 512:1024],
                             fsb[1][:, :]).then_inc(sem["c_dve"], 1)
                cd = 1
                for g2 in range(NG2):
                    for fo, (tt2, oc2) in enumerate(FIS):
                        v.wait_ge(sem["c_act"], _ca_F(g2, fo))
                        v.wait_ge(sem["c_dve"], cd)
                        v.tensor_add(fsb[fo][:, :], fsb[fo][:, :],
                                     bias_bc[:, 512 * oc2:512 * oc2 + 512]).then_inc(sem["c_dve"], 1)
                        cd += 1
                        v.wait_ge(sem["ld_xc"], 16 * (g2 + 1))
                        v.wait_ge(sem["c_dve"], cd)
                        v.tensor_add(fsb[fo][:, :], fsb[fo][:, :],
                                     xcs[:, 1024 * tt2 + 512 * oc2:1024 * tt2 + 512 * oc2 + 512]) \
                            .then_inc(sem["c_dve"], 1)
                        cd += 1
                    for yi, (oc2, tt2, l) in enumerate(YIS):
                        fo = 2 * tt2 + oc2
                        v.wait_ge(sem["c_act"], _ca_y(g2, yi))
                        v.wait_ge(sem[f"ld_mk{l}"], 16 * (2 * g2 + oc2 + 1))
                        v.wait_ge(sem["c_dve"], cd)
                        v.tensor_mul(tmp_sb[:, :], ysb[:, 512 * (yi % 4):512 * (yi % 4) + 512],
                                     mkrng[:, 1024 * l + 512 * tt2:1024 * l + 512 * tt2 + 512]) \
                            .then_inc(sem["c_dve"], 1)
                        cd += 1
                        v.wait_ge(sem["c_dve"], cd)
                        v.tensor_add(fsb[fo][:, :], fsb[fo][:, :], tmp_sb[:, :]) \
                            .then_inc(sem["c_dve"], 1)
                        cd += 1

    return nc



def _mask_u():
    def threefry_pair(k0, k1, x0, x1):
        ks = [np.uint32(k0), np.uint32(k1),
              np.uint32(np.uint32(k0) ^ np.uint32(k1) ^ np.uint32(0x1BD11BDA))]
        rot = [[13, 15, 26, 6], [17, 29, 16, 24]]
        x0 = x0 + ks[0]
        x1 = x1 + ks[1]
        for i in range(5):
            for r in rot[i % 2]:
                x0 = x0 + x1
                x1 = (x1 << np.uint32(r)) | (x1 >> np.uint32(32 - r))
                x1 = x1 ^ x0
            x0 = x0 + ks[(i + 1) % 3]
            x1 = x1 + ks[(i + 2) % 3] + np.uint32(i + 1)
        return x0, x1
    n = 2 * SBT * D
    idx = np.arange(n, dtype=np.uint64)
    hi = (idx >> np.uint64(32)).astype(np.uint32)
    lo = (idx & np.uint64(0xFFFFFFFF)).astype(np.uint32)
    o0, o1 = threefry_pair(0, 42, hi, lo)
    bits = o0 ^ o1
    f = ((bits >> np.uint32(9)) | np.uint32(0x3F800000)).view(np.float32) - np.float32(1.0)
    return f.reshape(2, SBT, D)


def kernel(L0_attn_x, L0_attn_w_qkv, L0_attn_b_qkv, L0_attn_w_out, L0_attn_b_out,
           L0_ffn_W0, L0_ffn_b0, L0_ffn_A, L0_ffn_B):
    x = np.asarray(L0_attn_x, dtype=np.float32)
    w_qkv = np.asarray(L0_attn_w_qkv, dtype=np.float32)
    b_qkv = np.asarray(L0_attn_b_qkv, dtype=np.float32)
    w_out = np.asarray(L0_attn_w_out, dtype=np.float32)
    b_out = np.asarray(L0_attn_b_out, dtype=np.float32)
    W0 = np.asarray(L0_ffn_W0, dtype=np.float32)
    b0 = np.asarray(L0_ffn_b0, dtype=np.float32)
    A = np.asarray(L0_ffn_A, dtype=np.float32)
    B = np.asarray(L0_ffn_B, dtype=np.float32)

    m = np.arange(1, 33, dtype=np.float32) * (8.0 / 32.0)
    slopes = (1.0 / np.power(np.float32(2.0), m)).astype(np.float32)
    u = _mask_u()
    keep = np.full(4, 0.25, np.float32)
    hi = np.cumsum(keep)
    lo = hi - keep

    d_perm = np.array([128 * (rk + 4 * hl) + hd
                       for hl in range(8) for rk in range(4) for hd in range(128)])
    woutT_perm = np.ascontiguousarray(w_out.T[d_perm])
    w0T_full = np.ascontiguousarray(W0.T)

    mtri_v = np.where(np.arange(128)[:, None] > np.arange(128)[None, :],
                      np.float32(MASKNEG), np.float32(0.0)).astype(np.float32)
    ident_v = np.eye(128, dtype=ml_dtypes.bfloat16)

    wq, wk, wv = w_qkv[0:D], w_qkv[D:2 * D], w_qkv[2 * D:3 * D]
    bq, bk, bv = b_qkv[0:D], b_qkv[D:2 * D], b_qkv[2 * D:3 * D]

    nc = build_nc()
    in_maps = []
    for c in range(8):
        r, beta = c % 4, c // 4
        heads = [r + 4 * j for j in range(NH)]
        cols = slice(1024 * r, 1024 * r + 1024)

        blocks = []
        vb_row = np.zeros(1024, np.float32)
        bqkvqk_v = np.zeros((128, 16), np.float32)
        for s in range(NSW):
            hA, hB = heads[2 * s], heads[2 * s + 1]
            blocks.append(np.concatenate([
                wq[128 * hA:128 * hA + 128], wk[128 * hA:128 * hA + 128],
                wq[128 * hB:128 * hB + 128], wk[128 * hB:128 * hB + 128],
                wv[128 * hA:128 * hA + 128], wv[128 * hB:128 * hB + 128]], axis=0))
            bqkvqk_v[:, 4 * s + 0] = bq[128 * hA:128 * hA + 128]
            bqkvqk_v[:, 4 * s + 1] = bk[128 * hA:128 * hA + 128]
            bqkvqk_v[:, 4 * s + 2] = bq[128 * hB:128 * hB + 128]
            bqkvqk_v[:, 4 * s + 3] = bk[128 * hB:128 * hB + 128]
            vb_row[256 * s:256 * s + 128] = bv[128 * hA:128 * hA + 128]
            vb_row[256 * s + 128:256 * s + 256] = bv[128 * hB:128 * hB + 128]
        wqkvT_v = np.ascontiguousarray(np.concatenate(blocks, axis=0).T)

        aeff_v = np.zeros((D, 64), np.float32)
        beff_v = np.zeros((4, 16, 1024), np.float32)
        for l in range(4):
            aeff_v[:, 16 * l:16 * l + 16] = A[l].T
            beff_v[l] = B[l, cols, :].T

        btab_v = np.zeros((128, NV), np.float32)
        i128 = np.arange(128, dtype=np.float32)
        for (hl, off), colx in TAB.items():
            btab_v[:, colx] = slopes[heads[hl]] * (i128 + off) - CEXP

        b2d_v = np.zeros((128, HB2D * 512), np.float32)
        for hl in range(HB2D):
            w = WCH[hl]
            sl = np.float64(slopes[heads[hl]])
            ql = np.arange(512)
            anc = (ql // w) * w + (w - 1)
            b2d_v[:, 512 * hl:512 * hl + 512] = (
                SQ128 * (sl * (i128[:, None] - anc[None, :]) - CEXP)).astype(np.float32)
        ctab_v = np.zeros((128, NCT), np.float32)
        for (hl, dd), colx in CTI.items():
            ctab_v[:, colx] = np.float32(SQ128 * float(slopes[heads[hl]]) * 128.0 * dd)
        mtri4_v = np.zeros((128, 2048), np.float32)
        for j in range(4):
            blk = np.zeros((128, 512), np.float32)
            blk[:, :128 * j] = MASKNEG
            ii = np.arange(128)
            blk[:, 128 * j:128 * j + 128] = np.where(
                ii[:, None] > ii[None, :], np.float32(MASKNEG), np.float32(0.0))
            mtri4_v[:, 512 * j:512 * j + 512] = blk

        masks_v = np.empty((4, SBT, 1024), dtype=ml_dtypes.bfloat16)
        ub = u[beta][:, cols]
        for l in range(4):
            masks_v[l] = ((ub >= lo[l]) & (ub < hi[l])).astype(ml_dtypes.bfloat16)

        in_maps.append({
            "x_full": np.ascontiguousarray(x[beta]),
            "x_cols": np.ascontiguousarray(x[beta][:, cols]),
            "wqkvT": wqkvT_v,
            "bqkv_qk": bqkvqk_v,
            "bqkv_vb": np.ascontiguousarray(
                np.broadcast_to(vb_row[None, :], (128, 1024))).astype(ml_dtypes.bfloat16),
            "woutT": np.ascontiguousarray(woutT_perm[:, cols]),
            "w0T": np.ascontiguousarray(w0T_full[:, cols]),
            "b0bc": np.ascontiguousarray(np.broadcast_to(b0[cols][None, :], (128, 1024))).astype(ml_dtypes.bfloat16),
            "boutbc": np.ascontiguousarray(np.broadcast_to(b_out[cols][None, :], (128, 1024))),
            "aeff": aeff_v,
            "beff": beff_v,
            "masks": masks_v,
            "btab": btab_v,
            "b2d": b2d_v,
            "ctab": ctab_v,
            "mtri4": mtri4_v,
            "mtri": mtri_v,
            "ident": ident_v,
        })

    res = run_bass_kernel_spmd(nc, in_maps, core_ids=list(range(8)))
    out = np.empty((2, SBT, D), np.float32)
    for c in range(8):
        r, beta = c % 4, c // 4
        out[beta][:, 1024 * r:1024 * r + 1024] = res.results[c]["out_shard"]
    return out
